# version 34
# baseline (speedup 1.0000x reference)
import contextlib
import numpy as np
import ml_dtypes

import concourse.bass as bass
import concourse.mybir as mybir
from concourse.bass_utils import run_bass_kernel_spmd

F32 = mybir.dt.float32
BF16 = mybir.dt.bfloat16
AF = mybir.ActivationFunctionType
ALU = mybir.AluOpType
AX = mybir.AxisListType

D = 1024
T = 4096
TOWN = 2048
NPAIR = 8
FF = 4096
NT = T // 128
NTO = TOWN // 128
EPS = 1e-5

ENGINES = ("sync", "vector", "scalar", "tensor", "gpsimd")
COL0 = [0, 0, 128, 128, 256, 256, 384, 384]
MASK_OFF = np.cumsum([0] + [512 - c for c in COL0]).tolist()


class Builder:
    def __init__(self):
        self.ops = {e: [] for e in ENGINES}
        self.cnt = {s: 0 for s in ("ld", "g", "st", "pe", "act", "dve", "pool")}

    def inc(self, sem, n):
        self.cnt[sem] += n
        return self.cnt[sem]

    def wait(self, eng, sem, val):
        if val and val > 0:
            self.ops[eng].append(lambda e, S, s=sem, v=val: e.wait_ge(S[s], v))

    def op(self, eng, fn, inc_sem=None, inc_n=1, drain=None):
        if inc_sem is None:
            self.ops[eng].append(lambda e, S, f=fn: f(e))
            return None
        if drain is None:
            drain = inc_sem == "pool"
        tgt = self.inc(inc_sem, inc_n)
        if drain:
            self.ops[eng].append(lambda e, S, f=fn: f(e))
            self.ops[eng].append(
                lambda e, S, s=inc_sem, n=inc_n: e.drain().then_inc(S[s], n))
        else:
            self.ops[eng].append(
                lambda e, S, f=fn, s=inc_sem, n=inc_n: f(e).then_inc(S[s], n))
        return tgt

    def flush(self, block, S):
        for eng in ENGINES:
            ops = self.ops[eng]
            if not ops:
                continue
            self.ops[eng] = []

            def body(e, ops=ops):
                for fn in ops:
                    fn(e, S)
            getattr(block, eng)(body)


def build_program():
    nc = bass.Bass("TRN2", target_bir_lowering=False)

    def din(name, shape, dt=F32):
        return nc.dram_tensor(name, shape, dt, kind="ExternalInput").ap()

    x_seq = din("x_seq", [T, D])
    x_own = din("x_own", [TOWN, D])
    wqT = din("wqT", [D, D])
    wkT = din("wkT", [D, D])
    wvT = din("wvT", [D, NPAIR * 130])
    bq = din("bq", [1, D])
    bk = din("bk", [1, D])
    bv = din("bv", [1, NPAIR * 130])
    woutT_in = din("woutT", [D, D])
    b_out = din("b_out", [1, D])
    wfcT_in = din("wfcT", [D, FF])
    wprojT_in = din("wprojT", [FF, D])
    rq_cos = din("ropeq_cos", [128, TOWN], BF16)
    rq_sin = din("ropeq_sin", [128, TOWN], BF16)
    rk_cos = din("ropek_cos", [128, T], BF16)
    rk_sin = din("ropek_sin", [128, T], BF16)
    masks_in = din("masks", [128, MASK_OFF[8]], BF16)
    ident_in = din("ident", [128, 128], BF16)
    perm_in = din("permT", [128, 128], BF16)

    out_hbm = nc.dram_tensor("out", [TOWN, D], F32, kind="ExternalOutput").ap()
    aproj_hbm = nc.dram_tensor("aproj_scratch", [TOWN, D], F32).ap()

    B = Builder()

    with contextlib.ExitStack() as top:
        S = {}
        for s in ("ld", "g", "st", "pe", "act", "dve", "pool"):
            S[s] = top.enter_context(nc.semaphore(s))
        block = top.enter_context(nc.Block())

        onesb = top.enter_context(nc.sbuf_tensor("onesb", [65, 128], F32, side="right"))
        ones_bf = top.enter_context(nc.sbuf_tensor("ones_bf", [65, 512], BF16, side="right"))
        ident = top.enter_context(nc.sbuf_tensor("ident_sb", [128, 128], BF16, side="right"))
        permT = top.enter_context(nc.sbuf_tensor("permT_sb", [128, 128], BF16, side="right"))
        epscol = top.enter_context(nc.sbuf_tensor("epscol", [128, 1], F32, side="right"))

        B.op("gpsimd", lambda e: e.memset(onesb[:, :], 1.0), "pool")
        B.op("gpsimd", lambda e: e.memset(ones_bf[:, :], 1.0), "pool")
        B.op("gpsimd", lambda e: e.memset(epscol[:, :], EPS), "pool")
        for dst, src in ((ident, ident_in), (permT, perm_in)):
            B.op("sync", lambda e, d=dst, s=src: e.dma_start(out=d[:, :], in_=s[:, :]),
                 "ld", 16)
        consts_ld = B.cnt["ld"]
        consts_pool = B.cnt["pool"]
        B.wait("scalar", "pool", consts_pool)

        xnf = top.enter_context(nc.sbuf_tensor("xnf", [128, 8 * T], BF16))
        p12 = contextlib.ExitStack()
        q_all = p12.enter_context(nc.sbuf_tensor("q_all", [128, NPAIR * TOWN], BF16))

        def emit_ln(src_hbm, ntiles, dst_fm, dst_cols, xstage, stat, xdump,
                    xn_tm, tr_ps, hook=None):
            nslots = 4
            dve0 = B.cnt["dve"]
            slot_free = {}
            tr_free = {}
            red2_prev = 0
            for t0 in range(0, ntiles, 4):
                nb = min(4, ntiles - t0)
                sls = [(t0 + i) % nslots for i in range(nb)]
                for i in range(nb):
                    t = t0 + i
                    B.wait("sync", "dve", slot_free.get(sls[i], dve0))
                    B.op("sync", lambda e, slot=sls[i], t=t: e.dma_start(
                        out=xstage[:, slot * 1024:(slot + 1) * 1024],
                        in_=src_hbm[t * 128:(t + 1) * 128, :]), "ld", 16)
                ld_done = B.cnt["ld"]
                B.wait("vector", "ld", ld_done)
                for s in sls:
                    B.op("vector", lambda e, s=s: e.reduce_sum(
                        stat[:, s * 8:s * 8 + 1],
                        xstage[:, s * 1024:(s + 1) * 1024], AX.X))
                B.wait("scalar", "ld", ld_done)
                B.wait("scalar", "dve", red2_prev)
                sq = None
                for s in sls:
                    sq = B.op("scalar", lambda e, s=s: e.activation(
                        xdump[:, s * 1024:(s + 1) * 1024],
                        xstage[:, s * 1024:(s + 1) * 1024], AF.Square), "act")
                B.wait("vector", "act", sq)
                red2_prev = 0
                for s in sls:
                    red2_prev = B.op("vector", lambda e, s=s: e.reduce_sum(
                        stat[:, s * 8 + 1:s * 8 + 2],
                        xdump[:, s * 1024:(s + 1) * 1024], AX.X), "dve")
                for s in sls:
                    B.op("vector", lambda e, s0=s * 8: e.tensor_scalar(
                        stat[:, s0:s0 + 1], stat[:, s0:s0 + 1], -1.0 / D, None,
                        ALU.mult))
                for s in sls:
                    B.op("vector", lambda e, s0=s * 8: e.tensor_scalar(
                        stat[:, s0 + 2:s0 + 3], stat[:, s0 + 1:s0 + 2], 1.0 / D, None,
                        ALU.mult))
                for s in sls:
                    B.op("vector", lambda e, s0=s * 8: e.tensor_tensor(
                        stat[:, s0 + 3:s0 + 4], stat[:, s0:s0 + 1], stat[:, s0:s0 + 1],
                        ALU.mult))
                last = None
                for s in sls:
                    last = B.op("vector", lambda e, s0=s * 8: e.tensor_tensor(
                        stat[:, s0 + 4:s0 + 5], stat[:, s0 + 2:s0 + 3],
                        stat[:, s0 + 3:s0 + 4], ALU.subtract), "dve")
                B.wait("scalar", "dve", last)
                la = None
                for s in sls:
                    la = B.op("scalar", lambda e, s0=s * 8: e.activation(
                        stat[:, s0 + 5:s0 + 6], stat[:, s0 + 4:s0 + 5], AF.Sqrt,
                        bias=epscol[:, 0:1]), "act")
                B.wait("vector", "act", la)
                for s in sls:
                    B.op("vector", lambda e, s0=s * 8: e.reciprocal(
                        stat[:, s0 + 6:s0 + 7], stat[:, s0 + 5:s0 + 6]))
                for i in range(nb):
                    s = sls[i]
                    slot_free[s] = B.op("vector", lambda e, slot=s, s0=s * 8:
                        e.tensor_scalar(
                            xn_tm[:, slot * 1024:(slot + 1) * 1024],
                            xstage[:, slot * 1024:(slot + 1) * 1024],
                            stat[:, s0:s0 + 1], stat[:, s0 + 6:s0 + 7],
                            ALU.add, ALU.mult), "dve")
                if hook is not None:
                    hook(t0 // 4)
                for i in range(nb):
                    t = t0 + i
                    slot = sls[i]
                    B.wait("tensor", "dve", slot_free[slot])
                    for c in range(8):
                        k = t * 8 + c
                        pb = tr_ps[k % 2]
                        B.wait("tensor", "dve", tr_free.get(k % 2, dve0))
                        B.op("tensor", lambda e, pb=pb, slot=slot, c=c: e.transpose(
                            pb[:, 0:128],
                            xn_tm[:, slot * 1024 + c * 128: slot * 1024 + (c + 1) * 128],
                            ident[:, :]), "pe")
                        B.wait("vector", "pe", B.cnt["pe"])
                        tr_free[k % 2] = B.op("vector",
                            lambda e, pb=pb, c=c, t=t: e.tensor_copy(
                                out=dst_fm[:, c * dst_cols + t * 128:
                                           c * dst_cols + t * 128 + 128],
                                in_=pb[:, 0:128]), "dve")

        def emit_rope(pre_ap, perm_bank, cosap, sinap, dstap, t1, t2, pre_act):
            B.wait("tensor", "act", pre_act)
            pe_perm = B.op("tensor", lambda e: e.matmul(
                perm_bank, permT[:, :], pre_ap, start=True, stop=True), "pe")
            B.wait("vector", "act", pre_act)
            B.op("vector", lambda e: e.tensor_tensor(t1[:, :], pre_ap, cosap, ALU.mult))
            B.wait("vector", "pe", pe_perm)
            t2done = B.op("vector", lambda e: e.tensor_tensor(t2[:, :], perm_bank, sinap,
                                                              ALU.mult), "dve")
            B.wait("vector", "dve", t2done)
            return B.op("vector", lambda e: e.tensor_tensor(dstap, t1[:, :], t2[:, :],
                                                            ALU.add), "dve")

        with contextlib.ExitStack() as p1:
            xn_own = p1.enter_context(nc.sbuf_tensor("xn_own", [128, 8 * TOWN], BF16))
            xstage = p1.enter_context(nc.sbuf_tensor("xstage", [128, 4 * 1024], F32))
            xdump = p1.enter_context(nc.sbuf_tensor("xdump", [128, 4 * 1024], F32))
            stat = p1.enter_context(nc.sbuf_tensor("stat", [128, 64], F32))
            xn_tm = p1.enter_context(nc.sbuf_tensor("xn_tm", [128, 4 * 1024], BF16))
            tr_ps = [p1.enter_context(nc.psum_tensor(f"trp{i}", [128, 1024], BF16))
                     for i in range(2)]
            brow_q = p1.enter_context(nc.sbuf_tensor("brow_q", [1, D], F32))
            brow_qb = p1.enter_context(nc.sbuf_tensor("brow_qb", [1, D], BF16))
            rqc = p1.enter_context(nc.sbuf_tensor("rqc", [128, TOWN], BF16))
            rqs = p1.enter_context(nc.sbuf_tensor("rqs", [128, TOWN], BF16))
            wstage = p1.enter_context(nc.sbuf_tensor("wstage", [128, 1024], F32))
            wq_bf = p1.enter_context(nc.sbuf_tensor("wq_bf", [128, 1024], BF16))
            qpre = p1.enter_context(nc.sbuf_tensor("qpre", [128, 2 * 512], BF16))
            t1b = p1.enter_context(nc.sbuf_tensor("t1b", [128, 512], BF16))
            t2b = p1.enter_context(nc.sbuf_tensor("t2b", [128, 512], BF16))
            pr_ps = [p1.enter_context(nc.psum_tensor(f"prp{i}", [128, 512], F32))
                     for i in range(2)]
            pm_ps = [p1.enter_context(nc.psum_tensor(f"pmp{i}", [128, 512], F32))
                     for i in range(2)]

            B.op("sync", lambda e: e.dma_start(out=rqc[:, :], in_=rq_cos[:, :]), "ld", 16)
            B.op("sync", lambda e: e.dma_start(out=rqs[:, :], in_=rq_sin[:, :]), "ld", 16)
            B.op("sync", lambda e: e.dma_start(out=brow_q[:, :], in_=bq[:, :]), "ld", 16)
            rope_tab_ld = B.cnt["ld"]
            B.wait("vector", "ld", rope_tab_ld)
            bqb_dve = B.op("vector", lambda e: e.tensor_copy(out=brow_qb[:, :],
                                                             in_=brow_q[:, :]), "dve")
            B.wait("tensor", "dve", bqb_dve)
            B.wait("tensor", "ld", max(consts_ld, rope_tab_ld))
            B.wait("tensor", "pool", consts_pool)
            emit_ln(x_own, NTO, xn_own, TOWN, xstage, stat, xdump, xn_tm, tr_ps)
            ln_done_dve = B.cnt["dve"]

            B.wait("tensor", "dve", ln_done_dve)
            B.wait("vector", "ld", rope_tab_ld)
            proj_free = {}
            rope_done = {}

            def emit_qpair(p):
                B.wait("sync", "pool", B.cnt["pool"])
                B.op("sync", lambda e, p=p: e.dma_start(
                    out=wstage[:, :].rearrange("q (dc f) -> q dc f", dc=8),
                    in_=wqT[:, p * 128:(p + 1) * 128]
                    .rearrange("(dc q) f -> q dc f", q=128)), "ld", 16)
                B.wait("gpsimd", "ld", B.cnt["ld"])
                B.wait("gpsimd", "pe", B.cnt["pe"])
                wq_ready = B.op("gpsimd", lambda e: e.tensor_copy(
                    out=wq_bf[:, :], in_=wstage[:, :]), "pool")
                B.wait("tensor", "pool", wq_ready)
                for ch in range(4):
                    k = p * 4 + ch
                    par = k % 2
                    bank = pr_ps[par]
                    B.wait("tensor", "act", proj_free.get(par, 0))
                    for dc in range(8):
                        B.op("tensor", lambda e, bank=bank, dc=dc, ch=ch: e.matmul(
                            bank[:, :],
                            wq_bf[:, dc * 128:(dc + 1) * 128],
                            xn_own[:, dc * TOWN + ch * 512: dc * TOWN + (ch + 1) * 512],
                            start=(dc == 0), stop=False))
                    pe_proj = B.op("tensor", lambda e, bank=bank, p=p: e.matmul(
                        bank[:, :], brow_qb[0:1, p * 128:(p + 1) * 128],
                        ones_bf[0:1, 0:512], start=False, stop=True), "pe")
                    B.wait("scalar", "pe", pe_proj)
                    B.wait("scalar", "dve", rope_done.get(par, 0))
                    proj_free[par] = B.op("scalar", lambda e, bank=bank, par=par:
                        e.activation(qpre[:, par * 512:(par + 1) * 512], bank[:, :],
                                     AF.Copy), "act")
                    B.wait("tensor", "dve", rope_done.get(par, 0))
                    rope_done[par] = emit_rope(
                        qpre[:, par * 512:(par + 1) * 512],
                        pm_ps[par][:, :],
                        rqc[:, ch * 512:(ch + 1) * 512],
                        rqs[:, ch * 512:(ch + 1) * 512],
                        q_all[:, p * TOWN + ch * 512: p * TOWN + (ch + 1) * 512],
                        t1b, t2b, proj_free[par])

            emit_ln(x_seq, NT, xnf, T, xstage, stat, xdump, xn_tm, tr_ps,
                    hook=emit_qpair)
            q_done_dve = B.cnt["dve"]
            B.flush(block, S)


        p23 = contextlib.ExitStack()
        out_norm = p23.enter_context(
            nc.sbuf_tensor("out_norm", [128, NPAIR * TOWN], BF16, side="right"))
        with contextlib.ExitStack() as p2:
            brow_k = p2.enter_context(nc.sbuf_tensor("brow_k", [1, D], F32))
            brow_kb = p2.enter_context(nc.sbuf_tensor("brow_kb", [1, D], BF16))
            brow_vb = p2.enter_context(nc.sbuf_tensor("brow_vb", [1, NPAIR * 130], BF16))
            rkc = p2.enter_context(nc.sbuf_tensor("rkc", [128, T], BF16))
            rks = p2.enter_context(nc.sbuf_tensor("rks", [128, T], BF16))
            masks = p2.enter_context(
                nc.sbuf_tensor("masks_sb", [128, MASK_OFF[8]], BF16))
            k_bf = p2.enter_context(nc.sbuf_tensor("k_bf", [128, T], BF16))
            v_store = p2.enter_context(
                nc.sbuf_tensor("v_store", [128, NT * 130], BF16))
            wstage2 = p2.enter_context(nc.sbuf_tensor("wstage2", [128, 1040], F32))
            wk_bf = p2.enter_context(nc.sbuf_tensor("wk_bf", [128, 1024], BF16))
            wv_bf = p2.enter_context(nc.sbuf_tensor("wv_bf", [128, 1040], BF16))
            kpre = p2.enter_context(nc.sbuf_tensor("kpre", [128, 2 * 512], BF16))
            t1k = p2.enter_context(nc.sbuf_tensor("t1k", [128, 512], BF16))
            t2k = p2.enter_context(nc.sbuf_tensor("t2k", [128, 512], BF16))
            p_t = p2.enter_context(nc.sbuf_tensor("p_t", [128, 4 * 1024], BF16))
            rowbuf = p2.enter_context(nc.sbuf_tensor("rowbuf", [65, 1024], F32))
            recipbf = p2.enter_context(nc.sbuf_tensor("recipbf", [65, 1024], BF16))
            evacA = p2.enter_context(nc.sbuf_tensor("evacA", [64, 512], BF16))
            evacB = p2.enter_context(nc.sbuf_tensor("evacB", [64, 512], BF16))
            normA = p2.enter_context(nc.sbuf_tensor("normA", [64, 512], BF16))
            normB = p2.enter_context(nc.sbuf_tensor("normB", [64, 512], BF16))
            cstat = p2.enter_context(nc.sbuf_tensor("cstat", [128, 4], F32))
            negc = p2.enter_context(nc.sbuf_tensor("negc", [128, 1], F32))
            s_ps = [p2.enter_context(nc.psum_tensor(f"sps{i}", [128, 1024], F32))
                    for i in range(2)]
            o_ps = p2.enter_context(nc.psum_tensor("ops", [128, 1024], F32))
            bc_ps = [p2.enter_context(nc.psum_tensor(f"bcp{i}", [64, 512], F32))
                     for i in range(2)]

            B.wait("sync", "dve", q_done_dve)
            B.op("sync", lambda e: e.dma_start(out=brow_k[:, :], in_=bk[:, :]), "ld", 16)
            B.op("sync", lambda e: e.dma_start(out=wstage2[0:1, 0:NPAIR * 130],
                                               in_=bv[:, :]), "ld", 16)
            B.op("sync", lambda e: e.dma_start(out=rkc[:, :], in_=rk_cos[:, :]),
                 "ld", 16)
            B.op("sync", lambda e: e.dma_start(out=rks[:, :], in_=rk_sin[:, :]),
                 "ld", 16)
            B.op("sync", lambda e: e.dma_start(out=masks[:, :], in_=masks_in[:, :]),
                 "ld", 16)
            ph2_ld = B.cnt["ld"]
            B.wait("vector", "ld", ph2_ld)
            B.wait("tensor", "ld", ph2_ld)
            B.op("vector", lambda e: e.tensor_copy(out=brow_kb[:, :], in_=brow_k[:, :]))
            bkb_dve = B.op("vector", lambda e: e.tensor_copy(
                out=brow_vb[:, :], in_=wstage2[0:1, 0:NPAIR * 130]), "dve")
            B.wait("tensor", "dve", bkb_dve)
            B.wait("sync", "dve", bkb_dve)

            jglob = 0
            sfree_mark = {}
            ptfree_mark = {}
            evac_mark = 0
            place_prev = 0
            v_done_act = 0

            wfcst = {"ready": 0, "next": 0}

            def emit_wfc_group(n8):
                for k in range(8):
                    n = n8 * 8 + k
                    piece, dc = n // 8, n % 8
                    B.wait("sync", "act", wfcst["ready"])
                    B.op("sync", lambda e, piece=piece, dc=dc: e.dma_start(
                        out=wstage2[:, 0:1024],
                        in_=wfcT_in[dc * 128:(dc + 1) * 128,
                                    piece * 1024:(piece + 1) * 1024]), "ld", 16)
                    B.wait("scalar", "ld", B.cnt["ld"])
                    wfcst["ready"] = B.op("scalar",
                        lambda e, piece=piece, dc=dc: e.activation(
                            xnf[:, dc * FF + piece * 1024:
                                dc * FF + (piece + 1) * 1024],
                            wstage2[:, 0:1024], AF.Copy), "act")

            def emit_norm(p, qc):
                nonlocal evac_mark, place_prev
                last_pv = B.cnt["pe"]
                B.wait("scalar", "pe", last_pv)
                B.op("scalar", lambda e: e.activation(
                    rowbuf[64:65, 0:512], o_ps[64:65, 0:512], AF.Ln))
                B.op("scalar", lambda e: e.activation(
                    rowbuf[64:65, 512:1024], o_ps[64:65, 512:1024], AF.Ln))
                B.op("scalar", lambda e: e.activation(
                    recipbf[64:65, 0:512], rowbuf[64:65, 0:512], AF.Exp,
                    scale=-1.0))
                rows_act = B.op("scalar", lambda e: e.activation(
                    recipbf[64:65, 512:1024], rowbuf[64:65, 512:1024], AF.Exp,
                    scale=-1.0), "act")
                B.wait("tensor", "act", rows_act)
                B.wait("tensor", "dve", evac_mark)
                B.op("tensor", lambda e: e.matmul(
                    bc_ps[0][0:64, :], ones_bf[64:65, 0:64],
                    recipbf[64:65, 0:512], start=True, stop=True))
                bc_pe = B.op("tensor", lambda e: e.matmul(
                    bc_ps[1][0:64, :], ones_bf[64:65, 0:64],
                    recipbf[64:65, 512:1024], start=True, stop=True), "pe")
                B.wait("vector", "pe", bc_pe)
                B.wait("vector", "g", place_prev)
                B.op("vector", lambda e: e.tensor_copy(
                    out=evacA[:, :], in_=o_ps[0:64, 0:512]))
                B.op("vector", lambda e: e.tensor_copy(
                    out=evacB[:, :], in_=o_ps[0:64, 512:1024]))
                B.op("vector", lambda e: e.tensor_tensor(
                    normA[:, :], evacA[:, :], bc_ps[0][0:64, :], ALU.mult))
                evac_mark = B.op("vector", lambda e: e.tensor_tensor(
                    normB[:, :], evacB[:, :], bc_ps[1][0:64, :], ALU.mult),
                    "dve")
                B.wait("gpsimd", "dve", evac_mark)
                B.op("gpsimd", lambda e, p=p, qc=qc: e.dma_start(
                    out=out_norm[0:64, p * TOWN + qc * 512:
                                 p * TOWN + (qc + 1) * 512],
                    in_=normA[:, :]), "g", 16)
                place_prev = B.op("gpsimd", lambda e, p=p, qc=qc: e.dma_start(
                    out=out_norm[64:128, p * TOWN + qc * 512:
                                 p * TOWN + (qc + 1) * 512],
                    in_=normB[:, :]), "g", 16)
                if p == NPAIR - 1:
                    if qc == 0:
                        B.wait("sync", "pool", B.cnt["pool"])
                    emit_wfc_group(qc)

            for p in range(NPAIR):
                B.wait("sync", "pool", B.cnt["pool"])
                B.op("sync", lambda e, p=p: e.dma_start(
                    out=wstage2[:, 0:1024].rearrange("q (dc f) -> q dc f", dc=8),
                    in_=wkT[:, p * 128:(p + 1) * 128]
                    .rearrange("(dc q) f -> q dc f", q=128)), "ld", 16)
                B.wait("gpsimd", "ld", B.cnt["ld"])
                B.wait("gpsimd", "pe", B.cnt["pe"])
                wk_ready = B.op("gpsimd", lambda e: e.tensor_copy(
                    out=wk_bf[:, :], in_=wstage2[:, 0:1024]), "pool")
                B.wait("sync", "pool", wk_ready)
                B.op("sync", lambda e, p=p: e.dma_start(
                    out=wstage2[:, 0:1040].rearrange("q (dc f) -> q dc f", dc=8),
                    in_=wvT[:, p * 130:(p + 1) * 130]
                    .rearrange("(dc q) f -> q dc f", q=128)), "ld", 16)
                B.wait("gpsimd", "ld", B.cnt["ld"])
                B.wait("gpsimd", "pe", B.cnt["pe"])
                wv_ready = B.op("gpsimd", lambda e: e.tensor_copy(
                    out=wv_bf[:, :], in_=wstage2[:, 0:1040]), "pool")

                prev_attn_pe = B.cnt["pe"]
                kproj_free = dict(sfree_mark)
                kpre_free = {}
                kperm_free = {0: evac_mark, 1: evac_mark}
                B.wait("tensor", "pool", wk_ready)
                B.wait("vector", "pe", prev_attn_pe)
                for ch in range(8):
                    par = ch % 2
                    bank = s_ps[par]
                    B.wait("tensor", "act", kproj_free.get(par, 0))
                    for dc in range(8):
                        B.op("tensor", lambda e, bank=bank, dc=dc, ch=ch: e.matmul(
                            bank[:, 0:512],
                            wk_bf[:, dc * 128:(dc + 1) * 128],
                            xnf[:, dc * T + ch * 512: dc * T + (ch + 1) * 512],
                            start=(dc == 0), stop=False))
                    pe_kproj = B.op("tensor", lambda e, bank=bank, p=p: e.matmul(
                        bank[:, 0:512], brow_kb[0:1, p * 128:(p + 1) * 128],
                        ones_bf[0:1, 0:512], start=False, stop=True), "pe")
                    B.wait("scalar", "pe", pe_kproj)
                    B.wait("scalar", "dve", kpre_free.get(par, 0))
                    kproj_free[par] = B.op("scalar",
                        lambda e, bank=bank, par=par: e.activation(
                            kpre[:, par * 512:(par + 1) * 512], bank[:, 0:512],
                            AF.Copy), "act")
                    B.wait("tensor", "dve", kperm_free.get(par, 0))
                    kpre_free[par] = kperm_free[par] = emit_rope(
                        kpre[:, par * 512:(par + 1) * 512],
                        o_ps[:, par * 512:(par + 1) * 512],
                        rkc[:, ch * 512:(ch + 1) * 512],
                        rks[:, ch * 512:(ch + 1) * 512],
                        k_bf[:, ch * 512:(ch + 1) * 512],
                        t1k, t2k, kproj_free[par])
                kv_rope_dve = B.cnt["dve"]

                vproj_free = dict(kproj_free)
                B.wait("tensor", "pool", wv_ready)
                B.wait("scalar", "pe", prev_attn_pe)
                for tt in range(NT):
                    par = tt % 2
                    bank = s_ps[par]
                    B.wait("tensor", "act", vproj_free.get(par, 0))
                    for dc in range(8):
                        B.op("tensor", lambda e, bank=bank, dc=dc, tt=tt: e.matmul(
                            bank[:, 0:130],
                            xnf[:, dc * T + tt * 128: dc * T + (tt + 1) * 128],
                            wv_bf[:, dc * 130:(dc + 1) * 130],
                            start=(dc == 0), stop=False))
                    pe_v = B.op("tensor", lambda e, bank=bank, p=p: e.matmul(
                        bank[:, 0:130], ones_bf[0:1, 0:128],
                        brow_vb[0:1, p * 130:(p + 1) * 130], start=False,
                        stop=True), "pe")
                    B.wait("scalar", "pe", pe_v)
                    vproj_free[par] = B.op("scalar",
                        lambda e, bank=bank, tt=tt: e.activation(
                            v_store[:, tt * 130:(tt + 1) * 130], bank[:, 0:130],
                            AF.Copy), "act")
                v_done_act = B.cnt["act"]

                B.wait("vector", "dve", kv_rope_dve)
                B.op("vector", lambda e, p=p: e.reduce_sum(
                    cstat[:, 0:1], q_all[:, p * TOWN:(p + 1) * TOWN:16], AX.X))
                cs_dve = B.op("vector", lambda e: e.reduce_sum(
                    cstat[:, 1:2], k_bf[:, ::32], AX.X), "dve")
                B.wait("tensor", "dve", max(cs_dve, evac_mark))
                cpe = B.op("tensor", lambda e: e.matmul(
                    bc_ps[0][0:1, 0:1], cstat[:, 1:2], cstat[:, 0:1],
                    start=True, stop=True), "pe")
                B.wait("vector", "pe", cpe)
                cdve = B.op("vector", lambda e: e.tensor_copy(
                    out=cstat[0:1, 3:4], in_=bc_ps[0][0:1, 0:1]), "dve")
                B.wait("tensor", "dve", cdve)
                B.wait("tensor", "act", v_done_act)
                cpe2 = B.op("tensor", lambda e: e.matmul(
                    s_ps[0][:, 0:1], onesb[0:1, 0:128], cstat[0:1, 3:4],
                    start=True, stop=True), "pe")
                B.wait("vector", "pe", cpe2)
                negc_dve = B.op("vector", lambda e: e.tensor_scalar(
                    negc[:, 0:1], s_ps[0][:, 0:1], -0.5 / (128.0 * 128.0), None,
                    ALU.mult), "dve")
                B.wait("scalar", "dve", negc_dve)

                B.wait("tensor", "dve", negc_dve)
                steps = [(qc, j) for qc in range(4) for j in range(8 * qc + 8)]
                pending = []

                def emit_pv(ctx):
                    qc, j, pt3, c0, gsem, gval = ctx
                    nsteps = 8 * qc + 8
                    B.wait("tensor", gsem, gval)
                    if j == 0:
                        B.wait("tensor", "dve", evac_mark)
                    B.op("tensor", lambda e, j=j, pt3=pt3, c0=c0, n=nsteps:
                        e.matmul(
                            o_ps[0:65, c0:512],
                            v_store[:, j * 130: j * 130 + 65],
                            p_t[:, pt3 * 1024 + c0: pt3 * 1024 + 512],
                            start=(j == 0), stop=(j == n - 1)))
                    pv_pe = B.op("tensor",
                        lambda e, j=j, pt3=pt3, c0=c0, n=nsteps: e.matmul(
                            o_ps[0:65, 512 + c0:1024],
                            v_store[:, j * 130 + 65: j * 130 + 130],
                            p_t[:, pt3 * 1024 + 512 + c0: pt3 * 1024 + 1024],
                            start=(j == 0), stop=(j == n - 1)), "pe")
                    ptfree_mark[pt3] = pv_pe

                def drain_one():
                    ctx = pending.pop(0)
                    emit_pv(ctx)
                    if ctx[1] == 8 * ctx[0] + 7:
                        emit_norm(p, ctx[0])

                for (qc, j) in steps:
                    r = j - 8 * qc
                    c0 = COL0[r] if r >= 0 else 0
                    qbase = p * TOWN + qc * 512
                    par = jglob % 2
                    pt3 = jglob % 4
                    bank = s_ps[par]
                    B.wait("tensor", "act", sfree_mark.get(par, 0))
                    B.op("tensor",
                         lambda e, bank=bank, j=j, c0=c0, qbase=qbase, r=r: e.matmul(
                             bank[:, c0:512],
                             k_bf[0:64, j * 128:(j + 1) * 128],
                             q_all[0:64, qbase + c0: qbase + 512],
                             start=True, stop=(r < 0)))
                    qk_pe = B.op("tensor",
                         lambda e, bank=bank, j=j, c0=c0, qbase=qbase, r=r: e.matmul(
                             bank[:, 512 + c0:1024],
                             k_bf[64:128, j * 128:(j + 1) * 128],
                             q_all[64:128, qbase + c0: qbase + 512],
                             start=True, stop=(r < 0)), "pe" if r < 0 else None)
                    if r >= 0:
                        B.op("tensor", lambda e, bank=bank, c0=c0, r=r: e.matmul(
                            bank[:, c0:512], ident[:, :],
                            masks[:, MASK_OFF[r]:MASK_OFF[r + 1]],
                            start=False, stop=True))
                        qk_pe = B.op("tensor",
                            lambda e, bank=bank, c0=c0, r=r: e.matmul(
                                bank[:, 512 + c0:1024], ident[:, :],
                                masks[:, MASK_OFF[r]:MASK_OFF[r + 1]],
                                start=False, stop=True), "pe")
                    if p == NPAIR - 1 and qc == 3 and j == 31:
                        wfcst["qk_last"] = qk_pe
                    B.wait("scalar", "pe", qk_pe)
                    B.wait("scalar", "pe", ptfree_mark.get(pt3, 0))
                    if r < 0:
                        em = B.op("scalar", lambda e, bank=bank, pt3=pt3:
                            e.activation(
                                p_t[:, pt3 * 1024:(pt3 + 1) * 1024],
                                bank[:, :], AF.Exp, bias=negc[:, 0:1]), "act")
                    else:
                        B.op("scalar", lambda e, bank=bank, pt3=pt3, c0=c0:
                            e.activation(
                                p_t[:, pt3 * 1024 + c0: pt3 * 1024 + 512],
                                bank[:, c0:512], AF.Exp, bias=negc[:, 0:1]))
                        em = B.op("scalar", lambda e, bank=bank, pt3=pt3, c0=c0:
                            e.activation(
                                p_t[:, pt3 * 1024 + 512 + c0:
                                    pt3 * 1024 + 1024],
                                bank[:, 512 + c0:1024], AF.Exp,
                                bias=negc[:, 0:1]), "act")
                    sfree_mark[par] = em
                    jglob += 1
                    if len(pending) >= 3:
                        drain_one()
                    pending.append((qc, j, pt3, c0, "act", em))
                while pending:
                    drain_one()

            attn_done_g = B.cnt["g"]
            ph2_act = B.cnt["act"]
            ph2_dve = B.cnt["dve"]
            B.flush(block, S)
        p12.close()

        wfc_bf = xnf
        wfc_ready = wfcst["ready"]
        with contextlib.ExitStack() as p3:
            wout_stage = p3.enter_context(
                nc.sbuf_tensor("wout_stage", [128, 8192], F32))
            wout_bf = p3.enter_context(nc.sbuf_tensor("wout_bf", [128, 8192], BF16))
            aev = [p3.enter_context(nc.sbuf_tensor(f"aev{i}", [128, 1024], F32))
                   for i in range(2)]
            pp = [p3.enter_context(nc.psum_tensor(f"pp{i}", [128, 1024], F32))
                  for i in range(2)]

            B.wait("sync", "pe", wfcst["qk_last"])
            B.op("sync", lambda e: e.dma_start(
                out=wout_stage[:, :].rearrange("q (dc f) -> q dc f", dc=8),
                in_=woutT_in.rearrange("(dc q) f -> q dc f", q=128)), "ld", 16)
            B.wait("gpsimd", "ld", B.cnt["ld"])
            wout_ready = B.op("gpsimd", lambda e: e.tensor_copy(
                out=wout_bf[:, :], in_=wout_stage[:, :]), "pool")

            B.wait("tensor", "pool", wout_ready)
            B.wait("tensor", "g", attn_done_g)
            B.wait("tensor", "act", ph2_act)
            B.wait("tensor", "dve", ph2_dve)
            ap_free = {}
            ap_store = {}
            for tt in range(NTO):
                par = tt % 2
                bank = pp[par]
                B.wait("tensor", "dve", ap_free.get(par, 0))
                op_pe = None
                for dc in range(8):
                    B.op("tensor", lambda e, bank=bank, dc=dc, tt=tt: e.matmul(
                        bank[:, 0:512],
                        out_norm[:, dc * TOWN + tt * 128: dc * TOWN + (tt + 1) * 128],
                        wout_bf[:, dc * 1024: dc * 1024 + 512],
                        start=(dc == 0), stop=(dc == 7)))
                    if dc < 7:
                        B.op("tensor", lambda e, bank=bank, dc=dc, tt=tt: e.matmul(
                            bank[:, 512:1024],
                            out_norm[:, dc * TOWN + tt * 128:
                                     dc * TOWN + (tt + 1) * 128],
                            wout_bf[:, dc * 1024 + 512:(dc + 1) * 1024],
                            start=(dc == 0), stop=False))
                    else:
                        op_pe = B.op("tensor",
                            lambda e, bank=bank, dc=dc, tt=tt: e.matmul(
                                bank[:, 512:1024],
                                out_norm[:, dc * TOWN + tt * 128:
                                         dc * TOWN + (tt + 1) * 128],
                                wout_bf[:, dc * 1024 + 512:(dc + 1) * 1024],
                                start=(dc == 0), stop=True), "pe")
                B.wait("vector", "pe", op_pe)
                B.wait("vector", "g", ap_store.get(par, 0))
                ap_free[par] = B.op("vector", lambda e, bank=bank, par=par:
                    e.tensor_copy(out=aev[par][:, :], in_=bank[:, :]), "dve")
                B.wait("sync", "dve", ap_free[par])
                ap_store[par] = B.op("sync", lambda e, tt=tt: e.dma_start(
                    out=aproj_hbm[tt * 128:(tt + 1) * 128, :],
                    in_=aev[tt % 2][:, :]), "g", 16)
            aproj_g = B.cnt["g"]
            ph3_dve = B.cnt["dve"]
            B.flush(block, S)
        p23.close()

        with contextlib.ExitStack() as p4:
            xn_own2 = p4.enter_context(nc.sbuf_tensor("xn_own2", [128, 8 * TOWN], BF16))
            b_bc = p4.enter_context(nc.sbuf_tensor("b_bc", [128, 1024], F32))
            brow_o = p4.enter_context(nc.sbuf_tensor("brow_o", [1, D], F32))
            with contextlib.ExitStack() as p4a:
                xstage4 = p4a.enter_context(nc.sbuf_tensor("xstage4", [128, 4096], F32))
                xdump4 = p4a.enter_context(nc.sbuf_tensor("xdump4", [128, 4 * 1024], F32))
                stat4 = p4a.enter_context(nc.sbuf_tensor("stat4", [128, 64], F32))
                xn_tm4 = p4a.enter_context(nc.sbuf_tensor("xn_tm4", [128, 4096], BF16))
                tr_ps4 = [p4a.enter_context(nc.psum_tensor(f"trq{i}", [128, 1024], BF16))
                          for i in range(2)]
                bb_ps = p4a.enter_context(nc.psum_tensor("bb_ps", [128, 1024], F32))

                B.wait("tensor", "dve", ph3_dve)
                B.wait("sync", "pool", wout_ready)
                B.op("sync", lambda e: e.dma_start(out=brow_o[:, :], in_=b_out[:, :]),
                     "ld", 16)
                bo_ld = B.cnt["ld"]
                emit_ln(x_own, NTO, xn_own2, TOWN, xstage4, stat4, xdump4, xn_tm4,
                        tr_ps4)
                xn2_dve = B.cnt["dve"]

                B.wait("tensor", "ld", bo_ld)
                B.op("tensor", lambda e: e.matmul(
                    bb_ps[:, 0:512], onesb[0:1, 0:128], brow_o[0:1, 0:512],
                    start=True, stop=True))
                bb_pe = B.op("tensor", lambda e: e.matmul(
                    bb_ps[:, 512:1024], onesb[0:1, 0:128], brow_o[0:1, 512:1024],
                    start=True, stop=True), "pe")
                B.wait("vector", "pe", bb_pe)
                bb_dve = B.op("vector", lambda e: e.tensor_copy(
                    out=b_bc[:, :], in_=bb_ps[:, :]), "dve")
                B.flush(block, S)

            h_store = p4.enter_context(nc.sbuf_tensor("h_store", [128, 32 * 512], BF16))
            ztmp = p4.enter_context(nc.sbuf_tensor("ztmp", [128, 2 * 512], BF16))
            wpstage = [p4.enter_context(nc.sbuf_tensor(f"wpst{i}", [128, 1024], F32))
                       for i in range(2)]
            wproj_bf = [p4.enter_context(nc.sbuf_tensor(f"wpbf{i}", [128, 1024], BF16))
                        for i in range(2)]
            ybuf = p4.enter_context(nc.sbuf_tensor("ybuf", [128, 2 * 1024], F32))
            smem = [p4.enter_context(nc.sbuf_tensor(f"smem{i}", [128, 1024], F32))
                    for i in range(2)]
            stmp = [p4.enter_context(nc.sbuf_tensor(f"stmp{i}", [128, 1024], F32))
                    for i in range(2)]
            sacc = [p4.enter_context(nc.sbuf_tensor(f"sacc{i}", [128, 1024], F32))
                    for i in range(2)]
            xo_t = [p4.enter_context(nc.sbuf_tensor(f"xo_t{i}", [128, 1024], F32))
                    for i in range(2)]
            ap_t = [p4.enter_context(nc.sbuf_tensor(f"ap_t{i}", [128, 1024], F32))
                    for i in range(2)]
            o_stage = stmp

            wp_consumed = {}
            wp_ready = {}
            out_store = {}
            fin_mark = {0: 0, 1: 0}
            y_evac_act = 0
            y_pe_prev = 0
            for tc in range(4):
                with contextlib.ExitStack() as pth:
                    hps = [pth.enter_context(
                        nc.psum_tensor(f"hps{tc}_{i}", [128, 512], F32))
                        for i in range(2)]
                    if tc == 0:
                        B.wait("tensor", "act", wfc_ready)
                        B.wait("tensor", "dve", max(xn2_dve, bb_dve))
                    else:
                        B.wait("tensor", "act", y_evac_act)
                        B.wait("scalar", "pe", y_pe_prev)
                    hz_free = {}
                    hsq_done = {}
                    for fc in range(32):
                        par = fc % 2
                        bank = hps[par]
                        B.wait("tensor", "act", hz_free.get(par, 0))
                        hmm = None
                        for dc in range(8):
                            if dc < 7:
                                B.op("tensor",
                                    lambda e, bank=bank, dc=dc, fc=fc, tc=tc: e.matmul(
                                        bank[:, :],
                                        wfc_bf[:, dc * FF + fc * 128:
                                               dc * FF + (fc + 1) * 128],
                                        xn_own2[:, dc * TOWN + tc * 512:
                                                dc * TOWN + (tc + 1) * 512],
                                        start=(dc == 0), stop=False))
                            else:
                                hmm = B.op("tensor",
                                    lambda e, bank=bank, dc=dc, fc=fc, tc=tc: e.matmul(
                                        bank[:, :],
                                        wfc_bf[:, dc * FF + fc * 128:
                                               dc * FF + (fc + 1) * 128],
                                        xn_own2[:, dc * TOWN + tc * 512:
                                                dc * TOWN + (tc + 1) * 512],
                                        start=False, stop=True), "pe")
                        B.wait("scalar", "pe", hmm)
                        hz = B.op("scalar",
                            lambda e, bank=bank, par=par: e.activation(
                                ztmp[:, par * 512:(par + 1) * 512], bank[:, :],
                                AF.Relu), "act", drain=False)
                        B.wait("scalar", "act", hz)
                        hz_free[par] = hsq_done[par] = B.op("scalar",
                            lambda e, fc=fc, par=par: e.activation(
                                h_store[:, fc * 512:(fc + 1) * 512],
                                ztmp[:, par * 512:(par + 1) * 512],
                                AF.Square), "act")
                    h_dve = B.cnt["act"]
                    B.flush(block, S)
                with contextlib.ExitStack() as pty:
                    yps = [pty.enter_context(
                        nc.psum_tensor(f"yps{tc}_{i}", [128, 1024], F32))
                        for i in range(4)]

                    for weng in ("tensor", "scalar", "vector", "gpsimd", "sync"):
                        for wsem in ("pe", "act", "dve", "pool"):
                            B.wait(weng, wsem, B.cnt[wsem])
                    B.wait("tensor", "act", h_dve)
                    for fc in range(32):
                        par = fc % 2
                        B.wait("sync", "pe", wp_consumed.get(par, 0))
                        B.wait("gpsimd", "pe", wp_consumed.get(par, 0))
                        B.op("sync", lambda e, fc=fc, par=par: e.dma_start(
                            out=wpstage[par][:, 0:512],
                            in_=wprojT_in[fc * 128:(fc + 1) * 128, 0:512]),
                            "ld", 16)
                        B.op("gpsimd", lambda e, fc=fc, par=par: e.dma_start(
                            out=wpstage[par][:, 512:1024],
                            in_=wprojT_in[fc * 128:(fc + 1) * 128, 512:1024]),
                            "g", 16)
                        B.wait("scalar", "ld", B.cnt["ld"])
                        B.wait("scalar", "g", B.cnt["g"])
                        B.wait("scalar", "pe", wp_consumed.get(par, 0))
                        wp_ready[par] = B.op("scalar", lambda e, par=par: e.activation(
                            wproj_bf[par][:, :], wpstage[par][:, :], AF.Copy), "act")
                        B.wait("tensor", "act", wp_ready[par])
                        last = None
                        for ti in range(4):
                            B.op("tensor", lambda e, fc=fc, ti=ti, par=par: e.matmul(
                                yps[ti][:, 0:512],
                                h_store[:, fc * 512 + ti * 128:
                                        fc * 512 + (ti + 1) * 128],
                                wproj_bf[par][:, 0:512],
                                start=(fc == 0), stop=(fc == 31)))
                            if ti < 3:
                                B.op("tensor", lambda e, fc=fc, ti=ti, par=par:
                                    e.matmul(
                                        yps[ti][:, 512:1024],
                                        h_store[:, fc * 512 + ti * 128:
                                                fc * 512 + (ti + 1) * 128],
                                        wproj_bf[par][:, 512:1024],
                                        start=(fc == 0), stop=(fc == 31)))
                            else:
                                last = B.op("tensor", lambda e, fc=fc, ti=ti, par=par:
                                    e.matmul(
                                        yps[ti][:, 512:1024],
                                        h_store[:, fc * 512 + ti * 128:
                                                fc * 512 + (ti + 1) * 128],
                                        wproj_bf[par][:, 512:1024],
                                        start=(fc == 0), stop=(fc == 31)), "pe")
                        wp_consumed[par] = last
                    y_pe = B.cnt["pe"]
                    y_pe_prev = y_pe

                    for weng in ("tensor", "scalar", "vector", "gpsimd", "sync"):
                        for wsem in ("pe", "act", "dve", "pool"):
                            B.wait(weng, wsem, B.cnt[wsem])
                    for ti in range(4):
                        tt = tc * 4 + ti
                        par = ti % 2
                        eng = "vector"
                        esem = "dve"
                        B.wait("scalar", "pe", y_pe)
                        B.wait("scalar", esem, fin_mark[par])
                        yv = B.op("scalar", lambda e, ti=ti, par=par: e.activation(
                            ybuf[:, par * 1024:(par + 1) * 1024], yps[ti][:, :],
                            AF.Copy), "act")
                        y_evac_act = yv
                        B.wait("sync", esem, fin_mark[par])
                        B.op("sync", lambda e, tt=tt, par=par: e.dma_start(
                            out=xo_t[par][:, :],
                            in_=x_own[tt * 128:(tt + 1) * 128, :]), "ld", 16)
                        B.wait("gpsimd", "g", aproj_g)
                        B.wait("gpsimd", esem, fin_mark[par])
                        B.op("gpsimd", lambda e, tt=tt, par=par: e.dma_start(
                            out=ap_t[par][:, :],
                            in_=aproj_hbm[tt * 128:(tt + 1) * 128, :]), "g", 16)
                        fin_ld = B.cnt["ld"]
                        fin_g = B.cnt["g"]
                        B.wait(eng, "act", yv)
                        B.wait(eng, "dve", bb_dve)
                        yb = ybuf[:, par * 1024:(par + 1) * 1024]
                        sm, st_, sa = smem[par], stmp[par], sacc[par]

                        def sop(fn, eng=eng, esem=esem):
                            t = B.op(eng, fn, esem, drain=False)
                            B.wait(eng, esem, t)
                            return t
                        sop(lambda e, yb=yb, sa=sa: e.tensor_scalar(
                            sa[:, :], yb, 1.0, None, ALU.is_gt))
                        sop(lambda e, yb=yb, st_=st_: e.scalar_tensor_tensor(
                            st_[:, :], yb, 1.0, yb, ALU.is_gt, ALU.subtract))
                        sop(lambda e, yb=yb, sm=sm, st_=st_: e.scalar_tensor_tensor(
                            sm[:, :], yb, 0.9, st_[:, :], ALU.mult, ALU.subtract))
                        sop(lambda e, sm=sm, sa=sa: e.scalar_tensor_tensor(
                            sa[:, :], sm[:, :], 1.0, sa[:, :], ALU.is_gt, ALU.add))
                        for _ in range(3):
                            sop(lambda e, yb=yb, sm=sm, st_=st_:
                                e.scalar_tensor_tensor(
                                    st_[:, :], sm[:, :], 1.0, yb, ALU.is_gt,
                                    ALU.subtract))
                            sop(lambda e, sm=sm, st_=st_: e.scalar_tensor_tensor(
                                sm[:, :], sm[:, :], 0.9, st_[:, :], ALU.mult,
                                ALU.subtract))
                            sop(lambda e, sm=sm, sa=sa: e.scalar_tensor_tensor(
                                sa[:, :], sm[:, :], 1.0, sa[:, :], ALU.is_gt, ALU.add))
                        B.wait(eng, "ld", fin_ld)
                        B.wait(eng, "g", fin_g)
                        B.wait(eng, "st", out_store.get(par, 0))
                        sop(lambda e, sa=sa, par=par: e.scalar_tensor_tensor(
                            o_stage[par][:, :], sa[:, :], 0.2, ap_t[par][:, :],
                            ALU.mult, ALU.add))
                        sop(lambda e, par=par: e.tensor_tensor(
                            o_stage[par][:, :], o_stage[par][:, :], xo_t[par][:, :],
                            ALU.add))
                        fin_mark[par] = sop(lambda e, par=par: e.tensor_tensor(
                            o_stage[par][:, :], o_stage[par][:, :], b_bc[:, :],
                            ALU.add))
                        B.wait("sync", esem, fin_mark[par])
                        out_store[par] = B.op("sync", lambda e, tt=tt, par=par:
                            e.dma_start(
                                out=out_hbm[tt * 128:(tt + 1) * 128, :],
                                in_=o_stage[par][:, :]), "st", 16)

                    B.flush(block, S)

            B.wait("sync", "st", B.cnt["st"])
            B.flush(block, S)

    return nc


def _rope_tables(positions, scale):
    inv = 1.0 / (10000.0 ** (np.arange(0, 64, 2, dtype=np.float64) / 64.0))
    fr = positions.astype(np.float64)[None, :] * inv[:, None]
    c = np.cos(fr) * scale
    s = np.sin(fr) * scale
    bf = ml_dtypes.bfloat16
    cos_t = np.concatenate([c, c, c, c], axis=0).astype(np.float32)
    sin_t = np.concatenate([-s, s, -s, s], axis=0).astype(np.float32)
    return cos_t.astype(bf), sin_t.astype(bf)


def _build_masks(g):
    bf = ml_dtypes.bfloat16
    m = np.zeros((128, MASK_OFF[8]), dtype=np.float32)
    tri = (np.arange(128)[None, :] >= np.arange(128)[:, None]).astype(np.float32)
    for r in range(8):
        w = 512 - COL0[r]
        tile = np.zeros((128, w), dtype=np.float32)
        for col in range(w):
            gc = COL0[r] + col
            i = gc // 128
            if r < 2 * i + g:
                tile[:, col] = 1.0
            elif r == 2 * i + g:
                tile[:, col] = tri[:, gc % 128]
        m[:, MASK_OFF[r]:MASK_OFF[r + 1]] = tile
    return np.ascontiguousarray(((m - 1.0) * 100.0).astype(bf))


_NC_CACHE = []


def kernel(L0_attn_x, L0_attn_w_qkv, L0_attn_b_qkv, L0_attn_w_out, L0_attn_b_out,
           L0_ffn_W_fc, L0_ffn_W_proj):
    x = np.ascontiguousarray(np.asarray(L0_attn_x, dtype=np.float32))
    w_qkv = np.asarray(L0_attn_w_qkv, dtype=np.float32)
    b_qkv = np.asarray(L0_attn_b_qkv, dtype=np.float32)
    w_out = np.asarray(L0_attn_w_out, dtype=np.float32)
    b_out = np.asarray(L0_attn_b_out, dtype=np.float32)
    W_fc = np.asarray(L0_ffn_W_fc, dtype=np.float32)
    W_proj = np.asarray(L0_ffn_W_proj, dtype=np.float32)
    bf = ml_dtypes.bfloat16

    perm64 = np.concatenate([np.arange(0, 64, 2), np.arange(1, 64, 2)])
    qk_rows = np.concatenate([64 * h + perm64 for h in range(16)])
    wqT = np.ascontiguousarray(w_qkv[qk_rows, :].T)
    wkT = np.ascontiguousarray(w_qkv[1024 + qk_rows, :].T)
    bq_row = np.ascontiguousarray(b_qkv[qk_rows][None, :])
    bk_row = np.ascontiguousarray(b_qkv[1024 + qk_rows][None, :])
    wvT_full = w_qkv[2048:3072, :].T
    wvT = np.zeros((D, NPAIR * 130), dtype=np.float32)
    bv_row = np.zeros((1, NPAIR * 130), dtype=np.float32)
    bv_full = b_qkv[2048:3072]
    for p in range(NPAIR):
        wvT[:, p * 130: p * 130 + 64] = wvT_full[:, p * 128: p * 128 + 64]
        wvT[:, p * 130 + 65: p * 130 + 129] = wvT_full[:, p * 128 + 64:(p + 1) * 128]
        bv_row[0, p * 130: p * 130 + 64] = bv_full[p * 128: p * 128 + 64]
        bv_row[0, p * 130 + 64] = 1.0
        bv_row[0, p * 130 + 65: p * 130 + 129] = bv_full[p * 128 + 64:(p + 1) * 128]
        bv_row[0, p * 130 + 129] = 1.0
    woutT = np.ascontiguousarray(w_out.T)
    wfcT = np.ascontiguousarray(W_fc.T)
    wprojT = np.ascontiguousarray(W_proj.T)

    ident = np.eye(128, dtype=np.float32).astype(bf)
    sigma = np.concatenate([np.arange(32, 64), np.arange(0, 32),
                            np.arange(96, 128), np.arange(64, 96)])
    permM = np.zeros((128, 128), dtype=np.float32)
    permM[sigma, np.arange(128)] = 1.0
    permM = permM.astype(bf)

    rk_cos, rk_sin = _rope_tables(np.arange(T), 1.0)

    if not _NC_CACHE:
        _NC_CACHE.append(build_program())
    nc = _NC_CACHE[0]

    in_maps = []
    own_idx_all = []
    for c in range(8):
        b, g = c // 2, c % 2
        own_idx = np.concatenate(
            [np.arange(128 * (2 * j + g), 128 * (2 * j + g) + 128)
             for j in range(NTO)])
        own_idx_all.append(own_idx)
        xb = x[b]
        q_cos, q_sin = _rope_tables(own_idx, 0.125)
        in_maps.append({
            "x_seq": xb,
            "x_own": np.ascontiguousarray(xb[own_idx]),
            "wqT": wqT, "wkT": wkT, "wvT": wvT,
            "bq": bq_row, "bk": bk_row, "bv": bv_row,
            "woutT": woutT, "b_out": np.ascontiguousarray(b_out[None, :]),
            "wfcT": wfcT, "wprojT": wprojT,
            "ropeq_cos": q_cos, "ropeq_sin": q_sin,
            "ropek_cos": rk_cos, "ropek_sin": rk_sin,
            "masks": _build_masks(g),
            "ident": ident, "permT": permM,
        })

    res = run_bass_kernel_spmd(nc, in_maps, core_ids=list(range(8)))

    out = np.empty((4, T, D), dtype=np.float32)
    for c in range(8):
        b = c // 2
        out[b, own_idx_all[c], :] = np.asarray(res.results[c]["out"],
                                               dtype=np.float32)
    return out


# version 35
# speedup vs baseline: 1.1658x; 1.1658x over previous
import contextlib
import numpy as np
import ml_dtypes

import concourse.bass as bass
import concourse.mybir as mybir
from concourse.bass_utils import run_bass_kernel_spmd

F32 = mybir.dt.float32
BF16 = mybir.dt.bfloat16
AF = mybir.ActivationFunctionType
ALU = mybir.AluOpType
AX = mybir.AxisListType

D = 1024
T = 4096
TOWN = 2048
NPAIR = 8
FF = 4096
NT = T // 128
NTO = TOWN // 128
EPS = 1e-5

ENGINES = ("sync", "vector", "scalar", "tensor", "gpsimd")
COL0 = [0, 0, 128, 128, 256, 256, 384, 384]
MASK_OFF = np.cumsum([0] + [512 - c for c in COL0]).tolist()


class Builder:
    def __init__(self):
        self.ops = {e: [] for e in ENGINES}
        self.cnt = {s: 0 for s in ("ld", "g", "st", "pe", "act", "dve", "pool")}

    def inc(self, sem, n):
        self.cnt[sem] += n
        return self.cnt[sem]

    def wait(self, eng, sem, val):
        if val and val > 0:
            self.ops[eng].append(lambda e, S, s=sem, v=val: e.wait_ge(S[s], v))

    def op(self, eng, fn, inc_sem=None, inc_n=1, drain=None):
        if inc_sem is None:
            self.ops[eng].append(lambda e, S, f=fn: f(e))
            return None
        if drain is None:
            drain = inc_sem == "pool"
        tgt = self.inc(inc_sem, inc_n)
        if drain:
            self.ops[eng].append(lambda e, S, f=fn: f(e))
            self.ops[eng].append(
                lambda e, S, s=inc_sem, n=inc_n: e.drain().then_inc(S[s], n))
        else:
            self.ops[eng].append(
                lambda e, S, f=fn, s=inc_sem, n=inc_n: f(e).then_inc(S[s], n))
        return tgt

    def flush(self, block, S):
        for eng in ENGINES:
            ops = self.ops[eng]
            if not ops:
                continue
            self.ops[eng] = []

            def body(e, ops=ops):
                for fn in ops:
                    fn(e, S)
            getattr(block, eng)(body)


def build_program():
    nc = bass.Bass("TRN2", target_bir_lowering=False)

    def din(name, shape, dt=F32):
        return nc.dram_tensor(name, shape, dt, kind="ExternalInput").ap()

    x_seq = din("x_seq", [T, D])
    x_own = din("x_own", [TOWN, D])
    wqT = din("wqT", [D, D])
    wkT = din("wkT", [D, D])
    wvT = din("wvT", [D, NPAIR * 130])
    bq = din("bq", [1, D])
    bk = din("bk", [1, D])
    bv = din("bv", [1, NPAIR * 130])
    woutT_in = din("woutT", [D, D])
    b_out = din("b_out", [1, D])
    wfcT_in = din("wfcT", [D, FF])
    wprojT_in = din("wprojT", [FF, D])
    rq_cos = din("ropeq_cos", [128, TOWN], BF16)
    rq_sin = din("ropeq_sin", [128, TOWN], BF16)
    rk_cos = din("ropek_cos", [128, T], BF16)
    rk_sin = din("ropek_sin", [128, T], BF16)
    masks_in = din("masks", [128, MASK_OFF[8]], BF16)
    ident_in = din("ident", [128, 128], BF16)
    perm_in = din("permT", [128, 128], BF16)

    out_hbm = nc.dram_tensor("out", [TOWN, D], F32, kind="ExternalOutput").ap()
    aproj_hbm = nc.dram_tensor("aproj_scratch", [TOWN, D], F32).ap()

    B = Builder()

    with contextlib.ExitStack() as top:
        S = {}
        for s in ("ld", "g", "st", "pe", "act", "dve", "pool"):
            S[s] = top.enter_context(nc.semaphore(s))
        block = top.enter_context(nc.Block())

        onesb = top.enter_context(nc.sbuf_tensor("onesb", [65, 128], F32, side="right"))
        ones_bf = top.enter_context(nc.sbuf_tensor("ones_bf", [65, 512], BF16, side="right"))
        ident = top.enter_context(nc.sbuf_tensor("ident_sb", [128, 128], BF16, side="right"))
        permT = top.enter_context(nc.sbuf_tensor("permT_sb", [128, 128], BF16, side="right"))
        epscol = top.enter_context(nc.sbuf_tensor("epscol", [128, 1], F32, side="right"))

        B.op("gpsimd", lambda e: e.memset(onesb[:, :], 1.0), "pool")
        B.op("gpsimd", lambda e: e.memset(ones_bf[:, :], 1.0), "pool")
        B.op("gpsimd", lambda e: e.memset(epscol[:, :], EPS), "pool")
        for dst, src in ((ident, ident_in), (permT, perm_in)):
            B.op("sync", lambda e, d=dst, s=src: e.dma_start(out=d[:, :], in_=s[:, :]),
                 "ld", 16)
        consts_ld = B.cnt["ld"]
        consts_pool = B.cnt["pool"]
        B.wait("scalar", "pool", consts_pool)

        xnf = top.enter_context(nc.sbuf_tensor("xnf", [128, 8 * T], BF16))
        p12 = contextlib.ExitStack()
        q_all = p12.enter_context(nc.sbuf_tensor("q_all", [128, NPAIR * TOWN], BF16))

        def emit_ln(src_hbm, ntiles, dst_fm, dst_cols, xstage, stat, xdump,
                    xn_tm, tr_ps, hook=None):
            nslots = 4
            dve0 = B.cnt["dve"]
            slot_free = {}
            tr_free = {}
            red2_prev = 0
            for t0 in range(0, ntiles, 4):
                nb = min(4, ntiles - t0)
                sls = [(t0 + i) % nslots for i in range(nb)]
                for i in range(nb):
                    t = t0 + i
                    B.wait("sync", "dve", slot_free.get(sls[i], dve0))
                    B.op("sync", lambda e, slot=sls[i], t=t: e.dma_start(
                        out=xstage[:, slot * 1024:(slot + 1) * 1024],
                        in_=src_hbm[t * 128:(t + 1) * 128, :]), "ld", 16)
                ld_done = B.cnt["ld"]
                B.wait("vector", "ld", ld_done)
                for s in sls:
                    B.op("vector", lambda e, s=s: e.reduce_sum(
                        stat[:, s * 8:s * 8 + 1],
                        xstage[:, s * 1024:(s + 1) * 1024], AX.X))
                B.wait("scalar", "ld", ld_done)
                B.wait("scalar", "dve", red2_prev)
                sq = None
                for s in sls:
                    sq = B.op("scalar", lambda e, s=s: e.activation(
                        xdump[:, s * 1024:(s + 1) * 1024],
                        xstage[:, s * 1024:(s + 1) * 1024], AF.Square), "act")
                B.wait("vector", "act", sq)
                red2_prev = 0
                for s in sls:
                    red2_prev = B.op("vector", lambda e, s=s: e.reduce_sum(
                        stat[:, s * 8 + 1:s * 8 + 2],
                        xdump[:, s * 1024:(s + 1) * 1024], AX.X), "dve")
                for s in sls:
                    B.op("vector", lambda e, s0=s * 8: e.tensor_scalar(
                        stat[:, s0:s0 + 1], stat[:, s0:s0 + 1], -1.0 / D, None,
                        ALU.mult))
                for s in sls:
                    B.op("vector", lambda e, s0=s * 8: e.tensor_scalar(
                        stat[:, s0 + 2:s0 + 3], stat[:, s0 + 1:s0 + 2], 1.0 / D, None,
                        ALU.mult))
                for s in sls:
                    B.op("vector", lambda e, s0=s * 8: e.tensor_tensor(
                        stat[:, s0 + 3:s0 + 4], stat[:, s0:s0 + 1], stat[:, s0:s0 + 1],
                        ALU.mult))
                last = None
                for s in sls:
                    last = B.op("vector", lambda e, s0=s * 8: e.tensor_tensor(
                        stat[:, s0 + 4:s0 + 5], stat[:, s0 + 2:s0 + 3],
                        stat[:, s0 + 3:s0 + 4], ALU.subtract), "dve")
                B.wait("scalar", "dve", last)
                la = None
                for s in sls:
                    la = B.op("scalar", lambda e, s0=s * 8: e.activation(
                        stat[:, s0 + 5:s0 + 6], stat[:, s0 + 4:s0 + 5], AF.Sqrt,
                        bias=epscol[:, 0:1]), "act")
                B.wait("vector", "act", la)
                for s in sls:
                    B.op("vector", lambda e, s0=s * 8: e.reciprocal(
                        stat[:, s0 + 6:s0 + 7], stat[:, s0 + 5:s0 + 6]))
                for i in range(nb):
                    s = sls[i]
                    slot_free[s] = B.op("vector", lambda e, slot=s, s0=s * 8:
                        e.tensor_scalar(
                            xn_tm[:, slot * 1024:(slot + 1) * 1024],
                            xstage[:, slot * 1024:(slot + 1) * 1024],
                            stat[:, s0:s0 + 1], stat[:, s0 + 6:s0 + 7],
                            ALU.add, ALU.mult), "dve")
                if hook is not None:
                    hook(t0 // 4)
                for i in range(nb):
                    t = t0 + i
                    slot = sls[i]
                    B.wait("tensor", "dve", slot_free[slot])
                    for c in range(8):
                        k = t * 8 + c
                        pb = tr_ps[k % 2]
                        B.wait("tensor", "dve", tr_free.get(k % 2, dve0))
                        B.op("tensor", lambda e, pb=pb, slot=slot, c=c: e.transpose(
                            pb[:, 0:128],
                            xn_tm[:, slot * 1024 + c * 128: slot * 1024 + (c + 1) * 128],
                            ident[:, :]), "pe")
                        B.wait("vector", "pe", B.cnt["pe"])
                        tr_free[k % 2] = B.op("vector",
                            lambda e, pb=pb, c=c, t=t: e.tensor_copy(
                                out=dst_fm[:, c * dst_cols + t * 128:
                                           c * dst_cols + t * 128 + 128],
                                in_=pb[:, 0:128]), "dve")

        def emit_rope(pre_ap, perm_bank, cosap, sinap, dstap, t1, t2, pre_act):
            B.wait("tensor", "act", pre_act)
            pe_perm = B.op("tensor", lambda e: e.matmul(
                perm_bank, permT[:, :], pre_ap, start=True, stop=True), "pe")
            B.wait("vector", "act", pre_act)
            B.op("vector", lambda e: e.tensor_tensor(t1[:, :], pre_ap, cosap, ALU.mult))
            B.wait("vector", "pe", pe_perm)
            t2done = B.op("vector", lambda e: e.tensor_tensor(t2[:, :], perm_bank, sinap,
                                                              ALU.mult), "dve")
            B.wait("vector", "dve", t2done)
            return B.op("vector", lambda e: e.tensor_tensor(dstap, t1[:, :], t2[:, :],
                                                            ALU.add), "dve")

        with contextlib.ExitStack() as p1:
            xn_own = p1.enter_context(nc.sbuf_tensor("xn_own", [128, 8 * TOWN], BF16))
            xstage = p1.enter_context(nc.sbuf_tensor("xstage", [128, 4 * 1024], F32))
            xdump = p1.enter_context(nc.sbuf_tensor("xdump", [128, 4 * 1024], F32))
            stat = p1.enter_context(nc.sbuf_tensor("stat", [128, 64], F32))
            xn_tm = p1.enter_context(nc.sbuf_tensor("xn_tm", [128, 4 * 1024], BF16))
            tr_ps = [p1.enter_context(nc.psum_tensor(f"trp{i}", [128, 1024], BF16))
                     for i in range(2)]
            brow_q = p1.enter_context(nc.sbuf_tensor("brow_q", [1, D], F32))
            brow_qb = p1.enter_context(nc.sbuf_tensor("brow_qb", [1, D], BF16))
            rqc = p1.enter_context(nc.sbuf_tensor("rqc", [128, TOWN], BF16))
            rqs = p1.enter_context(nc.sbuf_tensor("rqs", [128, TOWN], BF16))
            wstage = p1.enter_context(nc.sbuf_tensor("wstage", [128, 1024], F32))
            wq_bf = p1.enter_context(nc.sbuf_tensor("wq_bf", [128, 1024], BF16))
            qpre = p1.enter_context(nc.sbuf_tensor("qpre", [128, 2 * 512], BF16))
            t1b = p1.enter_context(nc.sbuf_tensor("t1b", [128, 512], BF16))
            t2b = p1.enter_context(nc.sbuf_tensor("t2b", [128, 512], BF16))
            pr_ps = [p1.enter_context(nc.psum_tensor(f"prp{i}", [128, 512], F32))
                     for i in range(2)]
            pm_ps = [p1.enter_context(nc.psum_tensor(f"pmp{i}", [128, 512], F32))
                     for i in range(2)]

            B.op("sync", lambda e: e.dma_start(out=rqc[:, :], in_=rq_cos[:, :]), "ld", 16)
            B.op("sync", lambda e: e.dma_start(out=rqs[:, :], in_=rq_sin[:, :]), "ld", 16)
            B.op("sync", lambda e: e.dma_start(out=brow_q[:, :], in_=bq[:, :]), "ld", 16)
            rope_tab_ld = B.cnt["ld"]
            B.wait("vector", "ld", rope_tab_ld)
            bqb_dve = B.op("vector", lambda e: e.tensor_copy(out=brow_qb[:, :],
                                                             in_=brow_q[:, :]), "dve")
            B.wait("tensor", "dve", bqb_dve)
            B.wait("tensor", "ld", max(consts_ld, rope_tab_ld))
            B.wait("tensor", "pool", consts_pool)
            emit_ln(x_own, NTO, xn_own, TOWN, xstage, stat, xdump, xn_tm, tr_ps)
            ln_done_dve = B.cnt["dve"]

            B.wait("tensor", "dve", ln_done_dve)
            B.wait("vector", "ld", rope_tab_ld)
            proj_free = {}
            rope_done = {}

            def emit_qpair(p):
                B.wait("sync", "pool", B.cnt["pool"])
                B.op("sync", lambda e, p=p: e.dma_start(
                    out=wstage[:, :].rearrange("q (dc f) -> q dc f", dc=8),
                    in_=wqT[:, p * 128:(p + 1) * 128]
                    .rearrange("(dc q) f -> q dc f", q=128)), "ld", 16)
                B.wait("gpsimd", "ld", B.cnt["ld"])
                B.wait("gpsimd", "pe", B.cnt["pe"])
                wq_ready = B.op("gpsimd", lambda e: e.tensor_copy(
                    out=wq_bf[:, :], in_=wstage[:, :]), "pool")
                B.wait("tensor", "pool", wq_ready)
                for ch in range(4):
                    k = p * 4 + ch
                    par = k % 2
                    bank = pr_ps[par]
                    B.wait("tensor", "act", proj_free.get(par, 0))
                    for dc in range(8):
                        B.op("tensor", lambda e, bank=bank, dc=dc, ch=ch: e.matmul(
                            bank[:, :],
                            wq_bf[:, dc * 128:(dc + 1) * 128],
                            xn_own[:, dc * TOWN + ch * 512: dc * TOWN + (ch + 1) * 512],
                            start=(dc == 0), stop=False))
                    pe_proj = B.op("tensor", lambda e, bank=bank, p=p: e.matmul(
                        bank[:, :], brow_qb[0:1, p * 128:(p + 1) * 128],
                        ones_bf[0:1, 0:512], start=False, stop=True), "pe")
                    B.wait("scalar", "pe", pe_proj)
                    B.wait("scalar", "dve", rope_done.get(par, 0))
                    proj_free[par] = B.op("scalar", lambda e, bank=bank, par=par:
                        e.activation(qpre[:, par * 512:(par + 1) * 512], bank[:, :],
                                     AF.Copy), "act")
                    B.wait("tensor", "dve", rope_done.get(par, 0))
                    rope_done[par] = emit_rope(
                        qpre[:, par * 512:(par + 1) * 512],
                        pm_ps[par][:, :],
                        rqc[:, ch * 512:(ch + 1) * 512],
                        rqs[:, ch * 512:(ch + 1) * 512],
                        q_all[:, p * TOWN + ch * 512: p * TOWN + (ch + 1) * 512],
                        t1b, t2b, proj_free[par])

            emit_ln(x_seq, NT, xnf, T, xstage, stat, xdump, xn_tm, tr_ps,
                    hook=emit_qpair)
            q_done_dve = B.cnt["dve"]
            B.flush(block, S)


        p23 = contextlib.ExitStack()
        out_norm = p23.enter_context(
            nc.sbuf_tensor("out_norm", [128, NPAIR * TOWN], BF16, side="right"))
        with contextlib.ExitStack() as p2:
            brow_k = p2.enter_context(nc.sbuf_tensor("brow_k", [1, D], F32))
            brow_kb = p2.enter_context(nc.sbuf_tensor("brow_kb", [1, D], BF16))
            brow_vb = p2.enter_context(nc.sbuf_tensor("brow_vb", [1, NPAIR * 130], BF16))
            rkc = p2.enter_context(nc.sbuf_tensor("rkc", [128, T], BF16))
            rks = p2.enter_context(nc.sbuf_tensor("rks", [128, T], BF16))
            masks = p2.enter_context(
                nc.sbuf_tensor("masks_sb", [128, MASK_OFF[8]], BF16))
            k_bf = p2.enter_context(nc.sbuf_tensor("k_bf", [128, T], BF16))
            v_store = p2.enter_context(
                nc.sbuf_tensor("v_store", [128, NT * 130], BF16))
            wstage2 = p2.enter_context(nc.sbuf_tensor("wstage2", [128, 1040], F32))
            wk_bf = p2.enter_context(nc.sbuf_tensor("wk_bf", [128, 1024], BF16))
            wv_bf = p2.enter_context(nc.sbuf_tensor("wv_bf", [128, 1040], BF16))
            kpre = p2.enter_context(nc.sbuf_tensor("kpre", [128, 2 * 512], BF16))
            t1k = p2.enter_context(nc.sbuf_tensor("t1k", [128, 512], BF16))
            t2k = p2.enter_context(nc.sbuf_tensor("t2k", [128, 512], BF16))
            p_t = p2.enter_context(nc.sbuf_tensor("p_t", [128, 4 * 1024], BF16))
            rowbuf = p2.enter_context(nc.sbuf_tensor("rowbuf", [65, 1024], F32))
            recipbf = p2.enter_context(nc.sbuf_tensor("recipbf", [65, 1024], BF16))
            evacA = p2.enter_context(nc.sbuf_tensor("evacA", [64, 512], BF16))
            evacB = p2.enter_context(nc.sbuf_tensor("evacB", [64, 512], BF16))
            normA = p2.enter_context(nc.sbuf_tensor("normA", [64, 512], BF16))
            normB = p2.enter_context(nc.sbuf_tensor("normB", [64, 512], BF16))
            cstat = p2.enter_context(nc.sbuf_tensor("cstat", [128, 4], F32))
            negc = p2.enter_context(nc.sbuf_tensor("negc", [128, 1], F32))
            s_ps = [p2.enter_context(nc.psum_tensor(f"sps{i}", [128, 1024], F32))
                    for i in range(2)]
            o_ps = p2.enter_context(nc.psum_tensor("ops", [128, 1024], F32))
            bc_ps = [p2.enter_context(nc.psum_tensor(f"bcp{i}", [64, 512], F32))
                     for i in range(2)]

            B.wait("sync", "dve", q_done_dve)
            B.op("sync", lambda e: e.dma_start(out=brow_k[:, :], in_=bk[:, :]), "ld", 16)
            B.op("sync", lambda e: e.dma_start(out=wstage2[0:1, 0:NPAIR * 130],
                                               in_=bv[:, :]), "ld", 16)
            B.op("sync", lambda e: e.dma_start(out=rkc[:, :], in_=rk_cos[:, :]),
                 "ld", 16)
            B.op("sync", lambda e: e.dma_start(out=rks[:, :], in_=rk_sin[:, :]),
                 "ld", 16)
            B.op("sync", lambda e: e.dma_start(out=masks[:, :], in_=masks_in[:, :]),
                 "ld", 16)
            ph2_ld = B.cnt["ld"]
            B.wait("vector", "ld", ph2_ld)
            B.wait("tensor", "ld", ph2_ld)
            B.op("vector", lambda e: e.tensor_copy(out=brow_kb[:, :], in_=brow_k[:, :]))
            bkb_dve = B.op("vector", lambda e: e.tensor_copy(
                out=brow_vb[:, :], in_=wstage2[0:1, 0:NPAIR * 130]), "dve")
            B.wait("tensor", "dve", bkb_dve)
            B.wait("sync", "dve", bkb_dve)

            jglob = 0
            sfree_mark = {}
            ptfree_mark = {}
            evac_mark = 0
            place_prev = 0
            v_done_act = 0

            wfcst = {"ready": 0, "next": 0}

            def emit_wfc_group(n8):
                for k in range(8):
                    n = n8 * 8 + k
                    piece, dc = n // 8, n % 8
                    B.wait("sync", "act", wfcst["ready"])
                    B.op("sync", lambda e, piece=piece, dc=dc: e.dma_start(
                        out=wstage2[:, 0:1024],
                        in_=wfcT_in[dc * 128:(dc + 1) * 128,
                                    piece * 1024:(piece + 1) * 1024]), "ld", 16)
                    B.wait("scalar", "ld", B.cnt["ld"])
                    wfcst["ready"] = B.op("scalar",
                        lambda e, piece=piece, dc=dc: e.activation(
                            xnf[:, dc * FF + piece * 1024:
                                dc * FF + (piece + 1) * 1024],
                            wstage2[:, 0:1024], AF.Copy), "act")

            def emit_norm(p, qc):
                nonlocal evac_mark, place_prev
                last_pv = B.cnt["pe"]
                B.wait("scalar", "pe", last_pv)
                B.op("scalar", lambda e: e.activation(
                    rowbuf[64:65, 0:512], o_ps[64:65, 0:512], AF.Ln))
                B.op("scalar", lambda e: e.activation(
                    rowbuf[64:65, 512:1024], o_ps[64:65, 512:1024], AF.Ln))
                B.op("scalar", lambda e: e.activation(
                    recipbf[64:65, 0:512], rowbuf[64:65, 0:512], AF.Exp,
                    scale=-1.0))
                rows_act = B.op("scalar", lambda e: e.activation(
                    recipbf[64:65, 512:1024], rowbuf[64:65, 512:1024], AF.Exp,
                    scale=-1.0), "act")
                B.wait("tensor", "act", rows_act)
                B.wait("tensor", "dve", evac_mark)
                B.op("tensor", lambda e: e.matmul(
                    bc_ps[0][0:64, :], ones_bf[64:65, 0:64],
                    recipbf[64:65, 0:512], start=True, stop=True))
                bc_pe = B.op("tensor", lambda e: e.matmul(
                    bc_ps[1][0:64, :], ones_bf[64:65, 0:64],
                    recipbf[64:65, 512:1024], start=True, stop=True), "pe")
                B.wait("vector", "pe", bc_pe)
                B.wait("vector", "g", place_prev)
                B.op("vector", lambda e: e.tensor_copy(
                    out=evacA[:, :], in_=o_ps[0:64, 0:512]))
                B.op("vector", lambda e: e.tensor_copy(
                    out=evacB[:, :], in_=o_ps[0:64, 512:1024]))
                B.op("vector", lambda e: e.tensor_tensor(
                    normA[:, :], evacA[:, :], bc_ps[0][0:64, :], ALU.mult))
                evac_mark = B.op("vector", lambda e: e.tensor_tensor(
                    normB[:, :], evacB[:, :], bc_ps[1][0:64, :], ALU.mult),
                    "dve")
                B.wait("gpsimd", "dve", evac_mark)
                B.op("gpsimd", lambda e, p=p, qc=qc: e.dma_start(
                    out=out_norm[0:64, p * TOWN + qc * 512:
                                 p * TOWN + (qc + 1) * 512],
                    in_=normA[:, :]), "g", 16)
                place_prev = B.op("gpsimd", lambda e, p=p, qc=qc: e.dma_start(
                    out=out_norm[64:128, p * TOWN + qc * 512:
                                 p * TOWN + (qc + 1) * 512],
                    in_=normB[:, :]), "g", 16)
                if p == NPAIR - 1:
                    if qc == 0:
                        B.wait("sync", "pool", B.cnt["pool"])
                    emit_wfc_group(qc)

            for p in range(NPAIR):
                B.wait("sync", "pool", B.cnt["pool"])
                B.op("sync", lambda e, p=p: e.dma_start(
                    out=wstage2[:, 0:1024].rearrange("q (dc f) -> q dc f", dc=8),
                    in_=wkT[:, p * 128:(p + 1) * 128]
                    .rearrange("(dc q) f -> q dc f", q=128)), "ld", 16)
                B.wait("gpsimd", "ld", B.cnt["ld"])
                B.wait("gpsimd", "pe", B.cnt["pe"])
                wk_ready = B.op("gpsimd", lambda e: e.tensor_copy(
                    out=wk_bf[:, :], in_=wstage2[:, 0:1024]), "pool")
                B.wait("sync", "pool", wk_ready)
                B.op("sync", lambda e, p=p: e.dma_start(
                    out=wstage2[:, 0:1040].rearrange("q (dc f) -> q dc f", dc=8),
                    in_=wvT[:, p * 130:(p + 1) * 130]
                    .rearrange("(dc q) f -> q dc f", q=128)), "ld", 16)
                B.wait("gpsimd", "ld", B.cnt["ld"])
                B.wait("gpsimd", "pe", B.cnt["pe"])
                wv_ready = B.op("gpsimd", lambda e: e.tensor_copy(
                    out=wv_bf[:, :], in_=wstage2[:, 0:1040]), "pool")

                prev_attn_pe = B.cnt["pe"]
                kproj_free = dict(sfree_mark)
                kpre_free = {}
                kperm_free = {0: evac_mark, 1: evac_mark}
                B.wait("tensor", "pool", wk_ready)
                B.wait("vector", "pe", prev_attn_pe)
                for ch in range(8):
                    par = ch % 2
                    bank = s_ps[par]
                    B.wait("tensor", "act", kproj_free.get(par, 0))
                    for dc in range(8):
                        B.op("tensor", lambda e, bank=bank, dc=dc, ch=ch: e.matmul(
                            bank[:, 0:512],
                            wk_bf[:, dc * 128:(dc + 1) * 128],
                            xnf[:, dc * T + ch * 512: dc * T + (ch + 1) * 512],
                            start=(dc == 0), stop=False))
                    pe_kproj = B.op("tensor", lambda e, bank=bank, p=p: e.matmul(
                        bank[:, 0:512], brow_kb[0:1, p * 128:(p + 1) * 128],
                        ones_bf[0:1, 0:512], start=False, stop=True), "pe")
                    B.wait("scalar", "pe", pe_kproj)
                    B.wait("scalar", "dve", kpre_free.get(par, 0))
                    kproj_free[par] = B.op("scalar",
                        lambda e, bank=bank, par=par: e.activation(
                            kpre[:, par * 512:(par + 1) * 512], bank[:, 0:512],
                            AF.Copy), "act")
                    B.wait("tensor", "dve", kperm_free.get(par, 0))
                    kpre_free[par] = kperm_free[par] = emit_rope(
                        kpre[:, par * 512:(par + 1) * 512],
                        o_ps[:, par * 512:(par + 1) * 512],
                        rkc[:, ch * 512:(ch + 1) * 512],
                        rks[:, ch * 512:(ch + 1) * 512],
                        k_bf[:, ch * 512:(ch + 1) * 512],
                        t1k, t2k, kproj_free[par])
                kv_rope_dve = B.cnt["dve"]

                vproj_free = dict(kproj_free)
                B.wait("tensor", "pool", wv_ready)
                B.wait("scalar", "pe", prev_attn_pe)
                for tt in range(NT):
                    par = tt % 2
                    bank = s_ps[par]
                    B.wait("tensor", "act", vproj_free.get(par, 0))
                    for dc in range(8):
                        B.op("tensor", lambda e, bank=bank, dc=dc, tt=tt: e.matmul(
                            bank[:, 0:130],
                            xnf[:, dc * T + tt * 128: dc * T + (tt + 1) * 128],
                            wv_bf[:, dc * 130:(dc + 1) * 130],
                            start=(dc == 0), stop=False))
                    pe_v = B.op("tensor", lambda e, bank=bank, p=p: e.matmul(
                        bank[:, 0:130], ones_bf[0:1, 0:128],
                        brow_vb[0:1, p * 130:(p + 1) * 130], start=False,
                        stop=True), "pe")
                    B.wait("scalar", "pe", pe_v)
                    vproj_free[par] = B.op("scalar",
                        lambda e, bank=bank, tt=tt: e.activation(
                            v_store[:, tt * 130:(tt + 1) * 130], bank[:, 0:130],
                            AF.Copy), "act")
                v_done_act = B.cnt["act"]

                B.wait("vector", "dve", kv_rope_dve)
                B.op("vector", lambda e, p=p: e.reduce_sum(
                    cstat[:, 0:1], q_all[:, p * TOWN:(p + 1) * TOWN:16], AX.X))
                cs_dve = B.op("vector", lambda e: e.reduce_sum(
                    cstat[:, 1:2], k_bf[:, ::32], AX.X), "dve")
                B.wait("tensor", "dve", max(cs_dve, evac_mark))
                cpe = B.op("tensor", lambda e: e.matmul(
                    bc_ps[0][0:1, 0:1], cstat[:, 1:2], cstat[:, 0:1],
                    start=True, stop=True), "pe")
                B.wait("vector", "pe", cpe)
                cdve = B.op("vector", lambda e: e.tensor_copy(
                    out=cstat[0:1, 3:4], in_=bc_ps[0][0:1, 0:1]), "dve")
                B.wait("tensor", "dve", cdve)
                B.wait("tensor", "act", v_done_act)
                cpe2 = B.op("tensor", lambda e: e.matmul(
                    s_ps[0][:, 0:1], onesb[0:1, 0:128], cstat[0:1, 3:4],
                    start=True, stop=True), "pe")
                B.wait("vector", "pe", cpe2)
                negc_dve = B.op("vector", lambda e: e.tensor_scalar(
                    negc[:, 0:1], s_ps[0][:, 0:1], -0.5 / (128.0 * 128.0), None,
                    ALU.mult), "dve")
                B.wait("scalar", "dve", negc_dve)

                B.wait("tensor", "dve", negc_dve)
                steps = [(qc, j) for qc in range(4) for j in range(8 * qc + 8)]
                pending = []

                def emit_pv(ctx):
                    qc, j, pt3, c0, gsem, gval = ctx
                    nsteps = 8 * qc + 8
                    B.wait("tensor", gsem, gval)
                    if j == 0:
                        B.wait("tensor", "dve", evac_mark)
                    B.op("tensor", lambda e, j=j, pt3=pt3, c0=c0, n=nsteps:
                        e.matmul(
                            o_ps[0:65, c0:512],
                            v_store[:, j * 130: j * 130 + 65],
                            p_t[:, pt3 * 1024 + c0: pt3 * 1024 + 512],
                            start=(j == 0), stop=(j == n - 1)))
                    pv_pe = B.op("tensor",
                        lambda e, j=j, pt3=pt3, c0=c0, n=nsteps: e.matmul(
                            o_ps[0:65, 512 + c0:1024],
                            v_store[:, j * 130 + 65: j * 130 + 130],
                            p_t[:, pt3 * 1024 + 512 + c0: pt3 * 1024 + 1024],
                            start=(j == 0), stop=(j == n - 1)), "pe")
                    ptfree_mark[pt3] = pv_pe

                def drain_one():
                    ctx = pending.pop(0)
                    emit_pv(ctx)
                    if ctx[1] == 8 * ctx[0] + 7:
                        emit_norm(p, ctx[0])

                for (qc, j) in steps:
                    r = j - 8 * qc
                    c0 = COL0[r] if r >= 0 else 0
                    qbase = p * TOWN + qc * 512
                    par = jglob % 2
                    pt3 = jglob % 4
                    bank = s_ps[par]
                    B.wait("tensor", "act", sfree_mark.get(par, 0))
                    B.op("tensor",
                         lambda e, bank=bank, j=j, c0=c0, qbase=qbase, r=r: e.matmul(
                             bank[:, c0:512],
                             k_bf[0:64, j * 128:(j + 1) * 128],
                             q_all[0:64, qbase + c0: qbase + 512],
                             start=True, stop=(r < 0)))
                    qk_pe = B.op("tensor",
                         lambda e, bank=bank, j=j, c0=c0, qbase=qbase, r=r: e.matmul(
                             bank[:, 512 + c0:1024],
                             k_bf[64:128, j * 128:(j + 1) * 128],
                             q_all[64:128, qbase + c0: qbase + 512],
                             start=True, stop=(r < 0)), "pe" if r < 0 else None)
                    if r >= 0:
                        B.op("tensor", lambda e, bank=bank, c0=c0, r=r: e.matmul(
                            bank[:, c0:512], ident[:, :],
                            masks[:, MASK_OFF[r]:MASK_OFF[r + 1]],
                            start=False, stop=True))
                        qk_pe = B.op("tensor",
                            lambda e, bank=bank, c0=c0, r=r: e.matmul(
                                bank[:, 512 + c0:1024], ident[:, :],
                                masks[:, MASK_OFF[r]:MASK_OFF[r + 1]],
                                start=False, stop=True), "pe")
                    if p == NPAIR - 1 and qc == 3 and j == 31:
                        wfcst["qk_last"] = qk_pe
                    B.wait("scalar", "pe", qk_pe)
                    B.wait("scalar", "pe", ptfree_mark.get(pt3, 0))
                    if r < 0:
                        em = B.op("scalar", lambda e, bank=bank, pt3=pt3:
                            e.activation(
                                p_t[:, pt3 * 1024:(pt3 + 1) * 1024],
                                bank[:, :], AF.Exp, bias=negc[:, 0:1]), "act")
                    else:
                        B.op("scalar", lambda e, bank=bank, pt3=pt3, c0=c0:
                            e.activation(
                                p_t[:, pt3 * 1024 + c0: pt3 * 1024 + 512],
                                bank[:, c0:512], AF.Exp, bias=negc[:, 0:1]))
                        em = B.op("scalar", lambda e, bank=bank, pt3=pt3, c0=c0:
                            e.activation(
                                p_t[:, pt3 * 1024 + 512 + c0:
                                    pt3 * 1024 + 1024],
                                bank[:, 512 + c0:1024], AF.Exp,
                                bias=negc[:, 0:1]), "act")
                    sfree_mark[par] = em
                    jglob += 1
                    if len(pending) >= 3:
                        drain_one()
                    pending.append((qc, j, pt3, c0, "act", em))
                while pending:
                    drain_one()

            attn_done_g = B.cnt["g"]
            ph2_act = B.cnt["act"]
            ph2_dve = B.cnt["dve"]
            B.flush(block, S)
        p12.close()

        wfc_bf = xnf
        wfc_ready = wfcst["ready"]
        with contextlib.ExitStack() as p3:
            wout_stage = p3.enter_context(
                nc.sbuf_tensor("wout_stage", [128, 8192], F32))
            wout_bf = p3.enter_context(nc.sbuf_tensor("wout_bf", [128, 8192], BF16))
            aev = [p3.enter_context(nc.sbuf_tensor(f"aev{i}", [128, 1024], F32))
                   for i in range(2)]
            pp = [p3.enter_context(nc.psum_tensor(f"pp{i}", [128, 1024], F32))
                  for i in range(2)]

            B.wait("sync", "pe", wfcst["qk_last"])
            B.op("sync", lambda e: e.dma_start(
                out=wout_stage[:, :].rearrange("q (dc f) -> q dc f", dc=8),
                in_=woutT_in.rearrange("(dc q) f -> q dc f", q=128)), "ld", 16)
            B.wait("gpsimd", "ld", B.cnt["ld"])
            wout_ready = B.op("gpsimd", lambda e: e.tensor_copy(
                out=wout_bf[:, :], in_=wout_stage[:, :]), "pool")

            B.wait("tensor", "pool", wout_ready)
            B.wait("tensor", "g", attn_done_g)
            B.wait("tensor", "act", ph2_act)
            B.wait("tensor", "dve", ph2_dve)
            ap_free = {}
            ap_store = {}
            for tt in range(NTO):
                par = tt % 2
                bank = pp[par]
                B.wait("tensor", "dve", ap_free.get(par, 0))
                op_pe = None
                for dc in range(8):
                    B.op("tensor", lambda e, bank=bank, dc=dc, tt=tt: e.matmul(
                        bank[:, 0:512],
                        out_norm[:, dc * TOWN + tt * 128: dc * TOWN + (tt + 1) * 128],
                        wout_bf[:, dc * 1024: dc * 1024 + 512],
                        start=(dc == 0), stop=(dc == 7)))
                    if dc < 7:
                        B.op("tensor", lambda e, bank=bank, dc=dc, tt=tt: e.matmul(
                            bank[:, 512:1024],
                            out_norm[:, dc * TOWN + tt * 128:
                                     dc * TOWN + (tt + 1) * 128],
                            wout_bf[:, dc * 1024 + 512:(dc + 1) * 1024],
                            start=(dc == 0), stop=False))
                    else:
                        op_pe = B.op("tensor",
                            lambda e, bank=bank, dc=dc, tt=tt: e.matmul(
                                bank[:, 512:1024],
                                out_norm[:, dc * TOWN + tt * 128:
                                         dc * TOWN + (tt + 1) * 128],
                                wout_bf[:, dc * 1024 + 512:(dc + 1) * 1024],
                                start=(dc == 0), stop=True), "pe")
                B.wait("vector", "pe", op_pe)
                B.wait("vector", "g", ap_store.get(par, 0))
                ap_free[par] = B.op("vector", lambda e, bank=bank, par=par:
                    e.tensor_copy(out=aev[par][:, :], in_=bank[:, :]), "dve")
                B.wait("sync", "dve", ap_free[par])
                ap_store[par] = B.op("sync", lambda e, tt=tt: e.dma_start(
                    out=aproj_hbm[tt * 128:(tt + 1) * 128, :],
                    in_=aev[tt % 2][:, :]), "g", 16)
            aproj_g = B.cnt["g"]
            ph3_dve = B.cnt["dve"]
            B.flush(block, S)
        p23.close()

        with contextlib.ExitStack() as p4:
            xn_own2 = p4.enter_context(nc.sbuf_tensor("xn_own2", [128, 8 * TOWN], BF16))
            b_bc = p4.enter_context(nc.sbuf_tensor("b_bc", [128, 1024], F32))
            brow_o = p4.enter_context(nc.sbuf_tensor("brow_o", [1, D], F32))
            with contextlib.ExitStack() as p4a:
                xstage4 = p4a.enter_context(nc.sbuf_tensor("xstage4", [128, 4096], F32))
                xdump4 = p4a.enter_context(nc.sbuf_tensor("xdump4", [128, 4 * 1024], F32))
                stat4 = p4a.enter_context(nc.sbuf_tensor("stat4", [128, 64], F32))
                xn_tm4 = p4a.enter_context(nc.sbuf_tensor("xn_tm4", [128, 4096], BF16))
                tr_ps4 = [p4a.enter_context(nc.psum_tensor(f"trq{i}", [128, 1024], BF16))
                          for i in range(2)]
                bb_ps = p4a.enter_context(nc.psum_tensor("bb_ps", [128, 1024], F32))

                B.wait("tensor", "dve", ph3_dve)
                B.wait("sync", "pool", wout_ready)
                B.op("sync", lambda e: e.dma_start(out=brow_o[:, :], in_=b_out[:, :]),
                     "ld", 16)
                bo_ld = B.cnt["ld"]
                emit_ln(x_own, NTO, xn_own2, TOWN, xstage4, stat4, xdump4, xn_tm4,
                        tr_ps4)
                xn2_dve = B.cnt["dve"]

                B.wait("tensor", "ld", bo_ld)
                B.op("tensor", lambda e: e.matmul(
                    bb_ps[:, 0:512], onesb[0:1, 0:128], brow_o[0:1, 0:512],
                    start=True, stop=True))
                bb_pe = B.op("tensor", lambda e: e.matmul(
                    bb_ps[:, 512:1024], onesb[0:1, 0:128], brow_o[0:1, 512:1024],
                    start=True, stop=True), "pe")
                B.wait("vector", "pe", bb_pe)
                bb_dve = B.op("vector", lambda e: e.tensor_copy(
                    out=b_bc[:, :], in_=bb_ps[:, :]), "dve")
                B.flush(block, S)

            h_store = p4.enter_context(nc.sbuf_tensor("h_store", [128, 32 * 512], BF16))
            ztmp = p4.enter_context(nc.sbuf_tensor("ztmp", [128, 2 * 512], BF16))
            wpstage = [p4.enter_context(nc.sbuf_tensor(f"wpst{i}", [128, 1024], F32))
                       for i in range(2)]
            wproj_bf = [p4.enter_context(nc.sbuf_tensor(f"wpbf{i}", [128, 1024], BF16))
                        for i in range(2)]
            ybuf = p4.enter_context(nc.sbuf_tensor("ybuf", [128, 2 * 1024], F32))
            smem = [p4.enter_context(nc.sbuf_tensor(f"smem{i}", [128, 1024], F32))
                    for i in range(2)]
            stmp = [p4.enter_context(nc.sbuf_tensor(f"stmp{i}", [128, 1024], F32))
                    for i in range(2)]
            sacc = [p4.enter_context(nc.sbuf_tensor(f"sacc{i}", [128, 1024], F32))
                    for i in range(2)]
            xo_t = [p4.enter_context(nc.sbuf_tensor(f"xo_t{i}", [128, 1024], F32))
                    for i in range(2)]
            ap_t = [p4.enter_context(nc.sbuf_tensor(f"ap_t{i}", [128, 1024], F32))
                    for i in range(2)]
            o_stage = stmp

            wp_consumed = {}
            wp_ready = {}
            out_store = {}
            fin_mark = {0: 0, 1: 0}
            y_evac_act = 0
            y_pe_prev = 0
            for tc in range(4):
                with contextlib.ExitStack() as pth:
                    hps = [pth.enter_context(
                        nc.psum_tensor(f"hps{tc}_{i}", [128, 512], F32))
                        for i in range(2)]
                    if tc == 0:
                        B.wait("tensor", "act", wfc_ready)
                        B.wait("tensor", "dve", max(xn2_dve, bb_dve))
                    else:
                        B.wait("tensor", "act", y_evac_act)
                        B.wait("scalar", "pe", y_pe_prev)
                    hz_free = {}
                    hsq_done = {}
                    for fc in range(32):
                        par = fc % 2
                        bank = hps[par]
                        B.wait("tensor", "act", hz_free.get(par, 0))
                        hmm = None
                        for dc in range(8):
                            if dc < 7:
                                B.op("tensor",
                                    lambda e, bank=bank, dc=dc, fc=fc, tc=tc: e.matmul(
                                        bank[:, :],
                                        wfc_bf[:, dc * FF + fc * 128:
                                               dc * FF + (fc + 1) * 128],
                                        xn_own2[:, dc * TOWN + tc * 512:
                                                dc * TOWN + (tc + 1) * 512],
                                        start=(dc == 0), stop=False))
                            else:
                                hmm = B.op("tensor",
                                    lambda e, bank=bank, dc=dc, fc=fc, tc=tc: e.matmul(
                                        bank[:, :],
                                        wfc_bf[:, dc * FF + fc * 128:
                                               dc * FF + (fc + 1) * 128],
                                        xn_own2[:, dc * TOWN + tc * 512:
                                                dc * TOWN + (tc + 1) * 512],
                                        start=False, stop=True), "pe")
                        B.wait("scalar", "pe", hmm)
                        hz = B.op("scalar",
                            lambda e, bank=bank, par=par: e.activation(
                                ztmp[:, par * 512:(par + 1) * 512], bank[:, :],
                                AF.Relu), "act", drain=False)
                        B.wait("scalar", "act", hz)
                        hz_free[par] = hsq_done[par] = B.op("scalar",
                            lambda e, fc=fc, par=par: e.activation(
                                h_store[:, fc * 512:(fc + 1) * 512],
                                ztmp[:, par * 512:(par + 1) * 512],
                                AF.Square), "act")
                    h_dve = B.cnt["act"]
                    B.flush(block, S)
                with contextlib.ExitStack() as pty:
                    yps = [pty.enter_context(
                        nc.psum_tensor(f"yps{tc}_{i}", [128, 1024], F32))
                        for i in range(4)]

                    for weng in ("tensor", "scalar", "vector", "gpsimd", "sync"):
                        for wsem in ("pe", "act", "dve", "pool"):
                            B.wait(weng, wsem, B.cnt[wsem])
                    B.wait("tensor", "act", h_dve)
                    for fc in range(32):
                        par = fc % 2
                        qeng, qsem = ("sync", "ld") if par == 0 else ("gpsimd", "g")
                        B.wait(qeng, "pe", wp_consumed.get(par, 0))
                        B.op(qeng, lambda e, fc=fc, par=par: e.dma_start(
                            out=wpstage[par][:, :],
                            in_=wprojT_in[fc * 128:(fc + 1) * 128, :]),
                            qsem, 16)
                        B.wait("scalar", qsem, B.cnt[qsem])
                        B.wait("scalar", "pe", wp_consumed.get(par, 0))
                        wp_ready[par] = B.op("scalar", lambda e, par=par: e.activation(
                            wproj_bf[par][:, :], wpstage[par][:, :], AF.Copy), "act")
                        B.wait("tensor", "act", wp_ready[par])
                        last = None
                        for ti in range(4):
                            B.op("tensor", lambda e, fc=fc, ti=ti, par=par: e.matmul(
                                yps[ti][:, 0:512],
                                h_store[:, fc * 512 + ti * 128:
                                        fc * 512 + (ti + 1) * 128],
                                wproj_bf[par][:, 0:512],
                                start=(fc == 0), stop=(fc == 31)))
                            if ti < 3:
                                B.op("tensor", lambda e, fc=fc, ti=ti, par=par:
                                    e.matmul(
                                        yps[ti][:, 512:1024],
                                        h_store[:, fc * 512 + ti * 128:
                                                fc * 512 + (ti + 1) * 128],
                                        wproj_bf[par][:, 512:1024],
                                        start=(fc == 0), stop=(fc == 31)))
                            else:
                                last = B.op("tensor", lambda e, fc=fc, ti=ti, par=par:
                                    e.matmul(
                                        yps[ti][:, 512:1024],
                                        h_store[:, fc * 512 + ti * 128:
                                                fc * 512 + (ti + 1) * 128],
                                        wproj_bf[par][:, 512:1024],
                                        start=(fc == 0), stop=(fc == 31)), "pe")
                        wp_consumed[par] = last
                    y_pe = B.cnt["pe"]
                    y_pe_prev = y_pe

                    for weng in ("tensor", "scalar", "vector", "gpsimd", "sync"):
                        for wsem in ("pe", "act", "dve", "pool"):
                            B.wait(weng, wsem, B.cnt[wsem])
                    for ti in range(4):
                        tt = tc * 4 + ti
                        par = ti % 2
                        eng = "vector"
                        esem = "dve"
                        B.wait("scalar", "pe", y_pe)
                        B.wait("scalar", esem, fin_mark[par])
                        yv = B.op("scalar", lambda e, ti=ti, par=par: e.activation(
                            ybuf[:, par * 1024:(par + 1) * 1024], yps[ti][:, :],
                            AF.Copy), "act")
                        y_evac_act = yv
                        B.wait("sync", esem, fin_mark[par])
                        B.op("sync", lambda e, tt=tt, par=par: e.dma_start(
                            out=xo_t[par][:, :],
                            in_=x_own[tt * 128:(tt + 1) * 128, :]), "ld", 16)
                        B.wait("gpsimd", "g", aproj_g)
                        B.wait("gpsimd", esem, fin_mark[par])
                        B.op("gpsimd", lambda e, tt=tt, par=par: e.dma_start(
                            out=ap_t[par][:, :],
                            in_=aproj_hbm[tt * 128:(tt + 1) * 128, :]), "g", 16)
                        fin_ld = B.cnt["ld"]
                        fin_g = B.cnt["g"]
                        B.wait(eng, "act", yv)
                        B.wait(eng, "dve", bb_dve)
                        yb = ybuf[:, par * 1024:(par + 1) * 1024]
                        sm, st_, sa = smem[par], stmp[par], sacc[par]

                        def sop(fn, eng=eng, esem=esem):
                            t = B.op(eng, fn, esem, drain=False)
                            B.wait(eng, esem, t)
                            return t
                        sop(lambda e, yb=yb, sa=sa: e.tensor_scalar(
                            sa[:, :], yb, 1.0, None, ALU.is_gt))
                        sop(lambda e, yb=yb, st_=st_: e.scalar_tensor_tensor(
                            st_[:, :], yb, 1.0, yb, ALU.is_gt, ALU.subtract))
                        sop(lambda e, yb=yb, sm=sm, st_=st_: e.scalar_tensor_tensor(
                            sm[:, :], yb, 0.9, st_[:, :], ALU.mult, ALU.subtract))
                        sop(lambda e, sm=sm, sa=sa: e.scalar_tensor_tensor(
                            sa[:, :], sm[:, :], 1.0, sa[:, :], ALU.is_gt, ALU.add))
                        for _ in range(3):
                            sop(lambda e, yb=yb, sm=sm, st_=st_:
                                e.scalar_tensor_tensor(
                                    st_[:, :], sm[:, :], 1.0, yb, ALU.is_gt,
                                    ALU.subtract))
                            sop(lambda e, sm=sm, st_=st_: e.scalar_tensor_tensor(
                                sm[:, :], sm[:, :], 0.9, st_[:, :], ALU.mult,
                                ALU.subtract))
                            sop(lambda e, sm=sm, sa=sa: e.scalar_tensor_tensor(
                                sa[:, :], sm[:, :], 1.0, sa[:, :], ALU.is_gt, ALU.add))
                        B.wait(eng, "ld", fin_ld)
                        B.wait(eng, "g", fin_g)
                        B.wait(eng, "st", out_store.get(par, 0))
                        sop(lambda e, sa=sa, par=par: e.scalar_tensor_tensor(
                            o_stage[par][:, :], sa[:, :], 0.2, ap_t[par][:, :],
                            ALU.mult, ALU.add))
                        sop(lambda e, par=par: e.tensor_tensor(
                            o_stage[par][:, :], o_stage[par][:, :], xo_t[par][:, :],
                            ALU.add))
                        fin_mark[par] = sop(lambda e, par=par: e.tensor_tensor(
                            o_stage[par][:, :], o_stage[par][:, :], b_bc[:, :],
                            ALU.add))
                        B.wait("sync", esem, fin_mark[par])
                        out_store[par] = B.op("sync", lambda e, tt=tt, par=par:
                            e.dma_start(
                                out=out_hbm[tt * 128:(tt + 1) * 128, :],
                                in_=o_stage[par][:, :]), "st", 16)

                    B.flush(block, S)

            B.wait("sync", "st", B.cnt["st"])
            B.flush(block, S)

    return nc


def _rope_tables(positions, scale):
    inv = 1.0 / (10000.0 ** (np.arange(0, 64, 2, dtype=np.float64) / 64.0))
    fr = positions.astype(np.float64)[None, :] * inv[:, None]
    c = np.cos(fr) * scale
    s = np.sin(fr) * scale
    bf = ml_dtypes.bfloat16
    cos_t = np.concatenate([c, c, c, c], axis=0).astype(np.float32)
    sin_t = np.concatenate([-s, s, -s, s], axis=0).astype(np.float32)
    return cos_t.astype(bf), sin_t.astype(bf)


def _build_masks(g):
    bf = ml_dtypes.bfloat16
    m = np.zeros((128, MASK_OFF[8]), dtype=np.float32)
    tri = (np.arange(128)[None, :] >= np.arange(128)[:, None]).astype(np.float32)
    for r in range(8):
        w = 512 - COL0[r]
        tile = np.zeros((128, w), dtype=np.float32)
        for col in range(w):
            gc = COL0[r] + col
            i = gc // 128
            if r < 2 * i + g:
                tile[:, col] = 1.0
            elif r == 2 * i + g:
                tile[:, col] = tri[:, gc % 128]
        m[:, MASK_OFF[r]:MASK_OFF[r + 1]] = tile
    return np.ascontiguousarray(((m - 1.0) * 100.0).astype(bf))


_NC_CACHE = []


def kernel(L0_attn_x, L0_attn_w_qkv, L0_attn_b_qkv, L0_attn_w_out, L0_attn_b_out,
           L0_ffn_W_fc, L0_ffn_W_proj):
    x = np.ascontiguousarray(np.asarray(L0_attn_x, dtype=np.float32))
    w_qkv = np.asarray(L0_attn_w_qkv, dtype=np.float32)
    b_qkv = np.asarray(L0_attn_b_qkv, dtype=np.float32)
    w_out = np.asarray(L0_attn_w_out, dtype=np.float32)
    b_out = np.asarray(L0_attn_b_out, dtype=np.float32)
    W_fc = np.asarray(L0_ffn_W_fc, dtype=np.float32)
    W_proj = np.asarray(L0_ffn_W_proj, dtype=np.float32)
    bf = ml_dtypes.bfloat16

    perm64 = np.concatenate([np.arange(0, 64, 2), np.arange(1, 64, 2)])
    qk_rows = np.concatenate([64 * h + perm64 for h in range(16)])
    wqT = np.ascontiguousarray(w_qkv[qk_rows, :].T)
    wkT = np.ascontiguousarray(w_qkv[1024 + qk_rows, :].T)
    bq_row = np.ascontiguousarray(b_qkv[qk_rows][None, :])
    bk_row = np.ascontiguousarray(b_qkv[1024 + qk_rows][None, :])
    wvT_full = w_qkv[2048:3072, :].T
    wvT = np.zeros((D, NPAIR * 130), dtype=np.float32)
    bv_row = np.zeros((1, NPAIR * 130), dtype=np.float32)
    bv_full = b_qkv[2048:3072]
    for p in range(NPAIR):
        wvT[:, p * 130: p * 130 + 64] = wvT_full[:, p * 128: p * 128 + 64]
        wvT[:, p * 130 + 65: p * 130 + 129] = wvT_full[:, p * 128 + 64:(p + 1) * 128]
        bv_row[0, p * 130: p * 130 + 64] = bv_full[p * 128: p * 128 + 64]
        bv_row[0, p * 130 + 64] = 1.0
        bv_row[0, p * 130 + 65: p * 130 + 129] = bv_full[p * 128 + 64:(p + 1) * 128]
        bv_row[0, p * 130 + 129] = 1.0
    woutT = np.ascontiguousarray(w_out.T)
    wfcT = np.ascontiguousarray(W_fc.T)
    wprojT = np.ascontiguousarray(W_proj.T)

    ident = np.eye(128, dtype=np.float32).astype(bf)
    sigma = np.concatenate([np.arange(32, 64), np.arange(0, 32),
                            np.arange(96, 128), np.arange(64, 96)])
    permM = np.zeros((128, 128), dtype=np.float32)
    permM[sigma, np.arange(128)] = 1.0
    permM = permM.astype(bf)

    rk_cos, rk_sin = _rope_tables(np.arange(T), 1.0)

    if not _NC_CACHE:
        _NC_CACHE.append(build_program())
    nc = _NC_CACHE[0]

    in_maps = []
    own_idx_all = []
    for c in range(8):
        b, g = c // 2, c % 2
        own_idx = np.concatenate(
            [np.arange(128 * (2 * j + g), 128 * (2 * j + g) + 128)
             for j in range(NTO)])
        own_idx_all.append(own_idx)
        xb = x[b]
        q_cos, q_sin = _rope_tables(own_idx, 0.125)
        in_maps.append({
            "x_seq": xb,
            "x_own": np.ascontiguousarray(xb[own_idx]),
            "wqT": wqT, "wkT": wkT, "wvT": wvT,
            "bq": bq_row, "bk": bk_row, "bv": bv_row,
            "woutT": woutT, "b_out": np.ascontiguousarray(b_out[None, :]),
            "wfcT": wfcT, "wprojT": wprojT,
            "ropeq_cos": q_cos, "ropeq_sin": q_sin,
            "ropek_cos": rk_cos, "ropek_sin": rk_sin,
            "masks": _build_masks(g),
            "ident": ident, "permT": permM,
        })

    res = run_bass_kernel_spmd(nc, in_maps, core_ids=list(range(8)))

    out = np.empty((4, T, D), dtype=np.float32)
    for c in range(8):
        b = c // 2
        out[b, own_idx_all[c], :] = np.asarray(res.results[c]["out"],
                                               dtype=np.float32)
    return out


# version 36
# speedup vs baseline: 1.2006x; 1.0298x over previous
import contextlib
import numpy as np
import ml_dtypes

import concourse.bass as bass
import concourse.mybir as mybir
from concourse.bass_utils import run_bass_kernel_spmd

F32 = mybir.dt.float32
BF16 = mybir.dt.bfloat16
AF = mybir.ActivationFunctionType
ALU = mybir.AluOpType
AX = mybir.AxisListType

D = 1024
T = 4096
TOWN = 2048
NPAIR = 8
FF = 4096
NT = T // 128
NTO = TOWN // 128
EPS = 1e-5

ENGINES = ("sync", "vector", "scalar", "tensor", "gpsimd")
COL0 = [0, 0, 128, 128, 256, 256, 384, 384]
MASK_OFF = np.cumsum([0] + [512 - c for c in COL0]).tolist()


class Builder:
    def __init__(self):
        self.ops = {e: [] for e in ENGINES}
        self.cnt = {s: 0 for s in ("ld", "g", "st", "pe", "act", "dve", "pool")}

    def inc(self, sem, n):
        self.cnt[sem] += n
        return self.cnt[sem]

    def wait(self, eng, sem, val):
        if val and val > 0:
            self.ops[eng].append(lambda e, S, s=sem, v=val: e.wait_ge(S[s], v))

    def op(self, eng, fn, inc_sem=None, inc_n=1, drain=None):
        if inc_sem is None:
            self.ops[eng].append(lambda e, S, f=fn: f(e))
            return None
        if drain is None:
            drain = inc_sem == "pool"
        tgt = self.inc(inc_sem, inc_n)
        if drain:
            self.ops[eng].append(lambda e, S, f=fn: f(e))
            self.ops[eng].append(
                lambda e, S, s=inc_sem, n=inc_n: e.drain().then_inc(S[s], n))
        else:
            self.ops[eng].append(
                lambda e, S, f=fn, s=inc_sem, n=inc_n: f(e).then_inc(S[s], n))
        return tgt

    def flush(self, block, S):
        for eng in ENGINES:
            ops = self.ops[eng]
            if not ops:
                continue
            self.ops[eng] = []

            def body(e, ops=ops):
                for fn in ops:
                    fn(e, S)
            getattr(block, eng)(body)


def build_program():
    nc = bass.Bass("TRN2", target_bir_lowering=False)

    def din(name, shape, dt=F32):
        return nc.dram_tensor(name, shape, dt, kind="ExternalInput").ap()

    x_seq = din("x_seq", [T, D])
    x_own = din("x_own", [TOWN, D])
    wqT = din("wqT", [D, D])
    wkT = din("wkT", [D, D])
    wvT = din("wvT", [D, NPAIR * 130])
    bq = din("bq", [1, D])
    bk = din("bk", [1, D])
    bv = din("bv", [1, NPAIR * 130])
    woutT_in = din("woutT", [D, D])
    b_out = din("b_out", [1, D])
    wfcT_in = din("wfcT", [D, FF])
    wprojT_in = din("wprojT", [FF, D])
    rq_cos = din("ropeq_cos", [128, TOWN], BF16)
    rq_sin = din("ropeq_sin", [128, TOWN], BF16)
    rk_cos = din("ropek_cos", [128, T], BF16)
    rk_sin = din("ropek_sin", [128, T], BF16)
    masks_in = din("masks", [128, MASK_OFF[8]], BF16)
    ident_in = din("ident", [128, 128], BF16)
    perm_in = din("permT", [128, 128], BF16)

    out_hbm = nc.dram_tensor("out", [TOWN, D], F32, kind="ExternalOutput").ap()
    aproj_hbm = nc.dram_tensor("aproj_scratch", [TOWN, D], F32).ap()

    B = Builder()

    with contextlib.ExitStack() as top:
        S = {}
        for s in ("ld", "g", "st", "pe", "act", "dve", "pool"):
            S[s] = top.enter_context(nc.semaphore(s))
        block = top.enter_context(nc.Block())

        onesb = top.enter_context(nc.sbuf_tensor("onesb", [65, 128], F32, side="right"))
        ones_bf = top.enter_context(nc.sbuf_tensor("ones_bf", [65, 512], BF16, side="right"))
        ident = top.enter_context(nc.sbuf_tensor("ident_sb", [128, 128], BF16, side="right"))
        permT = top.enter_context(nc.sbuf_tensor("permT_sb", [128, 128], BF16, side="right"))
        epscol = top.enter_context(nc.sbuf_tensor("epscol", [128, 1], F32, side="right"))

        B.op("gpsimd", lambda e: e.memset(onesb[:, :], 1.0), "pool")
        B.op("gpsimd", lambda e: e.memset(ones_bf[:, :], 1.0), "pool")
        B.op("gpsimd", lambda e: e.memset(epscol[:, :], EPS), "pool")
        for dst, src in ((ident, ident_in), (permT, perm_in)):
            B.op("sync", lambda e, d=dst, s=src: e.dma_start(out=d[:, :], in_=s[:, :]),
                 "ld", 16)
        consts_ld = B.cnt["ld"]
        consts_pool = B.cnt["pool"]
        B.wait("scalar", "pool", consts_pool)

        xnf = top.enter_context(nc.sbuf_tensor("xnf", [128, 8 * T], BF16))
        p12 = contextlib.ExitStack()
        q_all = p12.enter_context(nc.sbuf_tensor("q_all", [128, NPAIR * TOWN], BF16))

        def emit_ln(src_hbm, ntiles, dst_fm, dst_cols, xstage, stat, xdump,
                    xn_tm, tr_ps, hook=None):
            nslots = 4
            dve0 = B.cnt["dve"]
            slot_free = {}
            tr_free = {}
            red2_prev = 0
            for t0 in range(0, ntiles, 4):
                nb = min(4, ntiles - t0)
                sls = [(t0 + i) % nslots for i in range(nb)]
                for i in range(nb):
                    t = t0 + i
                    B.wait("sync", "dve", slot_free.get(sls[i], dve0))
                    B.op("sync", lambda e, slot=sls[i], t=t: e.dma_start(
                        out=xstage[:, slot * 1024:(slot + 1) * 1024],
                        in_=src_hbm[t * 128:(t + 1) * 128, :]), "ld", 16)
                ld_done = B.cnt["ld"]
                B.wait("vector", "ld", ld_done)
                for s in sls:
                    B.op("vector", lambda e, s=s: e.reduce_sum(
                        stat[:, s * 8:s * 8 + 1],
                        xstage[:, s * 1024:(s + 1) * 1024], AX.X))
                B.wait("scalar", "ld", ld_done)
                B.wait("scalar", "dve", red2_prev)
                sq = None
                for s in sls:
                    sq = B.op("scalar", lambda e, s=s: e.activation(
                        xdump[:, s * 1024:(s + 1) * 1024],
                        xstage[:, s * 1024:(s + 1) * 1024], AF.Square), "act")
                B.wait("vector", "act", sq)
                red2_prev = 0
                for s in sls:
                    red2_prev = B.op("vector", lambda e, s=s: e.reduce_sum(
                        stat[:, s * 8 + 1:s * 8 + 2],
                        xdump[:, s * 1024:(s + 1) * 1024], AX.X), "dve")
                for s in sls:
                    B.op("vector", lambda e, s0=s * 8: e.tensor_scalar(
                        stat[:, s0:s0 + 1], stat[:, s0:s0 + 1], -1.0 / D, None,
                        ALU.mult))
                for s in sls:
                    B.op("vector", lambda e, s0=s * 8: e.tensor_scalar(
                        stat[:, s0 + 2:s0 + 3], stat[:, s0 + 1:s0 + 2], 1.0 / D, None,
                        ALU.mult))
                for s in sls:
                    B.op("vector", lambda e, s0=s * 8: e.tensor_tensor(
                        stat[:, s0 + 3:s0 + 4], stat[:, s0:s0 + 1], stat[:, s0:s0 + 1],
                        ALU.mult))
                last = None
                for s in sls:
                    last = B.op("vector", lambda e, s0=s * 8: e.tensor_tensor(
                        stat[:, s0 + 4:s0 + 5], stat[:, s0 + 2:s0 + 3],
                        stat[:, s0 + 3:s0 + 4], ALU.subtract), "dve")
                B.wait("scalar", "dve", last)
                la = None
                for s in sls:
                    la = B.op("scalar", lambda e, s0=s * 8: e.activation(
                        stat[:, s0 + 5:s0 + 6], stat[:, s0 + 4:s0 + 5], AF.Sqrt,
                        bias=epscol[:, 0:1]), "act")
                B.wait("vector", "act", la)
                for s in sls:
                    B.op("vector", lambda e, s0=s * 8: e.reciprocal(
                        stat[:, s0 + 6:s0 + 7], stat[:, s0 + 5:s0 + 6]))
                for i in range(nb):
                    s = sls[i]
                    slot_free[s] = B.op("vector", lambda e, slot=s, s0=s * 8:
                        e.tensor_scalar(
                            xn_tm[:, slot * 1024:(slot + 1) * 1024],
                            xstage[:, slot * 1024:(slot + 1) * 1024],
                            stat[:, s0:s0 + 1], stat[:, s0 + 6:s0 + 7],
                            ALU.add, ALU.mult), "dve")
                if hook is not None:
                    hook(t0 // 4)
                for i in range(nb):
                    t = t0 + i
                    slot = sls[i]
                    B.wait("tensor", "dve", slot_free[slot])
                    for c in range(8):
                        k = t * 8 + c
                        pb = tr_ps[k % 2]
                        B.wait("tensor", "dve", tr_free.get(k % 2, dve0))
                        B.op("tensor", lambda e, pb=pb, slot=slot, c=c: e.transpose(
                            pb[:, 0:128],
                            xn_tm[:, slot * 1024 + c * 128: slot * 1024 + (c + 1) * 128],
                            ident[:, :]), "pe")
                        B.wait("vector", "pe", B.cnt["pe"])
                        tr_free[k % 2] = B.op("vector",
                            lambda e, pb=pb, c=c, t=t: e.tensor_copy(
                                out=dst_fm[:, c * dst_cols + t * 128:
                                           c * dst_cols + t * 128 + 128],
                                in_=pb[:, 0:128]), "dve")

        def emit_rope(pre_ap, perm_bank, cosap, sinap, dstap, t1, t2, pre_act):
            B.wait("tensor", "act", pre_act)
            pe_perm = B.op("tensor", lambda e: e.matmul(
                perm_bank, permT[:, :], pre_ap, start=True, stop=True), "pe")
            B.wait("vector", "act", pre_act)
            B.op("vector", lambda e: e.tensor_tensor(t1[:, :], pre_ap, cosap, ALU.mult))
            B.wait("vector", "pe", pe_perm)
            t2done = B.op("vector", lambda e: e.tensor_tensor(t2[:, :], perm_bank, sinap,
                                                              ALU.mult), "dve")
            B.wait("vector", "dve", t2done)
            return B.op("vector", lambda e: e.tensor_tensor(dstap, t1[:, :], t2[:, :],
                                                            ALU.add), "dve")

        with contextlib.ExitStack() as p1:
            xn_own = p1.enter_context(nc.sbuf_tensor("xn_own", [128, 8 * TOWN], BF16))
            xstage = p1.enter_context(nc.sbuf_tensor("xstage", [128, 4 * 1024], F32))
            xdump = p1.enter_context(nc.sbuf_tensor("xdump", [128, 4 * 1024], F32))
            stat = p1.enter_context(nc.sbuf_tensor("stat", [128, 64], F32))
            xn_tm = p1.enter_context(nc.sbuf_tensor("xn_tm", [128, 4 * 1024], BF16))
            tr_ps = [p1.enter_context(nc.psum_tensor(f"trp{i}", [128, 1024], BF16))
                     for i in range(2)]
            brow_q = p1.enter_context(nc.sbuf_tensor("brow_q", [1, D], F32))
            brow_qb = p1.enter_context(nc.sbuf_tensor("brow_qb", [1, D], BF16))
            rqc = p1.enter_context(nc.sbuf_tensor("rqc", [128, TOWN], BF16))
            rqs = p1.enter_context(nc.sbuf_tensor("rqs", [128, TOWN], BF16))
            wstage = p1.enter_context(nc.sbuf_tensor("wstage", [128, 1024], F32))
            wq_bf = p1.enter_context(nc.sbuf_tensor("wq_bf", [128, 1024], BF16))
            qpre = p1.enter_context(nc.sbuf_tensor("qpre", [128, 2 * 512], BF16))
            t1b = p1.enter_context(nc.sbuf_tensor("t1b", [128, 512], BF16))
            t2b = p1.enter_context(nc.sbuf_tensor("t2b", [128, 512], BF16))
            pr_ps = [p1.enter_context(nc.psum_tensor(f"prp{i}", [128, 512], F32))
                     for i in range(2)]
            pm_ps = [p1.enter_context(nc.psum_tensor(f"pmp{i}", [128, 512], F32))
                     for i in range(2)]

            B.op("sync", lambda e: e.dma_start(out=rqc[:, :], in_=rq_cos[:, :]), "ld", 16)
            B.op("sync", lambda e: e.dma_start(out=rqs[:, :], in_=rq_sin[:, :]), "ld", 16)
            B.op("sync", lambda e: e.dma_start(out=brow_q[:, :], in_=bq[:, :]), "ld", 16)
            rope_tab_ld = B.cnt["ld"]
            B.wait("vector", "ld", rope_tab_ld)
            bqb_dve = B.op("vector", lambda e: e.tensor_copy(out=brow_qb[:, :],
                                                             in_=brow_q[:, :]), "dve")
            B.wait("tensor", "dve", bqb_dve)
            B.wait("tensor", "ld", max(consts_ld, rope_tab_ld))
            B.wait("tensor", "pool", consts_pool)
            emit_ln(x_own, NTO, xn_own, TOWN, xstage, stat, xdump, xn_tm, tr_ps)
            ln_done_dve = B.cnt["dve"]

            B.wait("tensor", "dve", ln_done_dve)
            B.wait("vector", "ld", rope_tab_ld)
            proj_free = {}
            rope_done = {}

            def emit_qpair(p):
                B.wait("sync", "pool", B.cnt["pool"])
                B.op("sync", lambda e, p=p: e.dma_start(
                    out=wstage[:, :].rearrange("q (dc f) -> q dc f", dc=8),
                    in_=wqT[:, p * 128:(p + 1) * 128]
                    .rearrange("(dc q) f -> q dc f", q=128)), "ld", 16)
                B.wait("gpsimd", "ld", B.cnt["ld"])
                B.wait("gpsimd", "pe", B.cnt["pe"])
                wq_ready = B.op("gpsimd", lambda e: e.tensor_copy(
                    out=wq_bf[:, :], in_=wstage[:, :]), "pool")
                B.wait("tensor", "pool", wq_ready)
                for ch in range(4):
                    k = p * 4 + ch
                    par = k % 2
                    bank = pr_ps[par]
                    B.wait("tensor", "act", proj_free.get(par, 0))
                    for dc in range(8):
                        B.op("tensor", lambda e, bank=bank, dc=dc, ch=ch: e.matmul(
                            bank[:, :],
                            wq_bf[:, dc * 128:(dc + 1) * 128],
                            xn_own[:, dc * TOWN + ch * 512: dc * TOWN + (ch + 1) * 512],
                            start=(dc == 0), stop=False))
                    pe_proj = B.op("tensor", lambda e, bank=bank, p=p: e.matmul(
                        bank[:, :], brow_qb[0:1, p * 128:(p + 1) * 128],
                        ones_bf[0:1, 0:512], start=False, stop=True), "pe")
                    B.wait("scalar", "pe", pe_proj)
                    B.wait("scalar", "dve", rope_done.get(par, 0))
                    proj_free[par] = B.op("scalar", lambda e, bank=bank, par=par:
                        e.activation(qpre[:, par * 512:(par + 1) * 512], bank[:, :],
                                     AF.Copy), "act")
                    B.wait("tensor", "dve", rope_done.get(par, 0))
                    rope_done[par] = emit_rope(
                        qpre[:, par * 512:(par + 1) * 512],
                        pm_ps[par][:, :],
                        rqc[:, ch * 512:(ch + 1) * 512],
                        rqs[:, ch * 512:(ch + 1) * 512],
                        q_all[:, p * TOWN + ch * 512: p * TOWN + (ch + 1) * 512],
                        t1b, t2b, proj_free[par])

            emit_ln(x_seq, NT, xnf, T, xstage, stat, xdump, xn_tm, tr_ps,
                    hook=emit_qpair)
            q_done_dve = B.cnt["dve"]
            B.flush(block, S)


        p23 = contextlib.ExitStack()
        out_norm = p23.enter_context(
            nc.sbuf_tensor("out_norm", [128, NPAIR * TOWN], BF16, side="right"))
        with contextlib.ExitStack() as p2:
            brow_k = p2.enter_context(nc.sbuf_tensor("brow_k", [1, D], F32))
            brow_kb = p2.enter_context(nc.sbuf_tensor("brow_kb", [1, D], BF16))
            brow_vb = p2.enter_context(nc.sbuf_tensor("brow_vb", [1, NPAIR * 130], BF16))
            rkc = p2.enter_context(nc.sbuf_tensor("rkc", [128, T], BF16))
            rks = p2.enter_context(nc.sbuf_tensor("rks", [128, T], BF16))
            masks = p2.enter_context(
                nc.sbuf_tensor("masks_sb", [128, MASK_OFF[8]], BF16))
            k_bf = p2.enter_context(nc.sbuf_tensor("k_bf", [128, T], BF16))
            v_store = p2.enter_context(
                nc.sbuf_tensor("v_store", [128, NT * 130], BF16))
            wstage2 = p2.enter_context(nc.sbuf_tensor("wstage2", [128, 1040], F32))
            wk_bf = p2.enter_context(nc.sbuf_tensor("wk_bf", [128, 1024], BF16))
            wv_bf = p2.enter_context(nc.sbuf_tensor("wv_bf", [128, 1040], BF16))
            kpre = p2.enter_context(nc.sbuf_tensor("kpre", [128, 2 * 512], BF16))
            t1k = p2.enter_context(nc.sbuf_tensor("t1k", [128, 512], BF16))
            t2k = p2.enter_context(nc.sbuf_tensor("t2k", [128, 512], BF16))
            p_t = p2.enter_context(nc.sbuf_tensor("p_t", [128, 4 * 1024], BF16))
            rowbuf = p2.enter_context(nc.sbuf_tensor("rowbuf", [65, 1024], F32))
            recipbf = p2.enter_context(nc.sbuf_tensor("recipbf", [65, 1024], BF16))
            evacA = p2.enter_context(nc.sbuf_tensor("evacA", [64, 512], BF16))
            evacB = p2.enter_context(nc.sbuf_tensor("evacB", [64, 512], BF16))
            normA = p2.enter_context(nc.sbuf_tensor("normA", [64, 512], BF16))
            normB = p2.enter_context(nc.sbuf_tensor("normB", [64, 512], BF16))
            cstat = p2.enter_context(nc.sbuf_tensor("cstat", [128, 4], F32))
            negc = p2.enter_context(nc.sbuf_tensor("negc", [128, 1], F32))
            s_ps = [p2.enter_context(nc.psum_tensor(f"sps{i}", [128, 1024], F32))
                    for i in range(2)]
            o_ps = p2.enter_context(nc.psum_tensor("ops", [128, 1024], F32))
            bc_ps = [p2.enter_context(nc.psum_tensor(f"bcp{i}", [64, 512], F32))
                     for i in range(2)]

            B.wait("sync", "dve", q_done_dve)
            B.op("sync", lambda e: e.dma_start(out=brow_k[:, :], in_=bk[:, :]), "ld", 16)
            B.op("sync", lambda e: e.dma_start(out=wstage2[0:1, 0:NPAIR * 130],
                                               in_=bv[:, :]), "ld", 16)
            B.op("sync", lambda e: e.dma_start(out=rkc[:, :], in_=rk_cos[:, :]),
                 "ld", 16)
            B.op("sync", lambda e: e.dma_start(out=rks[:, :], in_=rk_sin[:, :]),
                 "ld", 16)
            B.op("sync", lambda e: e.dma_start(out=masks[:, :], in_=masks_in[:, :]),
                 "ld", 16)
            ph2_ld = B.cnt["ld"]
            B.wait("vector", "ld", ph2_ld)
            B.wait("tensor", "ld", ph2_ld)
            B.op("vector", lambda e: e.tensor_copy(out=brow_kb[:, :], in_=brow_k[:, :]))
            bkb_dve = B.op("vector", lambda e: e.tensor_copy(
                out=brow_vb[:, :], in_=wstage2[0:1, 0:NPAIR * 130]), "dve")
            B.wait("tensor", "dve", bkb_dve)
            B.wait("sync", "dve", bkb_dve)

            jglob = 0
            sfree_mark = {}
            ptfree_mark = {}
            evac_mark = 0
            place_prev = 0
            v_done_act = 0

            wfcst = {"ready": 0, "next": 0}

            def emit_wfc_group(n8):
                for k in range(8):
                    n = n8 * 8 + k
                    piece, dc = n // 8, n % 8
                    B.wait("sync", "act", wfcst["ready"])
                    B.op("sync", lambda e, piece=piece, dc=dc: e.dma_start(
                        out=wstage2[:, 0:1024],
                        in_=wfcT_in[dc * 128:(dc + 1) * 128,
                                    piece * 1024:(piece + 1) * 1024]), "ld", 16)
                    B.wait("scalar", "ld", B.cnt["ld"])
                    wfcst["ready"] = B.op("scalar",
                        lambda e, piece=piece, dc=dc: e.activation(
                            xnf[:, dc * FF + piece * 1024:
                                dc * FF + (piece + 1) * 1024],
                            wstage2[:, 0:1024], AF.Copy), "act")

            def emit_norm(p, qc):
                nonlocal evac_mark, place_prev
                last_pv = B.cnt["pe"]
                B.wait("scalar", "pe", last_pv)
                B.op("scalar", lambda e: e.activation(
                    rowbuf[64:65, 0:512], o_ps[64:65, 0:512], AF.Ln))
                B.op("scalar", lambda e: e.activation(
                    rowbuf[64:65, 512:1024], o_ps[64:65, 512:1024], AF.Ln))
                B.op("scalar", lambda e: e.activation(
                    recipbf[64:65, 0:512], rowbuf[64:65, 0:512], AF.Exp,
                    scale=-1.0))
                rows_act = B.op("scalar", lambda e: e.activation(
                    recipbf[64:65, 512:1024], rowbuf[64:65, 512:1024], AF.Exp,
                    scale=-1.0), "act")
                B.wait("tensor", "act", rows_act)
                B.wait("tensor", "dve", evac_mark)
                B.op("tensor", lambda e: e.matmul(
                    bc_ps[0][0:64, :], ones_bf[64:65, 0:64],
                    recipbf[64:65, 0:512], start=True, stop=True))
                bc_pe = B.op("tensor", lambda e: e.matmul(
                    bc_ps[1][0:64, :], ones_bf[64:65, 0:64],
                    recipbf[64:65, 512:1024], start=True, stop=True), "pe")
                B.wait("vector", "pe", bc_pe)
                B.wait("vector", "g", place_prev)
                B.op("vector", lambda e: e.tensor_copy(
                    out=evacA[:, :], in_=o_ps[0:64, 0:512]))
                B.op("vector", lambda e: e.tensor_copy(
                    out=evacB[:, :], in_=o_ps[0:64, 512:1024]))
                B.op("vector", lambda e: e.tensor_tensor(
                    normA[:, :], evacA[:, :], bc_ps[0][0:64, :], ALU.mult))
                evac_mark = B.op("vector", lambda e: e.tensor_tensor(
                    normB[:, :], evacB[:, :], bc_ps[1][0:64, :], ALU.mult),
                    "dve")
                B.wait("gpsimd", "dve", evac_mark)
                B.op("gpsimd", lambda e, p=p, qc=qc: e.dma_start(
                    out=out_norm[0:64, p * TOWN + qc * 512:
                                 p * TOWN + (qc + 1) * 512],
                    in_=normA[:, :]), "g", 16)
                place_prev = B.op("gpsimd", lambda e, p=p, qc=qc: e.dma_start(
                    out=out_norm[64:128, p * TOWN + qc * 512:
                                 p * TOWN + (qc + 1) * 512],
                    in_=normB[:, :]), "g", 16)
                if p == NPAIR - 1:
                    if qc == 0:
                        B.wait("sync", "pool", B.cnt["pool"])
                    emit_wfc_group(qc)

            for p in range(NPAIR):
                B.wait("sync", "pool", B.cnt["pool"])
                B.op("sync", lambda e, p=p: e.dma_start(
                    out=wstage2[:, 0:1024].rearrange("q (dc f) -> q dc f", dc=8),
                    in_=wkT[:, p * 128:(p + 1) * 128]
                    .rearrange("(dc q) f -> q dc f", q=128)), "ld", 16)
                B.wait("gpsimd", "ld", B.cnt["ld"])
                B.wait("gpsimd", "pe", B.cnt["pe"])
                wk_ready = B.op("gpsimd", lambda e: e.tensor_copy(
                    out=wk_bf[:, :], in_=wstage2[:, 0:1024]), "pool")
                B.wait("sync", "pool", wk_ready)
                B.op("sync", lambda e, p=p: e.dma_start(
                    out=wstage2[:, 0:1040].rearrange("q (dc f) -> q dc f", dc=8),
                    in_=wvT[:, p * 130:(p + 1) * 130]
                    .rearrange("(dc q) f -> q dc f", q=128)), "ld", 16)
                B.wait("gpsimd", "ld", B.cnt["ld"])
                B.wait("gpsimd", "pe", B.cnt["pe"])
                wv_ready = B.op("gpsimd", lambda e: e.tensor_copy(
                    out=wv_bf[:, :], in_=wstage2[:, 0:1040]), "pool")

                prev_attn_pe = B.cnt["pe"]
                kproj_free = dict(sfree_mark)
                kpre_free = {}
                kperm_free = {0: evac_mark, 1: evac_mark}
                B.wait("tensor", "pool", wk_ready)
                B.wait("vector", "pe", prev_attn_pe)
                for ch in range(8):
                    par = ch % 2
                    bank = s_ps[par]
                    B.wait("tensor", "act", kproj_free.get(par, 0))
                    for dc in range(8):
                        B.op("tensor", lambda e, bank=bank, dc=dc, ch=ch: e.matmul(
                            bank[:, 0:512],
                            wk_bf[:, dc * 128:(dc + 1) * 128],
                            xnf[:, dc * T + ch * 512: dc * T + (ch + 1) * 512],
                            start=(dc == 0), stop=False))
                    pe_kproj = B.op("tensor", lambda e, bank=bank, p=p: e.matmul(
                        bank[:, 0:512], brow_kb[0:1, p * 128:(p + 1) * 128],
                        ones_bf[0:1, 0:512], start=False, stop=True), "pe")
                    B.wait("scalar", "pe", pe_kproj)
                    B.wait("scalar", "dve", kpre_free.get(par, 0))
                    kproj_free[par] = B.op("scalar",
                        lambda e, bank=bank, par=par: e.activation(
                            kpre[:, par * 512:(par + 1) * 512], bank[:, 0:512],
                            AF.Copy), "act")
                    B.wait("tensor", "dve", kperm_free.get(par, 0))
                    kpre_free[par] = kperm_free[par] = emit_rope(
                        kpre[:, par * 512:(par + 1) * 512],
                        o_ps[:, par * 512:(par + 1) * 512],
                        rkc[:, ch * 512:(ch + 1) * 512],
                        rks[:, ch * 512:(ch + 1) * 512],
                        k_bf[:, ch * 512:(ch + 1) * 512],
                        t1k, t2k, kproj_free[par])
                kv_rope_dve = B.cnt["dve"]

                vproj_free = dict(kproj_free)
                B.wait("tensor", "pool", wv_ready)
                B.wait("scalar", "pe", prev_attn_pe)
                for tt in range(NT):
                    par = tt % 2
                    bank = s_ps[par]
                    B.wait("tensor", "act", vproj_free.get(par, 0))
                    for dc in range(8):
                        B.op("tensor", lambda e, bank=bank, dc=dc, tt=tt: e.matmul(
                            bank[:, 0:130],
                            xnf[:, dc * T + tt * 128: dc * T + (tt + 1) * 128],
                            wv_bf[:, dc * 130:(dc + 1) * 130],
                            start=(dc == 0), stop=False))
                    pe_v = B.op("tensor", lambda e, bank=bank, p=p: e.matmul(
                        bank[:, 0:130], ones_bf[0:1, 0:128],
                        brow_vb[0:1, p * 130:(p + 1) * 130], start=False,
                        stop=True), "pe")
                    B.wait("scalar", "pe", pe_v)
                    vproj_free[par] = B.op("scalar",
                        lambda e, bank=bank, tt=tt: e.activation(
                            v_store[:, tt * 130:(tt + 1) * 130], bank[:, 0:130],
                            AF.Copy), "act")
                v_done_act = B.cnt["act"]

                B.wait("vector", "dve", kv_rope_dve)
                B.op("vector", lambda e, p=p: e.reduce_sum(
                    cstat[:, 0:1], q_all[:, p * TOWN:(p + 1) * TOWN:16], AX.X))
                cs_dve = B.op("vector", lambda e: e.reduce_sum(
                    cstat[:, 1:2], k_bf[:, ::32], AX.X), "dve")
                B.wait("tensor", "dve", max(cs_dve, evac_mark))
                cpe = B.op("tensor", lambda e: e.matmul(
                    bc_ps[0][0:1, 0:1], cstat[:, 1:2], cstat[:, 0:1],
                    start=True, stop=True), "pe")
                B.wait("vector", "pe", cpe)
                cdve = B.op("vector", lambda e: e.tensor_copy(
                    out=cstat[0:1, 3:4], in_=bc_ps[0][0:1, 0:1]), "dve")
                B.wait("tensor", "dve", cdve)
                B.wait("tensor", "act", v_done_act)
                cpe2 = B.op("tensor", lambda e: e.matmul(
                    s_ps[0][:, 0:1], onesb[0:1, 0:128], cstat[0:1, 3:4],
                    start=True, stop=True), "pe")
                B.wait("vector", "pe", cpe2)
                negc_dve = B.op("vector", lambda e: e.tensor_scalar(
                    negc[:, 0:1], s_ps[0][:, 0:1], -0.5 / (128.0 * 128.0), None,
                    ALU.mult), "dve")
                B.wait("scalar", "dve", negc_dve)

                B.wait("tensor", "dve", negc_dve)
                steps = [(qc, j) for qc in range(4) for j in range(8 * qc + 8)]
                pending = []

                def emit_pv(ctx):
                    qc, j, pt3, c0, gsem, gval = ctx
                    nsteps = 8 * qc + 8
                    B.wait("tensor", gsem, gval)
                    if j == 0:
                        B.wait("tensor", "dve", evac_mark)
                    B.op("tensor", lambda e, j=j, pt3=pt3, c0=c0, n=nsteps:
                        e.matmul(
                            o_ps[0:65, c0:512],
                            v_store[:, j * 130: j * 130 + 65],
                            p_t[:, pt3 * 1024 + c0: pt3 * 1024 + 512],
                            start=(j == 0), stop=(j == n - 1)))
                    pv_pe = B.op("tensor",
                        lambda e, j=j, pt3=pt3, c0=c0, n=nsteps: e.matmul(
                            o_ps[0:65, 512 + c0:1024],
                            v_store[:, j * 130 + 65: j * 130 + 130],
                            p_t[:, pt3 * 1024 + 512 + c0: pt3 * 1024 + 1024],
                            start=(j == 0), stop=(j == n - 1)), "pe")
                    ptfree_mark[pt3] = pv_pe

                def drain_one():
                    ctx = pending.pop(0)
                    emit_pv(ctx)
                    if ctx[1] == 8 * ctx[0] + 7:
                        emit_norm(p, ctx[0])

                for (qc, j) in steps:
                    r = j - 8 * qc
                    c0 = COL0[r] if r >= 0 else 0
                    qbase = p * TOWN + qc * 512
                    par = jglob % 2
                    pt3 = jglob % 4
                    bank = s_ps[par]
                    B.wait("tensor", "act", sfree_mark.get(par, 0))
                    B.op("tensor",
                         lambda e, bank=bank, j=j, c0=c0, qbase=qbase, r=r: e.matmul(
                             bank[:, c0:512],
                             k_bf[0:64, j * 128:(j + 1) * 128],
                             q_all[0:64, qbase + c0: qbase + 512],
                             start=True, stop=(r < 0)))
                    qk_pe = B.op("tensor",
                         lambda e, bank=bank, j=j, c0=c0, qbase=qbase, r=r: e.matmul(
                             bank[:, 512 + c0:1024],
                             k_bf[64:128, j * 128:(j + 1) * 128],
                             q_all[64:128, qbase + c0: qbase + 512],
                             start=True, stop=(r < 0)), "pe" if r < 0 else None)
                    if r >= 0:
                        B.op("tensor", lambda e, bank=bank, c0=c0, r=r: e.matmul(
                            bank[:, c0:512], ident[:, :],
                            masks[:, MASK_OFF[r]:MASK_OFF[r + 1]],
                            start=False, stop=True))
                        qk_pe = B.op("tensor",
                            lambda e, bank=bank, c0=c0, r=r: e.matmul(
                                bank[:, 512 + c0:1024], ident[:, :],
                                masks[:, MASK_OFF[r]:MASK_OFF[r + 1]],
                                start=False, stop=True), "pe")
                    if p == NPAIR - 1 and qc == 3 and j == 31:
                        wfcst["qk_last"] = qk_pe
                    B.wait("scalar", "pe", qk_pe)
                    B.wait("scalar", "pe", ptfree_mark.get(pt3, 0))
                    if r < 0:
                        em = B.op("scalar", lambda e, bank=bank, pt3=pt3:
                            e.activation(
                                p_t[:, pt3 * 1024:(pt3 + 1) * 1024],
                                bank[:, :], AF.Exp, bias=negc[:, 0:1]), "act")
                    else:
                        B.op("scalar", lambda e, bank=bank, pt3=pt3, c0=c0:
                            e.activation(
                                p_t[:, pt3 * 1024 + c0: pt3 * 1024 + 512],
                                bank[:, c0:512], AF.Exp, bias=negc[:, 0:1]))
                        em = B.op("scalar", lambda e, bank=bank, pt3=pt3, c0=c0:
                            e.activation(
                                p_t[:, pt3 * 1024 + 512 + c0:
                                    pt3 * 1024 + 1024],
                                bank[:, 512 + c0:1024], AF.Exp,
                                bias=negc[:, 0:1]), "act")
                    sfree_mark[par] = em
                    jglob += 1
                    if len(pending) >= 3:
                        drain_one()
                    pending.append((qc, j, pt3, c0, "act", em))
                while pending:
                    drain_one()

            attn_done_g = B.cnt["g"]
            ph2_act = B.cnt["act"]
            ph2_dve = B.cnt["dve"]
            B.flush(block, S)
        p12.close()

        wfc_bf = xnf
        wfc_ready = wfcst["ready"]
        with contextlib.ExitStack() as p3:
            wout_stage = p3.enter_context(
                nc.sbuf_tensor("wout_stage", [128, 8192], F32))
            wout_bf = p3.enter_context(nc.sbuf_tensor("wout_bf", [128, 8192], BF16))
            aev = [p3.enter_context(nc.sbuf_tensor(f"aev{i}", [128, 1024], F32))
                   for i in range(2)]
            pp = [p3.enter_context(nc.psum_tensor(f"pp{i}", [128, 1024], F32))
                  for i in range(2)]

            B.wait("sync", "pe", wfcst["qk_last"])
            B.op("sync", lambda e: e.dma_start(
                out=wout_stage[:, :].rearrange("q (dc f) -> q dc f", dc=8),
                in_=woutT_in.rearrange("(dc q) f -> q dc f", q=128)), "ld", 16)
            B.wait("gpsimd", "ld", B.cnt["ld"])
            wout_ready = B.op("gpsimd", lambda e: e.tensor_copy(
                out=wout_bf[:, :], in_=wout_stage[:, :]), "pool")

            B.wait("tensor", "pool", wout_ready)
            B.wait("tensor", "g", attn_done_g)
            B.wait("tensor", "act", ph2_act)
            B.wait("tensor", "dve", ph2_dve)
            ap_free = {}
            ap_store = {}
            for tt in range(NTO):
                par = tt % 2
                bank = pp[par]
                B.wait("tensor", "dve", ap_free.get(par, 0))
                op_pe = None
                for dc in range(8):
                    B.op("tensor", lambda e, bank=bank, dc=dc, tt=tt: e.matmul(
                        bank[:, 0:512],
                        out_norm[:, dc * TOWN + tt * 128: dc * TOWN + (tt + 1) * 128],
                        wout_bf[:, dc * 1024: dc * 1024 + 512],
                        start=(dc == 0), stop=(dc == 7)))
                    if dc < 7:
                        B.op("tensor", lambda e, bank=bank, dc=dc, tt=tt: e.matmul(
                            bank[:, 512:1024],
                            out_norm[:, dc * TOWN + tt * 128:
                                     dc * TOWN + (tt + 1) * 128],
                            wout_bf[:, dc * 1024 + 512:(dc + 1) * 1024],
                            start=(dc == 0), stop=False))
                    else:
                        op_pe = B.op("tensor",
                            lambda e, bank=bank, dc=dc, tt=tt: e.matmul(
                                bank[:, 512:1024],
                                out_norm[:, dc * TOWN + tt * 128:
                                         dc * TOWN + (tt + 1) * 128],
                                wout_bf[:, dc * 1024 + 512:(dc + 1) * 1024],
                                start=(dc == 0), stop=True), "pe")
                B.wait("vector", "pe", op_pe)
                B.wait("vector", "g", ap_store.get(par, 0))
                ap_free[par] = B.op("vector", lambda e, bank=bank, par=par:
                    e.tensor_copy(out=aev[par][:, :], in_=bank[:, :]), "dve")
                B.wait("sync", "dve", ap_free[par])
                ap_store[par] = B.op("sync", lambda e, tt=tt: e.dma_start(
                    out=aproj_hbm[tt * 128:(tt + 1) * 128, :],
                    in_=aev[tt % 2][:, :]), "g", 16)
            aproj_g = B.cnt["g"]
            ph3_dve = B.cnt["dve"]
            B.flush(block, S)
        p23.close()

        with contextlib.ExitStack() as p4:
            xn_own2 = p4.enter_context(nc.sbuf_tensor("xn_own2", [128, 8 * TOWN], BF16))
            b_bc = p4.enter_context(nc.sbuf_tensor("b_bc", [128, 1024], F32))
            brow_o = p4.enter_context(nc.sbuf_tensor("brow_o", [1, D], F32))
            with contextlib.ExitStack() as p4a:
                xstage4 = p4a.enter_context(nc.sbuf_tensor("xstage4", [128, 4096], F32))
                xdump4 = p4a.enter_context(nc.sbuf_tensor("xdump4", [128, 4 * 1024], F32))
                stat4 = p4a.enter_context(nc.sbuf_tensor("stat4", [128, 64], F32))
                xn_tm4 = p4a.enter_context(nc.sbuf_tensor("xn_tm4", [128, 4096], BF16))
                tr_ps4 = [p4a.enter_context(nc.psum_tensor(f"trq{i}", [128, 1024], BF16))
                          for i in range(2)]
                bb_ps = p4a.enter_context(nc.psum_tensor("bb_ps", [128, 1024], F32))

                B.wait("tensor", "dve", ph3_dve)
                B.wait("sync", "pool", wout_ready)
                B.op("sync", lambda e: e.dma_start(out=brow_o[:, :], in_=b_out[:, :]),
                     "ld", 16)
                bo_ld = B.cnt["ld"]
                emit_ln(x_own, NTO, xn_own2, TOWN, xstage4, stat4, xdump4, xn_tm4,
                        tr_ps4)
                xn2_dve = B.cnt["dve"]

                B.wait("tensor", "ld", bo_ld)
                B.op("tensor", lambda e: e.matmul(
                    bb_ps[:, 0:512], onesb[0:1, 0:128], brow_o[0:1, 0:512],
                    start=True, stop=True))
                bb_pe = B.op("tensor", lambda e: e.matmul(
                    bb_ps[:, 512:1024], onesb[0:1, 0:128], brow_o[0:1, 512:1024],
                    start=True, stop=True), "pe")
                B.wait("vector", "pe", bb_pe)
                bb_dve = B.op("vector", lambda e: e.tensor_copy(
                    out=b_bc[:, :], in_=bb_ps[:, :]), "dve")
                B.flush(block, S)

            h_store = p4.enter_context(nc.sbuf_tensor("h_store", [128, 32 * 512], BF16))
            ztmp = p4.enter_context(nc.sbuf_tensor("ztmp", [128, 2 * 512], BF16))
            wpstage = [p4.enter_context(nc.sbuf_tensor(f"wpst{i}", [128, 1024], F32))
                       for i in range(2)]
            wproj_bf = [p4.enter_context(nc.sbuf_tensor(f"wpbf{i}", [128, 1024], BF16))
                        for i in range(2)]
            ybuf = p4.enter_context(nc.sbuf_tensor("ybuf", [128, 2 * 1024], F32))
            smem = [p4.enter_context(nc.sbuf_tensor(f"smem{i}", [128, 1024], F32))
                    for i in range(2)]
            stmp = [p4.enter_context(nc.sbuf_tensor(f"stmp{i}", [128, 1024], F32))
                    for i in range(2)]
            sacc = [p4.enter_context(nc.sbuf_tensor(f"sacc{i}", [128, 1024], F32))
                    for i in range(2)]
            xo_t = [p4.enter_context(nc.sbuf_tensor(f"xo_t{i}", [128, 1024], F32))
                    for i in range(2)]
            ap_t = [p4.enter_context(nc.sbuf_tensor(f"ap_t{i}", [128, 1024], F32))
                    for i in range(2)]
            o_stage = stmp

            wp_consumed = {}
            wp_ready = {}
            out_store = {}
            fin_mark = {0: 0, 1: 0}
            y_evac_act = 0
            y_pe_prev = 0
            for tc in range(4):
                with contextlib.ExitStack() as pth:
                    hps = [pth.enter_context(
                        nc.psum_tensor(f"hps{tc}_{i}", [128, 512], F32))
                        for i in range(2)]
                    if tc == 0:
                        B.wait("tensor", "act", wfc_ready)
                        B.wait("tensor", "dve", max(xn2_dve, bb_dve))
                    else:
                        B.wait("tensor", "act", y_evac_act)
                        B.wait("scalar", "pe", y_pe_prev)
                    hz_free = {}
                    hsq_done = {}
                    for fc in range(32):
                        par = fc % 2
                        bank = hps[par]
                        B.wait("tensor", "act", hz_free.get(par, 0))
                        hmm = None
                        for dc in range(8):
                            if dc < 7:
                                B.op("tensor",
                                    lambda e, bank=bank, dc=dc, fc=fc, tc=tc: e.matmul(
                                        bank[:, :],
                                        wfc_bf[:, dc * FF + fc * 128:
                                               dc * FF + (fc + 1) * 128],
                                        xn_own2[:, dc * TOWN + tc * 512:
                                                dc * TOWN + (tc + 1) * 512],
                                        start=(dc == 0), stop=False))
                            else:
                                hmm = B.op("tensor",
                                    lambda e, bank=bank, dc=dc, fc=fc, tc=tc: e.matmul(
                                        bank[:, :],
                                        wfc_bf[:, dc * FF + fc * 128:
                                               dc * FF + (fc + 1) * 128],
                                        xn_own2[:, dc * TOWN + tc * 512:
                                                dc * TOWN + (tc + 1) * 512],
                                        start=False, stop=True), "pe")
                        B.wait("scalar", "pe", hmm)
                        hz = B.op("scalar",
                            lambda e, bank=bank, par=par: e.activation(
                                ztmp[:, par * 512:(par + 1) * 512], bank[:, :],
                                AF.Relu), "act", drain=False)
                        B.wait("scalar", "act", hz)
                        hz_free[par] = hsq_done[par] = B.op("scalar",
                            lambda e, fc=fc, par=par: e.activation(
                                h_store[:, fc * 512:(fc + 1) * 512],
                                ztmp[:, par * 512:(par + 1) * 512],
                                AF.Square), "act")
                    h_dve = B.cnt["act"]
                    B.flush(block, S)
                with contextlib.ExitStack() as pty:
                    yps = [pty.enter_context(
                        nc.psum_tensor(f"yps{tc}_{i}", [128, 1024], F32))
                        for i in range(4)]

                    for weng in ("tensor", "scalar", "vector", "gpsimd", "sync"):
                        for wsem in ("pe", "act", "dve", "pool"):
                            B.wait(weng, wsem, B.cnt[wsem])
                    B.wait("tensor", "act", h_dve)
                    for fc in range(32):
                        par = fc % 2
                        B.wait("sync", "pe", wp_consumed.get(par, 0))
                        B.op("sync", lambda e, fc=fc: e.dma_start(
                            out=wpstage[fc % 2][:, :],
                            in_=wprojT_in[fc * 128:(fc + 1) * 128, :]), "ld", 16)
                        B.wait("scalar", "ld", B.cnt["ld"])
                        B.wait("scalar", "pe", wp_consumed.get(par, 0))
                        wp_ready[par] = B.op("scalar", lambda e, par=par: e.activation(
                            wproj_bf[par][:, :], wpstage[par][:, :], AF.Copy), "act")
                        B.wait("tensor", "act", wp_ready[par])
                        last = None
                        for ti in range(4):
                            B.op("tensor", lambda e, fc=fc, ti=ti, par=par: e.matmul(
                                yps[ti][:, 0:512],
                                h_store[:, fc * 512 + ti * 128:
                                        fc * 512 + (ti + 1) * 128],
                                wproj_bf[par][:, 0:512],
                                start=(fc == 0), stop=(fc == 31)))
                            if ti < 3:
                                B.op("tensor", lambda e, fc=fc, ti=ti, par=par:
                                    e.matmul(
                                        yps[ti][:, 512:1024],
                                        h_store[:, fc * 512 + ti * 128:
                                                fc * 512 + (ti + 1) * 128],
                                        wproj_bf[par][:, 512:1024],
                                        start=(fc == 0), stop=(fc == 31)))
                            else:
                                last = B.op("tensor", lambda e, fc=fc, ti=ti, par=par:
                                    e.matmul(
                                        yps[ti][:, 512:1024],
                                        h_store[:, fc * 512 + ti * 128:
                                                fc * 512 + (ti + 1) * 128],
                                        wproj_bf[par][:, 512:1024],
                                        start=(fc == 0), stop=(fc == 31)), "pe")
                        wp_consumed[par] = last
                    y_pe = B.cnt["pe"]
                    y_pe_prev = y_pe

                    for weng in ("tensor", "scalar", "vector", "gpsimd", "sync"):
                        for wsem in ("pe", "act", "dve", "pool"):
                            B.wait(weng, wsem, B.cnt[wsem])
                    for ti in range(4):
                        tt = tc * 4 + ti
                        par = ti % 2
                        eng = "vector"
                        esem = "dve"
                        B.wait("scalar", "pe", y_pe)
                        B.wait("scalar", esem, fin_mark[par])
                        yv = B.op("scalar", lambda e, ti=ti, par=par: e.activation(
                            ybuf[:, par * 1024:(par + 1) * 1024], yps[ti][:, :],
                            AF.Copy), "act")
                        y_evac_act = yv
                        B.wait("sync", "g", aproj_g)
                        B.wait("sync", esem, fin_mark[par])
                        B.op("sync", lambda e, tt=tt, par=par: e.dma_start(
                            out=xo_t[par][:, :],
                            in_=x_own[tt * 128:(tt + 1) * 128, :]), "ld", 16)
                        B.op("sync", lambda e, tt=tt, par=par: e.dma_start(
                            out=ap_t[par][:, :],
                            in_=aproj_hbm[tt * 128:(tt + 1) * 128, :]), "ld", 16)
                        fin_ld = B.cnt["ld"]
                        B.wait(eng, "act", yv)
                        B.wait(eng, "dve", bb_dve)
                        yb = ybuf[:, par * 1024:(par + 1) * 1024]
                        sm, st_, sa = smem[par], stmp[par], sacc[par]

                        yoff = par * 1024

                        def sop(fn, eng=eng, esem=esem, fence=False):
                            B.op(eng, lambda e, f=fn: f(e, 0, 512),
                                 esem, drain=False)
                            t = B.op(eng, lambda e, f=fn: f(e, 512, 1024),
                                     esem, drain=False)
                            if fence:
                                B.wait(eng, esem, t)
                            return t
                        sop(lambda e, lo, hi, yoff=yoff, sa=sa: e.tensor_scalar(
                            sa[:, lo:hi], ybuf[:, yoff + lo:yoff + hi],
                            1.0, None, ALU.is_gt))
                        sop(lambda e, lo, hi, yoff=yoff, st_=st_:
                            e.scalar_tensor_tensor(
                                st_[:, lo:hi], ybuf[:, yoff + lo:yoff + hi], 1.0,
                                ybuf[:, yoff + lo:yoff + hi],
                                ALU.is_gt, ALU.subtract))
                        sop(lambda e, lo, hi, yoff=yoff, sm=sm, st_=st_:
                            e.scalar_tensor_tensor(
                                sm[:, lo:hi], ybuf[:, yoff + lo:yoff + hi], 0.9,
                                st_[:, lo:hi], ALU.mult, ALU.subtract))
                        sop(lambda e, lo, hi, sm=sm, sa=sa:
                            e.scalar_tensor_tensor(
                                sa[:, lo:hi], sm[:, lo:hi], 1.0, sa[:, lo:hi],
                                ALU.is_gt, ALU.add))
                        for _ in range(3):
                            sop(lambda e, lo, hi, yoff=yoff, sm=sm, st_=st_:
                                e.scalar_tensor_tensor(
                                    st_[:, lo:hi], sm[:, lo:hi], 1.0,
                                    ybuf[:, yoff + lo:yoff + hi],
                                    ALU.is_gt, ALU.subtract))
                            sop(lambda e, lo, hi, sm=sm, st_=st_:
                                e.scalar_tensor_tensor(
                                    sm[:, lo:hi], sm[:, lo:hi], 0.9, st_[:, lo:hi],
                                    ALU.mult, ALU.subtract))
                            sop(lambda e, lo, hi, sm=sm, sa=sa:
                                e.scalar_tensor_tensor(
                                    sa[:, lo:hi], sm[:, lo:hi], 1.0, sa[:, lo:hi],
                                    ALU.is_gt, ALU.add))
                        B.wait(eng, "ld", fin_ld)
                        B.wait(eng, "st", out_store.get(par, 0))
                        sop(lambda e, lo, hi, sa=sa, par=par:
                            e.scalar_tensor_tensor(
                                o_stage[par][:, lo:hi], sa[:, lo:hi], 0.2,
                                ap_t[par][:, lo:hi], ALU.mult, ALU.add))
                        sop(lambda e, lo, hi, par=par: e.tensor_tensor(
                            o_stage[par][:, lo:hi], o_stage[par][:, lo:hi],
                            xo_t[par][:, lo:hi], ALU.add))
                        fin_mark[par] = sop(lambda e, lo, hi, par=par:
                            e.tensor_tensor(
                                o_stage[par][:, lo:hi], o_stage[par][:, lo:hi],
                                b_bc[:, lo:hi], ALU.add), fence=True)
                        B.wait("sync", esem, fin_mark[par])
                        out_store[par] = B.op("sync", lambda e, tt=tt, par=par:
                            e.dma_start(
                                out=out_hbm[tt * 128:(tt + 1) * 128, :],
                                in_=o_stage[par][:, :]), "st", 16)

                    B.flush(block, S)

            B.wait("sync", "st", B.cnt["st"])
            B.flush(block, S)

    return nc


def _rope_tables(positions, scale):
    inv = 1.0 / (10000.0 ** (np.arange(0, 64, 2, dtype=np.float64) / 64.0))
    fr = positions.astype(np.float64)[None, :] * inv[:, None]
    c = np.cos(fr) * scale
    s = np.sin(fr) * scale
    bf = ml_dtypes.bfloat16
    cos_t = np.concatenate([c, c, c, c], axis=0).astype(np.float32)
    sin_t = np.concatenate([-s, s, -s, s], axis=0).astype(np.float32)
    return cos_t.astype(bf), sin_t.astype(bf)


def _build_masks(g):
    bf = ml_dtypes.bfloat16
    m = np.zeros((128, MASK_OFF[8]), dtype=np.float32)
    tri = (np.arange(128)[None, :] >= np.arange(128)[:, None]).astype(np.float32)
    for r in range(8):
        w = 512 - COL0[r]
        tile = np.zeros((128, w), dtype=np.float32)
        for col in range(w):
            gc = COL0[r] + col
            i = gc // 128
            if r < 2 * i + g:
                tile[:, col] = 1.0
            elif r == 2 * i + g:
                tile[:, col] = tri[:, gc % 128]
        m[:, MASK_OFF[r]:MASK_OFF[r + 1]] = tile
    return np.ascontiguousarray(((m - 1.0) * 100.0).astype(bf))


_NC_CACHE = []


def kernel(L0_attn_x, L0_attn_w_qkv, L0_attn_b_qkv, L0_attn_w_out, L0_attn_b_out,
           L0_ffn_W_fc, L0_ffn_W_proj):
    x = np.ascontiguousarray(np.asarray(L0_attn_x, dtype=np.float32))
    w_qkv = np.asarray(L0_attn_w_qkv, dtype=np.float32)
    b_qkv = np.asarray(L0_attn_b_qkv, dtype=np.float32)
    w_out = np.asarray(L0_attn_w_out, dtype=np.float32)
    b_out = np.asarray(L0_attn_b_out, dtype=np.float32)
    W_fc = np.asarray(L0_ffn_W_fc, dtype=np.float32)
    W_proj = np.asarray(L0_ffn_W_proj, dtype=np.float32)
    bf = ml_dtypes.bfloat16

    perm64 = np.concatenate([np.arange(0, 64, 2), np.arange(1, 64, 2)])
    qk_rows = np.concatenate([64 * h + perm64 for h in range(16)])
    wqT = np.ascontiguousarray(w_qkv[qk_rows, :].T)
    wkT = np.ascontiguousarray(w_qkv[1024 + qk_rows, :].T)
    bq_row = np.ascontiguousarray(b_qkv[qk_rows][None, :])
    bk_row = np.ascontiguousarray(b_qkv[1024 + qk_rows][None, :])
    wvT_full = w_qkv[2048:3072, :].T
    wvT = np.zeros((D, NPAIR * 130), dtype=np.float32)
    bv_row = np.zeros((1, NPAIR * 130), dtype=np.float32)
    bv_full = b_qkv[2048:3072]
    for p in range(NPAIR):
        wvT[:, p * 130: p * 130 + 64] = wvT_full[:, p * 128: p * 128 + 64]
        wvT[:, p * 130 + 65: p * 130 + 129] = wvT_full[:, p * 128 + 64:(p + 1) * 128]
        bv_row[0, p * 130: p * 130 + 64] = bv_full[p * 128: p * 128 + 64]
        bv_row[0, p * 130 + 64] = 1.0
        bv_row[0, p * 130 + 65: p * 130 + 129] = bv_full[p * 128 + 64:(p + 1) * 128]
        bv_row[0, p * 130 + 129] = 1.0
    woutT = np.ascontiguousarray(w_out.T)
    wfcT = np.ascontiguousarray(W_fc.T)
    wprojT = np.ascontiguousarray(W_proj.T)

    ident = np.eye(128, dtype=np.float32).astype(bf)
    sigma = np.concatenate([np.arange(32, 64), np.arange(0, 32),
                            np.arange(96, 128), np.arange(64, 96)])
    permM = np.zeros((128, 128), dtype=np.float32)
    permM[sigma, np.arange(128)] = 1.0
    permM = permM.astype(bf)

    rk_cos, rk_sin = _rope_tables(np.arange(T), 1.0)

    if not _NC_CACHE:
        _NC_CACHE.append(build_program())
    nc = _NC_CACHE[0]

    in_maps = []
    own_idx_all = []
    for c in range(8):
        b, g = c // 2, c % 2
        own_idx = np.concatenate(
            [np.arange(128 * (2 * j + g), 128 * (2 * j + g) + 128)
             for j in range(NTO)])
        own_idx_all.append(own_idx)
        xb = x[b]
        q_cos, q_sin = _rope_tables(own_idx, 0.125)
        in_maps.append({
            "x_seq": xb,
            "x_own": np.ascontiguousarray(xb[own_idx]),
            "wqT": wqT, "wkT": wkT, "wvT": wvT,
            "bq": bq_row, "bk": bk_row, "bv": bv_row,
            "woutT": woutT, "b_out": np.ascontiguousarray(b_out[None, :]),
            "wfcT": wfcT, "wprojT": wprojT,
            "ropeq_cos": q_cos, "ropeq_sin": q_sin,
            "ropek_cos": rk_cos, "ropek_sin": rk_sin,
            "masks": _build_masks(g),
            "ident": ident, "permT": permM,
        })

    res = run_bass_kernel_spmd(nc, in_maps, core_ids=list(range(8)))

    out = np.empty((4, T, D), dtype=np.float32)
    for c in range(8):
        b = c // 2
        out[b, own_idx_all[c], :] = np.asarray(res.results[c]["out"],
                                               dtype=np.float32)
    return out


# version 38
# speedup vs baseline: 1.2164x; 1.0132x over previous
import contextlib
import numpy as np
import ml_dtypes

import concourse.bass as bass
import concourse.mybir as mybir
from concourse.bass_utils import run_bass_kernel_spmd

F32 = mybir.dt.float32
BF16 = mybir.dt.bfloat16
AF = mybir.ActivationFunctionType
ALU = mybir.AluOpType
AX = mybir.AxisListType

D = 1024
T = 4096
TOWN = 2048
NPAIR = 8
FF = 4096
NT = T // 128
NTO = TOWN // 128
EPS = 1e-5

ENGINES = ("sync", "vector", "scalar", "tensor", "gpsimd")
COL0 = [0, 0, 128, 128, 256, 256, 384, 384]
MASK_OFF = np.cumsum([0] + [512 - c for c in COL0]).tolist()


class Builder:
    def __init__(self):
        self.ops = {e: [] for e in ENGINES}
        self.cnt = {s: 0 for s in ("ld", "g", "st", "pe", "act", "dve", "pool")}

    def inc(self, sem, n):
        self.cnt[sem] += n
        return self.cnt[sem]

    def wait(self, eng, sem, val):
        if val and val > 0:
            self.ops[eng].append(lambda e, S, s=sem, v=val: e.wait_ge(S[s], v))

    def op(self, eng, fn, inc_sem=None, inc_n=1, drain=None):
        if inc_sem is None:
            self.ops[eng].append(lambda e, S, f=fn: f(e))
            return None
        if drain is None:
            drain = inc_sem == "pool"
        tgt = self.inc(inc_sem, inc_n)
        if drain:
            self.ops[eng].append(lambda e, S, f=fn: f(e))
            self.ops[eng].append(
                lambda e, S, s=inc_sem, n=inc_n: e.drain().then_inc(S[s], n))
        else:
            self.ops[eng].append(
                lambda e, S, f=fn, s=inc_sem, n=inc_n: f(e).then_inc(S[s], n))
        return tgt

    def flush(self, block, S):
        for eng in ENGINES:
            ops = self.ops[eng]
            if not ops:
                continue
            self.ops[eng] = []

            def body(e, ops=ops):
                for fn in ops:
                    fn(e, S)
            getattr(block, eng)(body)


def build_program():
    nc = bass.Bass("TRN2", target_bir_lowering=False)

    def din(name, shape, dt=F32):
        return nc.dram_tensor(name, shape, dt, kind="ExternalInput").ap()

    x_seq = din("x_seq", [T, D])
    x_own = din("x_own", [TOWN, D])
    wqT = din("wqT", [D, D])
    wkT = din("wkT", [D, D])
    wvT = din("wvT", [D, NPAIR * 130])
    bq = din("bq", [1, D])
    bk = din("bk", [1, D])
    bv = din("bv", [1, NPAIR * 130])
    woutT_in = din("woutT", [D, D])
    b_out = din("b_out", [1, D])
    wfcT_in = din("wfcT", [D, FF])
    wprojT_in = din("wprojT", [FF, D])
    rq_cos = din("ropeq_cos", [128, TOWN], BF16)
    rq_sin = din("ropeq_sin", [128, TOWN], BF16)
    rk_cos = din("ropek_cos", [128, T], BF16)
    rk_sin = din("ropek_sin", [128, T], BF16)
    masks_in = din("masks", [128, MASK_OFF[8]], BF16)
    ident_in = din("ident", [128, 128], BF16)
    perm_in = din("permT", [128, 128], BF16)

    out_hbm = nc.dram_tensor("out", [TOWN, D], F32, kind="ExternalOutput").ap()
    aproj_hbm = nc.dram_tensor("aproj_scratch", [TOWN, D], F32).ap()

    B = Builder()

    with contextlib.ExitStack() as top:
        S = {}
        for s in ("ld", "g", "st", "pe", "act", "dve", "pool"):
            S[s] = top.enter_context(nc.semaphore(s))
        block = top.enter_context(nc.Block())

        onesb = top.enter_context(nc.sbuf_tensor("onesb", [65, 128], F32, side="right"))
        ones_bf = top.enter_context(nc.sbuf_tensor("ones_bf", [65, 512], BF16, side="right"))
        ident = top.enter_context(nc.sbuf_tensor("ident_sb", [128, 128], BF16, side="right"))
        permT = top.enter_context(nc.sbuf_tensor("permT_sb", [128, 128], BF16, side="right"))
        epscol = top.enter_context(nc.sbuf_tensor("epscol", [128, 1], F32, side="right"))

        B.op("gpsimd", lambda e: e.memset(onesb[:, :], 1.0), "pool")
        B.op("gpsimd", lambda e: e.memset(ones_bf[:, :], 1.0), "pool")
        B.op("gpsimd", lambda e: e.memset(epscol[:, :], EPS), "pool")
        for dst, src in ((ident, ident_in), (permT, perm_in)):
            B.op("sync", lambda e, d=dst, s=src: e.dma_start(out=d[:, :], in_=s[:, :]),
                 "ld", 16)
        consts_ld = B.cnt["ld"]
        consts_pool = B.cnt["pool"]
        B.wait("scalar", "pool", consts_pool)

        xnf = top.enter_context(nc.sbuf_tensor("xnf", [128, 8 * T], BF16))
        p12 = contextlib.ExitStack()
        q_all = p12.enter_context(nc.sbuf_tensor("q_all", [128, NPAIR * TOWN], BF16))

        def emit_ln(src_hbm, ntiles, dst_fm, dst_cols, xstage, stat, xdump,
                    xn_tm, tr_ps, hook=None):
            nslots = 4
            dve0 = B.cnt["dve"]
            slot_free = {}
            tr_free = {}
            red2_prev = 0
            for t0 in range(0, ntiles, 4):
                nb = min(4, ntiles - t0)
                sls = [(t0 + i) % nslots for i in range(nb)]
                for i in range(nb):
                    t = t0 + i
                    B.wait("sync", "dve", slot_free.get(sls[i], dve0))
                    B.op("sync", lambda e, slot=sls[i], t=t: e.dma_start(
                        out=xstage[:, slot * 1024:(slot + 1) * 1024],
                        in_=src_hbm[t * 128:(t + 1) * 128, :]), "ld", 16)
                ld_done = B.cnt["ld"]
                B.wait("vector", "ld", ld_done)
                for s in sls:
                    B.op("vector", lambda e, s=s: e.reduce_sum(
                        stat[:, s * 8:s * 8 + 1],
                        xstage[:, s * 1024:(s + 1) * 1024], AX.X))
                B.wait("scalar", "ld", ld_done)
                B.wait("scalar", "dve", red2_prev)
                sq = None
                for s in sls:
                    sq = B.op("scalar", lambda e, s=s: e.activation(
                        xdump[:, s * 1024:(s + 1) * 1024],
                        xstage[:, s * 1024:(s + 1) * 1024], AF.Square), "act")
                B.wait("vector", "act", sq)
                red2_prev = 0
                for s in sls:
                    red2_prev = B.op("vector", lambda e, s=s: e.reduce_sum(
                        stat[:, s * 8 + 1:s * 8 + 2],
                        xdump[:, s * 1024:(s + 1) * 1024], AX.X), "dve")
                for s in sls:
                    B.op("vector", lambda e, s0=s * 8: e.tensor_scalar(
                        stat[:, s0:s0 + 1], stat[:, s0:s0 + 1], -1.0 / D, None,
                        ALU.mult))
                for s in sls:
                    B.op("vector", lambda e, s0=s * 8: e.tensor_scalar(
                        stat[:, s0 + 2:s0 + 3], stat[:, s0 + 1:s0 + 2], 1.0 / D, None,
                        ALU.mult))
                for s in sls:
                    B.op("vector", lambda e, s0=s * 8: e.tensor_tensor(
                        stat[:, s0 + 3:s0 + 4], stat[:, s0:s0 + 1], stat[:, s0:s0 + 1],
                        ALU.mult))
                last = None
                for s in sls:
                    last = B.op("vector", lambda e, s0=s * 8: e.tensor_tensor(
                        stat[:, s0 + 4:s0 + 5], stat[:, s0 + 2:s0 + 3],
                        stat[:, s0 + 3:s0 + 4], ALU.subtract), "dve")
                B.wait("scalar", "dve", last)
                la = None
                for s in sls:
                    la = B.op("scalar", lambda e, s0=s * 8: e.activation(
                        stat[:, s0 + 5:s0 + 6], stat[:, s0 + 4:s0 + 5], AF.Sqrt,
                        bias=epscol[:, 0:1]), "act")
                B.wait("vector", "act", la)
                for s in sls:
                    B.op("vector", lambda e, s0=s * 8: e.reciprocal(
                        stat[:, s0 + 6:s0 + 7], stat[:, s0 + 5:s0 + 6]))
                for i in range(nb):
                    s = sls[i]
                    slot_free[s] = B.op("vector", lambda e, slot=s, s0=s * 8:
                        e.tensor_scalar(
                            xn_tm[:, slot * 1024:(slot + 1) * 1024],
                            xstage[:, slot * 1024:(slot + 1) * 1024],
                            stat[:, s0:s0 + 1], stat[:, s0 + 6:s0 + 7],
                            ALU.add, ALU.mult), "dve")
                if hook is not None:
                    hook(t0 // 4)
                for i in range(nb):
                    t = t0 + i
                    slot = sls[i]
                    B.wait("tensor", "dve", slot_free[slot])
                    for c in range(8):
                        k = t * 8 + c
                        pb = tr_ps[k % 2]
                        B.wait("tensor", "dve", tr_free.get(k % 2, dve0))
                        B.op("tensor", lambda e, pb=pb, slot=slot, c=c: e.transpose(
                            pb[:, 0:128],
                            xn_tm[:, slot * 1024 + c * 128: slot * 1024 + (c + 1) * 128],
                            ident[:, :]), "pe")
                        B.wait("vector", "pe", B.cnt["pe"])
                        tr_free[k % 2] = B.op("vector",
                            lambda e, pb=pb, c=c, t=t: e.tensor_copy(
                                out=dst_fm[:, c * dst_cols + t * 128:
                                           c * dst_cols + t * 128 + 128],
                                in_=pb[:, 0:128]), "dve")

        def emit_rope(pre_ap, perm_bank, cosap, sinap, dstap, t1, t2, pre_act):
            B.wait("tensor", "act", pre_act)
            pe_perm = B.op("tensor", lambda e: e.matmul(
                perm_bank, permT[:, :], pre_ap, start=True, stop=True), "pe")
            B.wait("vector", "act", pre_act)
            B.op("vector", lambda e: e.tensor_tensor(t1[:, :], pre_ap, cosap, ALU.mult))
            B.wait("vector", "pe", pe_perm)
            t2done = B.op("vector", lambda e: e.tensor_tensor(t2[:, :], perm_bank, sinap,
                                                              ALU.mult), "dve")
            B.wait("vector", "dve", t2done)
            return B.op("vector", lambda e: e.tensor_tensor(dstap, t1[:, :], t2[:, :],
                                                            ALU.add), "dve")

        with contextlib.ExitStack() as p1:
            xn_own = p1.enter_context(nc.sbuf_tensor("xn_own", [128, 8 * TOWN], BF16))
            xstage = p1.enter_context(nc.sbuf_tensor("xstage", [128, 4 * 1024], F32))
            xdump = p1.enter_context(nc.sbuf_tensor("xdump", [128, 4 * 1024], F32))
            stat = p1.enter_context(nc.sbuf_tensor("stat", [128, 64], F32))
            xn_tm = p1.enter_context(nc.sbuf_tensor("xn_tm", [128, 4 * 1024], BF16))
            tr_ps = [p1.enter_context(nc.psum_tensor(f"trp{i}", [128, 1024], BF16))
                     for i in range(2)]
            brow_q = p1.enter_context(nc.sbuf_tensor("brow_q", [1, D], F32))
            brow_qb = p1.enter_context(nc.sbuf_tensor("brow_qb", [1, D], BF16))
            rqc = p1.enter_context(nc.sbuf_tensor("rqc", [128, TOWN], BF16))
            rqs = p1.enter_context(nc.sbuf_tensor("rqs", [128, TOWN], BF16))
            wstage = p1.enter_context(nc.sbuf_tensor("wstage", [128, 1024], F32))
            wq_bf = p1.enter_context(nc.sbuf_tensor("wq_bf", [128, 1024], BF16))
            qpre = p1.enter_context(nc.sbuf_tensor("qpre", [128, 2 * 512], BF16))
            t1b = p1.enter_context(nc.sbuf_tensor("t1b", [128, 512], BF16))
            t2b = p1.enter_context(nc.sbuf_tensor("t2b", [128, 512], BF16))
            pr_ps = [p1.enter_context(nc.psum_tensor(f"prp{i}", [128, 512], F32))
                     for i in range(2)]
            pm_ps = [p1.enter_context(nc.psum_tensor(f"pmp{i}", [128, 512], F32))
                     for i in range(2)]

            B.op("sync", lambda e: e.dma_start(out=rqc[:, :], in_=rq_cos[:, :]), "ld", 16)
            B.op("sync", lambda e: e.dma_start(out=rqs[:, :], in_=rq_sin[:, :]), "ld", 16)
            B.op("sync", lambda e: e.dma_start(out=brow_q[:, :], in_=bq[:, :]), "ld", 16)
            rope_tab_ld = B.cnt["ld"]
            B.wait("vector", "ld", rope_tab_ld)
            bqb_dve = B.op("vector", lambda e: e.tensor_copy(out=brow_qb[:, :],
                                                             in_=brow_q[:, :]), "dve")
            B.wait("tensor", "dve", bqb_dve)
            B.wait("tensor", "ld", max(consts_ld, rope_tab_ld))
            B.wait("tensor", "pool", consts_pool)
            emit_ln(x_own, NTO, xn_own, TOWN, xstage, stat, xdump, xn_tm, tr_ps)
            ln_done_dve = B.cnt["dve"]

            B.wait("tensor", "dve", ln_done_dve)
            B.wait("vector", "ld", rope_tab_ld)
            proj_free = {}
            rope_done = {}

            def emit_qpair(p):
                B.wait("sync", "pool", B.cnt["pool"])
                B.op("sync", lambda e, p=p: e.dma_start(
                    out=wstage[:, :].rearrange("q (dc f) -> q dc f", dc=8),
                    in_=wqT[:, p * 128:(p + 1) * 128]
                    .rearrange("(dc q) f -> q dc f", q=128)), "ld", 16)
                B.wait("gpsimd", "ld", B.cnt["ld"])
                B.wait("gpsimd", "pe", B.cnt["pe"])
                wq_ready = B.op("gpsimd", lambda e: e.tensor_copy(
                    out=wq_bf[:, :], in_=wstage[:, :]), "pool")
                B.wait("tensor", "pool", wq_ready)
                for ch in range(4):
                    k = p * 4 + ch
                    par = k % 2
                    bank = pr_ps[par]
                    B.wait("tensor", "act", proj_free.get(par, 0))
                    for dc in range(8):
                        B.op("tensor", lambda e, bank=bank, dc=dc, ch=ch: e.matmul(
                            bank[:, :],
                            wq_bf[:, dc * 128:(dc + 1) * 128],
                            xn_own[:, dc * TOWN + ch * 512: dc * TOWN + (ch + 1) * 512],
                            start=(dc == 0), stop=False))
                    pe_proj = B.op("tensor", lambda e, bank=bank, p=p: e.matmul(
                        bank[:, :], brow_qb[0:1, p * 128:(p + 1) * 128],
                        ones_bf[0:1, 0:512], start=False, stop=True), "pe")
                    B.wait("scalar", "pe", pe_proj)
                    B.wait("scalar", "dve", rope_done.get(par, 0))
                    proj_free[par] = B.op("scalar", lambda e, bank=bank, par=par:
                        e.activation(qpre[:, par * 512:(par + 1) * 512], bank[:, :],
                                     AF.Copy), "act")
                    B.wait("tensor", "dve", rope_done.get(par, 0))
                    rope_done[par] = emit_rope(
                        qpre[:, par * 512:(par + 1) * 512],
                        pm_ps[par][:, :],
                        rqc[:, ch * 512:(ch + 1) * 512],
                        rqs[:, ch * 512:(ch + 1) * 512],
                        q_all[:, p * TOWN + ch * 512: p * TOWN + (ch + 1) * 512],
                        t1b, t2b, proj_free[par])

            emit_ln(x_seq, NT, xnf, T, xstage, stat, xdump, xn_tm, tr_ps,
                    hook=emit_qpair)
            q_done_dve = B.cnt["dve"]
            B.flush(block, S)


        p23 = contextlib.ExitStack()
        out_norm = p23.enter_context(
            nc.sbuf_tensor("out_norm", [128, NPAIR * TOWN], BF16, side="right"))
        with contextlib.ExitStack() as p2:
            brow_k = p2.enter_context(nc.sbuf_tensor("brow_k", [1, D], F32))
            brow_kb = p2.enter_context(nc.sbuf_tensor("brow_kb", [1, D], BF16))
            brow_vb = p2.enter_context(nc.sbuf_tensor("brow_vb", [1, NPAIR * 130], BF16))
            rkc = p2.enter_context(nc.sbuf_tensor("rkc", [128, T], BF16))
            rks = p2.enter_context(nc.sbuf_tensor("rks", [128, T], BF16))
            masks = p2.enter_context(
                nc.sbuf_tensor("masks_sb", [128, MASK_OFF[8]], BF16))
            k_bf = p2.enter_context(nc.sbuf_tensor("k_bf", [128, T], BF16))
            v_store = p2.enter_context(
                nc.sbuf_tensor("v_store", [128, NT * 130], BF16))
            wstage2 = p2.enter_context(nc.sbuf_tensor("wstage2", [128, 1040], F32))
            wk_bf = p2.enter_context(nc.sbuf_tensor("wk_bf", [128, 1024], BF16))
            wv_bf = p2.enter_context(nc.sbuf_tensor("wv_bf", [128, 1040], BF16))
            kpre = p2.enter_context(nc.sbuf_tensor("kpre", [128, 2 * 512], BF16))
            t1k = p2.enter_context(nc.sbuf_tensor("t1k", [128, 512], BF16))
            t2k = p2.enter_context(nc.sbuf_tensor("t2k", [128, 512], BF16))
            p_t = p2.enter_context(nc.sbuf_tensor("p_t", [128, 4 * 1024], BF16))
            rowbuf = p2.enter_context(nc.sbuf_tensor("rowbuf", [65, 1024], F32))
            recipbf = p2.enter_context(nc.sbuf_tensor("recipbf", [65, 1024], BF16))
            evacA = p2.enter_context(nc.sbuf_tensor("evacA", [64, 512], BF16))
            evacB = p2.enter_context(nc.sbuf_tensor("evacB", [64, 512], BF16))
            normA = p2.enter_context(nc.sbuf_tensor("normA", [64, 512], BF16))
            normB = p2.enter_context(nc.sbuf_tensor("normB", [64, 512], BF16))
            cstat = p2.enter_context(nc.sbuf_tensor("cstat", [128, 4], F32))
            negc = p2.enter_context(nc.sbuf_tensor("negc", [128, 1], F32))
            s_ps = [p2.enter_context(nc.psum_tensor(f"sps{i}", [128, 1024], F32))
                    for i in range(2)]
            o_ps = p2.enter_context(nc.psum_tensor("ops", [128, 1024], F32))
            bc_ps = [p2.enter_context(nc.psum_tensor(f"bcp{i}", [64, 512], F32))
                     for i in range(2)]

            B.wait("sync", "dve", q_done_dve)
            B.op("sync", lambda e: e.dma_start(out=brow_k[:, :], in_=bk[:, :]), "ld", 16)
            B.op("sync", lambda e: e.dma_start(out=wstage2[0:1, 0:NPAIR * 130],
                                               in_=bv[:, :]), "ld", 16)
            B.op("sync", lambda e: e.dma_start(out=rkc[:, :], in_=rk_cos[:, :]),
                 "ld", 16)
            B.op("sync", lambda e: e.dma_start(out=rks[:, :], in_=rk_sin[:, :]),
                 "ld", 16)
            B.op("sync", lambda e: e.dma_start(out=masks[:, :], in_=masks_in[:, :]),
                 "ld", 16)
            ph2_ld = B.cnt["ld"]
            B.wait("vector", "ld", ph2_ld)
            B.wait("tensor", "ld", ph2_ld)
            B.op("vector", lambda e: e.tensor_copy(out=brow_kb[:, :], in_=brow_k[:, :]))
            bkb_dve = B.op("vector", lambda e: e.tensor_copy(
                out=brow_vb[:, :], in_=wstage2[0:1, 0:NPAIR * 130]), "dve")
            B.wait("tensor", "dve", bkb_dve)
            B.wait("sync", "dve", bkb_dve)

            jglob = 0
            sfree_mark = {}
            ptfree_mark = {}
            evac_mark = 0
            place_prev = 0
            v_done_act = 0

            wfcst = {"ready": 0, "next": 0}

            def emit_wfc_group(n8):
                for k in range(8):
                    n = n8 * 8 + k
                    piece, dc = n // 8, n % 8
                    B.wait("sync", "act", wfcst["ready"])
                    B.op("sync", lambda e, piece=piece, dc=dc: e.dma_start(
                        out=wstage2[:, 0:1024],
                        in_=wfcT_in[dc * 128:(dc + 1) * 128,
                                    piece * 1024:(piece + 1) * 1024]), "ld", 16)
                    B.wait("scalar", "ld", B.cnt["ld"])
                    wfcst["ready"] = B.op("scalar",
                        lambda e, piece=piece, dc=dc: e.activation(
                            xnf[:, dc * FF + piece * 1024:
                                dc * FF + (piece + 1) * 1024],
                            wstage2[:, 0:1024], AF.Copy), "act")

            def emit_norm(p, qc):
                nonlocal evac_mark, place_prev
                last_pv = B.cnt["pe"]
                B.wait("scalar", "pe", last_pv)
                B.op("scalar", lambda e: e.activation(
                    rowbuf[64:65, 0:512], o_ps[64:65, 0:512], AF.Ln))
                B.op("scalar", lambda e: e.activation(
                    rowbuf[64:65, 512:1024], o_ps[64:65, 512:1024], AF.Ln))
                B.op("scalar", lambda e: e.activation(
                    recipbf[64:65, 0:512], rowbuf[64:65, 0:512], AF.Exp,
                    scale=-1.0))
                rows_act = B.op("scalar", lambda e: e.activation(
                    recipbf[64:65, 512:1024], rowbuf[64:65, 512:1024], AF.Exp,
                    scale=-1.0), "act")
                B.wait("tensor", "act", rows_act)
                B.wait("tensor", "dve", evac_mark)
                B.op("tensor", lambda e: e.matmul(
                    bc_ps[0][0:64, :], ones_bf[64:65, 0:64],
                    recipbf[64:65, 0:512], start=True, stop=True))
                bc_pe = B.op("tensor", lambda e: e.matmul(
                    bc_ps[1][0:64, :], ones_bf[64:65, 0:64],
                    recipbf[64:65, 512:1024], start=True, stop=True), "pe")
                B.wait("vector", "pe", bc_pe)
                B.wait("vector", "g", place_prev)
                B.op("vector", lambda e: e.tensor_copy(
                    out=evacA[:, :], in_=o_ps[0:64, 0:512]))
                B.op("vector", lambda e: e.tensor_copy(
                    out=evacB[:, :], in_=o_ps[0:64, 512:1024]))
                B.op("vector", lambda e: e.tensor_tensor(
                    normA[:, :], evacA[:, :], bc_ps[0][0:64, :], ALU.mult))
                evac_mark = B.op("vector", lambda e: e.tensor_tensor(
                    normB[:, :], evacB[:, :], bc_ps[1][0:64, :], ALU.mult),
                    "dve")
                B.wait("gpsimd", "dve", evac_mark)
                B.op("gpsimd", lambda e, p=p, qc=qc: e.dma_start(
                    out=out_norm[0:64, p * TOWN + qc * 512:
                                 p * TOWN + (qc + 1) * 512],
                    in_=normA[:, :]), "g", 16)
                place_prev = B.op("gpsimd", lambda e, p=p, qc=qc: e.dma_start(
                    out=out_norm[64:128, p * TOWN + qc * 512:
                                 p * TOWN + (qc + 1) * 512],
                    in_=normB[:, :]), "g", 16)
                if p == NPAIR - 1:
                    if qc == 0:
                        B.wait("sync", "pool", B.cnt["pool"])
                    emit_wfc_group(qc)

            for p in range(NPAIR):
                B.wait("sync", "pool", B.cnt["pool"])
                B.op("sync", lambda e, p=p: e.dma_start(
                    out=wstage2[:, 0:1024].rearrange("q (dc f) -> q dc f", dc=8),
                    in_=wkT[:, p * 128:(p + 1) * 128]
                    .rearrange("(dc q) f -> q dc f", q=128)), "ld", 16)
                B.wait("gpsimd", "ld", B.cnt["ld"])
                B.wait("gpsimd", "pe", B.cnt["pe"])
                wk_ready = B.op("gpsimd", lambda e: e.tensor_copy(
                    out=wk_bf[:, :], in_=wstage2[:, 0:1024]), "pool")
                B.wait("sync", "pool", wk_ready)
                B.op("sync", lambda e, p=p: e.dma_start(
                    out=wstage2[:, 0:1040].rearrange("q (dc f) -> q dc f", dc=8),
                    in_=wvT[:, p * 130:(p + 1) * 130]
                    .rearrange("(dc q) f -> q dc f", q=128)), "ld", 16)
                B.wait("gpsimd", "ld", B.cnt["ld"])
                B.wait("gpsimd", "pe", B.cnt["pe"])
                wv_ready = B.op("gpsimd", lambda e: e.tensor_copy(
                    out=wv_bf[:, :], in_=wstage2[:, 0:1040]), "pool")

                prev_attn_pe = B.cnt["pe"]
                kproj_free = dict(sfree_mark)
                kpre_free = {}
                kperm_free = {0: evac_mark, 1: evac_mark}
                B.wait("tensor", "pool", wk_ready)
                B.wait("vector", "pe", prev_attn_pe)
                for ch in range(8):
                    par = ch % 2
                    bank = s_ps[par]
                    B.wait("tensor", "act", kproj_free.get(par, 0))
                    for dc in range(8):
                        B.op("tensor", lambda e, bank=bank, dc=dc, ch=ch: e.matmul(
                            bank[:, 0:512],
                            wk_bf[:, dc * 128:(dc + 1) * 128],
                            xnf[:, dc * T + ch * 512: dc * T + (ch + 1) * 512],
                            start=(dc == 0), stop=False))
                    pe_kproj = B.op("tensor", lambda e, bank=bank, p=p: e.matmul(
                        bank[:, 0:512], brow_kb[0:1, p * 128:(p + 1) * 128],
                        ones_bf[0:1, 0:512], start=False, stop=True), "pe")
                    B.wait("scalar", "pe", pe_kproj)
                    B.wait("scalar", "dve", kpre_free.get(par, 0))
                    kproj_free[par] = B.op("scalar",
                        lambda e, bank=bank, par=par: e.activation(
                            kpre[:, par * 512:(par + 1) * 512], bank[:, 0:512],
                            AF.Copy), "act")
                    B.wait("tensor", "dve", kperm_free.get(par, 0))
                    kpre_free[par] = kperm_free[par] = emit_rope(
                        kpre[:, par * 512:(par + 1) * 512],
                        o_ps[:, par * 512:(par + 1) * 512],
                        rkc[:, ch * 512:(ch + 1) * 512],
                        rks[:, ch * 512:(ch + 1) * 512],
                        k_bf[:, ch * 512:(ch + 1) * 512],
                        t1k, t2k, kproj_free[par])
                kv_rope_dve = B.cnt["dve"]

                vproj_free = dict(kproj_free)
                B.wait("tensor", "pool", wv_ready)
                B.wait("scalar", "pe", prev_attn_pe)
                for tt in range(NT):
                    par = tt % 2
                    bank = s_ps[par]
                    B.wait("tensor", "act", vproj_free.get(par, 0))
                    for dc in range(8):
                        B.op("tensor", lambda e, bank=bank, dc=dc, tt=tt: e.matmul(
                            bank[:, 0:130],
                            xnf[:, dc * T + tt * 128: dc * T + (tt + 1) * 128],
                            wv_bf[:, dc * 130:(dc + 1) * 130],
                            start=(dc == 0), stop=False))
                    pe_v = B.op("tensor", lambda e, bank=bank, p=p: e.matmul(
                        bank[:, 0:130], ones_bf[0:1, 0:128],
                        brow_vb[0:1, p * 130:(p + 1) * 130], start=False,
                        stop=True), "pe")
                    B.wait("scalar", "pe", pe_v)
                    vproj_free[par] = B.op("scalar",
                        lambda e, bank=bank, tt=tt: e.activation(
                            v_store[:, tt * 130:(tt + 1) * 130], bank[:, 0:130],
                            AF.Copy), "act")
                v_done_act = B.cnt["act"]

                B.wait("vector", "dve", kv_rope_dve)
                B.op("vector", lambda e, p=p: e.reduce_sum(
                    cstat[:, 0:1], q_all[:, p * TOWN:(p + 1) * TOWN:16], AX.X))
                cs_dve = B.op("vector", lambda e: e.reduce_sum(
                    cstat[:, 1:2], k_bf[:, ::32], AX.X), "dve")
                B.wait("tensor", "dve", max(cs_dve, evac_mark))
                cpe = B.op("tensor", lambda e: e.matmul(
                    bc_ps[0][0:1, 0:1], cstat[:, 1:2], cstat[:, 0:1],
                    start=True, stop=True), "pe")
                B.wait("vector", "pe", cpe)
                cdve = B.op("vector", lambda e: e.tensor_copy(
                    out=cstat[0:1, 3:4], in_=bc_ps[0][0:1, 0:1]), "dve")
                B.wait("tensor", "dve", cdve)
                B.wait("tensor", "act", v_done_act)
                cpe2 = B.op("tensor", lambda e: e.matmul(
                    s_ps[0][:, 0:1], onesb[0:1, 0:128], cstat[0:1, 3:4],
                    start=True, stop=True), "pe")
                B.wait("vector", "pe", cpe2)
                negc_dve = B.op("vector", lambda e: e.tensor_scalar(
                    negc[:, 0:1], s_ps[0][:, 0:1], -0.5 / (128.0 * 128.0), None,
                    ALU.mult), "dve")
                B.wait("scalar", "dve", negc_dve)

                B.wait("tensor", "dve", negc_dve)
                steps = [(qc, j) for qc in range(4) for j in range(8 * qc + 8)]
                pending = []

                def emit_pv(ctx):
                    qc, j, pt3, c0, gsem, gval = ctx
                    nsteps = 8 * qc + 8
                    B.wait("tensor", gsem, gval)
                    if j == 0:
                        B.wait("tensor", "dve", evac_mark)
                    B.op("tensor", lambda e, j=j, pt3=pt3, c0=c0, n=nsteps:
                        e.matmul(
                            o_ps[0:65, c0:512],
                            v_store[:, j * 130: j * 130 + 65],
                            p_t[:, pt3 * 1024 + c0: pt3 * 1024 + 512],
                            start=(j == 0), stop=(j == n - 1)))
                    pv_pe = B.op("tensor",
                        lambda e, j=j, pt3=pt3, c0=c0, n=nsteps: e.matmul(
                            o_ps[0:65, 512 + c0:1024],
                            v_store[:, j * 130 + 65: j * 130 + 130],
                            p_t[:, pt3 * 1024 + 512 + c0: pt3 * 1024 + 1024],
                            start=(j == 0), stop=(j == n - 1)), "pe")
                    ptfree_mark[pt3] = pv_pe

                def drain_one():
                    ctx = pending.pop(0)
                    emit_pv(ctx)
                    if ctx[1] == 8 * ctx[0] + 7:
                        emit_norm(p, ctx[0])

                for (qc, j) in steps:
                    r = j - 8 * qc
                    c0 = COL0[r] if r >= 0 else 0
                    qbase = p * TOWN + qc * 512
                    par = jglob % 2
                    pt3 = jglob % 4
                    bank = s_ps[par]
                    B.wait("tensor", "act", sfree_mark.get(par, 0))
                    B.op("tensor",
                         lambda e, bank=bank, j=j, c0=c0, qbase=qbase, r=r: e.matmul(
                             bank[:, c0:512],
                             k_bf[0:64, j * 128:(j + 1) * 128],
                             q_all[0:64, qbase + c0: qbase + 512],
                             start=True, stop=(r < 0)))
                    qk_pe = B.op("tensor",
                         lambda e, bank=bank, j=j, c0=c0, qbase=qbase, r=r: e.matmul(
                             bank[:, 512 + c0:1024],
                             k_bf[64:128, j * 128:(j + 1) * 128],
                             q_all[64:128, qbase + c0: qbase + 512],
                             start=True, stop=(r < 0)), "pe" if r < 0 else None)
                    if r >= 0:
                        B.op("tensor", lambda e, bank=bank, c0=c0, r=r: e.matmul(
                            bank[:, c0:512], ident[:, :],
                            masks[:, MASK_OFF[r]:MASK_OFF[r + 1]],
                            start=False, stop=True))
                        qk_pe = B.op("tensor",
                            lambda e, bank=bank, c0=c0, r=r: e.matmul(
                                bank[:, 512 + c0:1024], ident[:, :],
                                masks[:, MASK_OFF[r]:MASK_OFF[r + 1]],
                                start=False, stop=True), "pe")
                    if p == NPAIR - 1 and qc == 3 and j == 31:
                        wfcst["qk_last"] = qk_pe
                    B.wait("scalar", "pe", qk_pe)
                    B.wait("scalar", "pe", ptfree_mark.get(pt3, 0))
                    if r < 0:
                        em = B.op("scalar", lambda e, bank=bank, pt3=pt3:
                            e.activation(
                                p_t[:, pt3 * 1024:(pt3 + 1) * 1024],
                                bank[:, :], AF.Exp, bias=negc[:, 0:1]), "act")
                    else:
                        B.op("scalar", lambda e, bank=bank, pt3=pt3, c0=c0:
                            e.activation(
                                p_t[:, pt3 * 1024 + c0: pt3 * 1024 + 512],
                                bank[:, c0:512], AF.Exp, bias=negc[:, 0:1]))
                        em = B.op("scalar", lambda e, bank=bank, pt3=pt3, c0=c0:
                            e.activation(
                                p_t[:, pt3 * 1024 + 512 + c0:
                                    pt3 * 1024 + 1024],
                                bank[:, 512 + c0:1024], AF.Exp,
                                bias=negc[:, 0:1]), "act")
                    sfree_mark[par] = em
                    jglob += 1
                    if len(pending) >= 3:
                        drain_one()
                    pending.append((qc, j, pt3, c0, "act", em))
                while pending:
                    drain_one()

            attn_done_g = B.cnt["g"]
            ph2_act = B.cnt["act"]
            ph2_dve = B.cnt["dve"]
            B.flush(block, S)
        p12.close()

        wfc_bf = xnf
        wfc_ready = wfcst["ready"]
        with contextlib.ExitStack() as p3:
            wout_stage = p3.enter_context(
                nc.sbuf_tensor("wout_stage", [128, 8192], F32))
            wout_bf = p3.enter_context(nc.sbuf_tensor("wout_bf", [128, 8192], BF16))
            aev = [p3.enter_context(nc.sbuf_tensor(f"aev{i}", [128, 1024], F32))
                   for i in range(2)]
            pp = [p3.enter_context(nc.psum_tensor(f"pp{i}", [128, 1024], F32))
                  for i in range(2)]

            B.wait("sync", "pe", wfcst["qk_last"])
            B.op("sync", lambda e: e.dma_start(
                out=wout_stage[:, :].rearrange("q (dc f) -> q dc f", dc=8),
                in_=woutT_in.rearrange("(dc q) f -> q dc f", q=128)), "ld", 16)
            B.wait("gpsimd", "ld", B.cnt["ld"])
            wout_ready = B.op("gpsimd", lambda e: e.tensor_copy(
                out=wout_bf[:, :], in_=wout_stage[:, :]), "pool")

            B.wait("tensor", "pool", wout_ready)
            B.wait("tensor", "g", attn_done_g)
            B.wait("tensor", "act", ph2_act)
            B.wait("tensor", "dve", ph2_dve)
            ap_free = {}
            ap_store = {}
            for tt in range(NTO):
                par = tt % 2
                bank = pp[par]
                B.wait("tensor", "dve", ap_free.get(par, 0))
                op_pe = None
                for dc in range(8):
                    B.op("tensor", lambda e, bank=bank, dc=dc, tt=tt: e.matmul(
                        bank[:, 0:512],
                        out_norm[:, dc * TOWN + tt * 128: dc * TOWN + (tt + 1) * 128],
                        wout_bf[:, dc * 1024: dc * 1024 + 512],
                        start=(dc == 0), stop=(dc == 7)))
                    if dc < 7:
                        B.op("tensor", lambda e, bank=bank, dc=dc, tt=tt: e.matmul(
                            bank[:, 512:1024],
                            out_norm[:, dc * TOWN + tt * 128:
                                     dc * TOWN + (tt + 1) * 128],
                            wout_bf[:, dc * 1024 + 512:(dc + 1) * 1024],
                            start=(dc == 0), stop=False))
                    else:
                        op_pe = B.op("tensor",
                            lambda e, bank=bank, dc=dc, tt=tt: e.matmul(
                                bank[:, 512:1024],
                                out_norm[:, dc * TOWN + tt * 128:
                                         dc * TOWN + (tt + 1) * 128],
                                wout_bf[:, dc * 1024 + 512:(dc + 1) * 1024],
                                start=(dc == 0), stop=True), "pe")
                B.wait("vector", "pe", op_pe)
                B.wait("vector", "g", ap_store.get(par, 0))
                ap_free[par] = B.op("vector", lambda e, bank=bank, par=par:
                    e.tensor_copy(out=aev[par][:, :], in_=bank[:, :]), "dve")
                B.wait("sync", "dve", ap_free[par])
                ap_store[par] = B.op("sync", lambda e, tt=tt: e.dma_start(
                    out=aproj_hbm[tt * 128:(tt + 1) * 128, :],
                    in_=aev[tt % 2][:, :]), "g", 16)
            aproj_g = B.cnt["g"]
            ph3_dve = B.cnt["dve"]
            B.flush(block, S)
        p23.close()

        with contextlib.ExitStack() as p4:
            xn_own2 = p4.enter_context(nc.sbuf_tensor("xn_own2", [128, 8 * TOWN], BF16))
            b_bc = p4.enter_context(nc.sbuf_tensor("b_bc", [128, 1024], F32))
            brow_o = p4.enter_context(nc.sbuf_tensor("brow_o", [1, D], F32))
            with contextlib.ExitStack() as p4a:
                xstage4 = p4a.enter_context(nc.sbuf_tensor("xstage4", [128, 4096], F32))
                xdump4 = p4a.enter_context(nc.sbuf_tensor("xdump4", [128, 4 * 1024], F32))
                stat4 = p4a.enter_context(nc.sbuf_tensor("stat4", [128, 64], F32))
                xn_tm4 = p4a.enter_context(nc.sbuf_tensor("xn_tm4", [128, 4096], BF16))
                tr_ps4 = [p4a.enter_context(nc.psum_tensor(f"trq{i}", [128, 1024], BF16))
                          for i in range(2)]
                bb_ps = p4a.enter_context(nc.psum_tensor("bb_ps", [128, 1024], F32))

                B.wait("tensor", "dve", ph3_dve)
                B.wait("sync", "pool", wout_ready)
                B.op("sync", lambda e: e.dma_start(out=brow_o[:, :], in_=b_out[:, :]),
                     "ld", 16)
                bo_ld = B.cnt["ld"]
                emit_ln(x_own, NTO, xn_own2, TOWN, xstage4, stat4, xdump4, xn_tm4,
                        tr_ps4)
                xn2_dve = B.cnt["dve"]

                B.wait("tensor", "ld", bo_ld)
                B.op("tensor", lambda e: e.matmul(
                    bb_ps[:, 0:512], onesb[0:1, 0:128], brow_o[0:1, 0:512],
                    start=True, stop=True))
                bb_pe = B.op("tensor", lambda e: e.matmul(
                    bb_ps[:, 512:1024], onesb[0:1, 0:128], brow_o[0:1, 512:1024],
                    start=True, stop=True), "pe")
                B.wait("vector", "pe", bb_pe)
                bb_dve = B.op("vector", lambda e: e.tensor_copy(
                    out=b_bc[:, :], in_=bb_ps[:, :]), "dve")
                B.flush(block, S)

            h_store = p4.enter_context(nc.sbuf_tensor("h_store", [128, 32 * 512], BF16))
            ztmp = p4.enter_context(nc.sbuf_tensor("ztmp", [128, 2 * 512], BF16))
            wpstage = [p4.enter_context(nc.sbuf_tensor(f"wpst{i}", [128, 1024], F32))
                       for i in range(2)]
            wproj_bf = [p4.enter_context(nc.sbuf_tensor(f"wpbf{i}", [128, 1024], BF16))
                        for i in range(2)]
            ybuf = p4.enter_context(nc.sbuf_tensor("ybuf", [128, 3 * 1024], F32))
            smem = [p4.enter_context(nc.sbuf_tensor(f"smem{i}", [128, 1024], F32))
                    for i in range(2)]
            stmp = [p4.enter_context(nc.sbuf_tensor(f"stmp{i}", [128, 1024], F32))
                    for i in range(2)]
            sacc = [p4.enter_context(nc.sbuf_tensor(f"sacc{i}", [128, 1024], F32))
                    for i in range(2)]
            xo_t = [p4.enter_context(nc.sbuf_tensor(f"xo_t{i}", [128, 1024], F32))
                    for i in range(2)]
            ap_t = [p4.enter_context(nc.sbuf_tensor(f"ap_t{i}", [128, 1024], F32))
                    for i in range(2)]
            o_stage = stmp

            wp_consumed = {}
            wp_ready = {}
            out_store = {}
            fin_mark = {0: 0, 1: 0}
            rel4 = {}
            y_evac_act = 0
            y_pe_prev = 0
            for tc in range(4):
                with contextlib.ExitStack() as pth:
                    hps = [pth.enter_context(
                        nc.psum_tensor(f"hps{tc}_{i}", [128, 512], F32))
                        for i in range(2)]
                    if tc == 0:
                        B.wait("tensor", "act", wfc_ready)
                        B.wait("tensor", "dve", max(xn2_dve, bb_dve))
                    else:
                        B.wait("tensor", "act", y_evac_act)
                        B.wait("scalar", "pe", y_pe_prev)
                    hz_free = {}
                    hsq_done = {}
                    for fc in range(32):
                        par = fc % 2
                        bank = hps[par]
                        B.wait("tensor", "act", hz_free.get(par, 0))
                        hmm = None
                        for dc in range(8):
                            if dc < 7:
                                B.op("tensor",
                                    lambda e, bank=bank, dc=dc, fc=fc, tc=tc: e.matmul(
                                        bank[:, :],
                                        wfc_bf[:, dc * FF + fc * 128:
                                               dc * FF + (fc + 1) * 128],
                                        xn_own2[:, dc * TOWN + tc * 512:
                                                dc * TOWN + (tc + 1) * 512],
                                        start=(dc == 0), stop=False))
                            else:
                                hmm = B.op("tensor",
                                    lambda e, bank=bank, dc=dc, fc=fc, tc=tc: e.matmul(
                                        bank[:, :],
                                        wfc_bf[:, dc * FF + fc * 128:
                                               dc * FF + (fc + 1) * 128],
                                        xn_own2[:, dc * TOWN + tc * 512:
                                                dc * TOWN + (tc + 1) * 512],
                                        start=False, stop=True), "pe")
                        B.wait("scalar", "pe", hmm)
                        hz = B.op("scalar",
                            lambda e, bank=bank, par=par: e.activation(
                                ztmp[:, par * 512:(par + 1) * 512], bank[:, :],
                                AF.Relu), "act", drain=False)
                        B.wait("scalar", "act", hz)
                        hz_free[par] = hsq_done[par] = B.op("scalar",
                            lambda e, fc=fc, par=par: e.activation(
                                h_store[:, fc * 512:(fc + 1) * 512],
                                ztmp[:, par * 512:(par + 1) * 512],
                                AF.Square), "act")
                    h_dve = B.cnt["act"]
                    B.flush(block, S)
                with contextlib.ExitStack() as pty:
                    yps = [pty.enter_context(
                        nc.psum_tensor(f"yps{tc}_{i}", [128, 1024], F32))
                        for i in range(4)]

                    for weng in ("tensor", "scalar", "vector", "gpsimd", "sync"):
                        for wsem in ("pe", "act", "dve", "pool"):
                            if wsem == "dve" and weng in ("tensor", "scalar"):
                                continue
                            B.wait(weng, wsem, B.cnt[wsem])
                    B.wait("tensor", "act", h_dve)
                    for fc in range(32):
                        par = fc % 2
                        B.wait("sync", "pe", wp_consumed.get(par, 0))
                        B.op("sync", lambda e, fc=fc: e.dma_start(
                            out=wpstage[fc % 2][:, :],
                            in_=wprojT_in[fc * 128:(fc + 1) * 128, :]), "ld", 16)
                        B.wait("scalar", "ld", B.cnt["ld"])
                        B.wait("scalar", "pe", wp_consumed.get(par, 0))
                        wp_ready[par] = B.op("scalar", lambda e, par=par: e.activation(
                            wproj_bf[par][:, :], wpstage[par][:, :], AF.Copy), "act")
                        B.wait("tensor", "act", wp_ready[par])
                        last = None
                        for ti in range(4):
                            B.op("tensor", lambda e, fc=fc, ti=ti, par=par: e.matmul(
                                yps[ti][:, 0:512],
                                h_store[:, fc * 512 + ti * 128:
                                        fc * 512 + (ti + 1) * 128],
                                wproj_bf[par][:, 0:512],
                                start=(fc == 0), stop=(fc == 31)))
                            if ti < 3:
                                B.op("tensor", lambda e, fc=fc, ti=ti, par=par:
                                    e.matmul(
                                        yps[ti][:, 512:1024],
                                        h_store[:, fc * 512 + ti * 128:
                                                fc * 512 + (ti + 1) * 128],
                                        wproj_bf[par][:, 512:1024],
                                        start=(fc == 0), stop=(fc == 31)))
                            else:
                                last = B.op("tensor", lambda e, fc=fc, ti=ti, par=par:
                                    e.matmul(
                                        yps[ti][:, 512:1024],
                                        h_store[:, fc * 512 + ti * 128:
                                                fc * 512 + (ti + 1) * 128],
                                        wproj_bf[par][:, 512:1024],
                                        start=(fc == 0), stop=(fc == 31)), "pe")
                        wp_consumed[par] = last
                    y_pe = B.cnt["pe"]
                    y_pe_prev = y_pe

                    for weng in ("tensor", "scalar", "vector", "gpsimd", "sync"):
                        for wsem in ("pe", "act", "dve", "pool"):
                            if wsem == "dve" and weng in ("tensor", "scalar"):
                                continue
                            B.wait(weng, wsem, B.cnt[wsem])
                    for ti in range(4):
                        tt = tc * 4 + ti
                        par = ti % 2
                        eng = "vector"
                        esem = "dve"
                        slot = ti % 3
                        B.wait("scalar", "pe", y_pe)
                        B.wait("scalar", esem, rel4.get(slot, 0))
                        yv = B.op("scalar", lambda e, slot=slot, ti=ti:
                            e.activation(
                                ybuf[:, slot * 1024:(slot + 1) * 1024],
                                yps[ti][:, :], AF.Copy), "act")
                        y_evac_act = yv
                        B.wait("sync", "g", aproj_g)
                        B.wait("sync", esem, fin_mark[par])
                        B.op("sync", lambda e, tt=tt, par=par: e.dma_start(
                            out=xo_t[par][:, :],
                            in_=x_own[tt * 128:(tt + 1) * 128, :]), "ld", 16)
                        B.op("sync", lambda e, tt=tt, par=par: e.dma_start(
                            out=ap_t[par][:, :],
                            in_=aproj_hbm[tt * 128:(tt + 1) * 128, :]), "ld", 16)
                        fin_ld = B.cnt["ld"]
                        B.wait(eng, "act", yv)
                        B.wait(eng, "dve", bb_dve)
                        sm, st_, sa = smem[par], stmp[par], sacc[par]

                        yoff = (ti % 3) * 1024

                        def sop(fn, eng=eng, esem=esem, fence=False):
                            B.op(eng, lambda e, f=fn: f(e, 0, 512),
                                 esem, drain=False)
                            t = B.op(eng, lambda e, f=fn: f(e, 512, 1024),
                                     esem, drain=False)
                            if fence:
                                B.wait(eng, esem, t)
                            return t
                        sop(lambda e, lo, hi, yoff=yoff, sa=sa: e.tensor_scalar(
                            sa[:, lo:hi], ybuf[:, yoff + lo:yoff + hi],
                            1.0, None, ALU.is_gt))
                        sop(lambda e, lo, hi, yoff=yoff, st_=st_:
                            e.scalar_tensor_tensor(
                                st_[:, lo:hi], ybuf[:, yoff + lo:yoff + hi], 1.0,
                                ybuf[:, yoff + lo:yoff + hi],
                                ALU.is_gt, ALU.subtract))
                        sop(lambda e, lo, hi, yoff=yoff, sm=sm, st_=st_:
                            e.scalar_tensor_tensor(
                                sm[:, lo:hi], ybuf[:, yoff + lo:yoff + hi], 0.9,
                                st_[:, lo:hi], ALU.mult, ALU.subtract))
                        sop(lambda e, lo, hi, sm=sm, sa=sa:
                            e.scalar_tensor_tensor(
                                sa[:, lo:hi], sm[:, lo:hi], 1.0, sa[:, lo:hi],
                                ALU.is_gt, ALU.add))
                        for _ in range(3):
                            sop(lambda e, lo, hi, yoff=yoff, sm=sm, st_=st_:
                                e.scalar_tensor_tensor(
                                    st_[:, lo:hi], sm[:, lo:hi], 1.0,
                                    ybuf[:, yoff + lo:yoff + hi],
                                    ALU.is_gt, ALU.subtract))
                            sop(lambda e, lo, hi, sm=sm, st_=st_:
                                e.scalar_tensor_tensor(
                                    sm[:, lo:hi], sm[:, lo:hi], 0.9, st_[:, lo:hi],
                                    ALU.mult, ALU.subtract))
                            sop(lambda e, lo, hi, sm=sm, sa=sa:
                                e.scalar_tensor_tensor(
                                    sa[:, lo:hi], sm[:, lo:hi], 1.0, sa[:, lo:hi],
                                    ALU.is_gt, ALU.add))
                        B.wait(eng, "ld", fin_ld)
                        B.wait(eng, "st", out_store.get(par, 0))
                        sop(lambda e, lo, hi, sa=sa, par=par:
                            e.scalar_tensor_tensor(
                                o_stage[par][:, lo:hi], sa[:, lo:hi], 0.2,
                                ap_t[par][:, lo:hi], ALU.mult, ALU.add))
                        sop(lambda e, lo, hi, par=par: e.tensor_tensor(
                            o_stage[par][:, lo:hi], o_stage[par][:, lo:hi],
                            xo_t[par][:, lo:hi], ALU.add))
                        fin_mark[par] = sop(lambda e, lo, hi, par=par:
                            e.tensor_tensor(
                                o_stage[par][:, lo:hi], o_stage[par][:, lo:hi],
                                b_bc[:, lo:hi], ALU.add), fence=True)
                        rel4[ti % 3] = fin_mark[par]
                        B.wait("sync", esem, fin_mark[par])
                        out_store[par] = B.op("sync", lambda e, tt=tt, par=par:
                            e.dma_start(
                                out=out_hbm[tt * 128:(tt + 1) * 128, :],
                                in_=o_stage[par][:, :]), "st", 16)

                    B.flush(block, S)

            B.wait("sync", "st", B.cnt["st"])
            B.flush(block, S)

    return nc


def _rope_tables(positions, scale):
    inv = 1.0 / (10000.0 ** (np.arange(0, 64, 2, dtype=np.float64) / 64.0))
    fr = positions.astype(np.float64)[None, :] * inv[:, None]
    c = np.cos(fr) * scale
    s = np.sin(fr) * scale
    bf = ml_dtypes.bfloat16
    cos_t = np.concatenate([c, c, c, c], axis=0).astype(np.float32)
    sin_t = np.concatenate([-s, s, -s, s], axis=0).astype(np.float32)
    return cos_t.astype(bf), sin_t.astype(bf)


def _build_masks(g):
    bf = ml_dtypes.bfloat16
    m = np.zeros((128, MASK_OFF[8]), dtype=np.float32)
    tri = (np.arange(128)[None, :] >= np.arange(128)[:, None]).astype(np.float32)
    for r in range(8):
        w = 512 - COL0[r]
        tile = np.zeros((128, w), dtype=np.float32)
        for col in range(w):
            gc = COL0[r] + col
            i = gc // 128
            if r < 2 * i + g:
                tile[:, col] = 1.0
            elif r == 2 * i + g:
                tile[:, col] = tri[:, gc % 128]
        m[:, MASK_OFF[r]:MASK_OFF[r + 1]] = tile
    return np.ascontiguousarray(((m - 1.0) * 100.0).astype(bf))


_NC_CACHE = []


def kernel(L0_attn_x, L0_attn_w_qkv, L0_attn_b_qkv, L0_attn_w_out, L0_attn_b_out,
           L0_ffn_W_fc, L0_ffn_W_proj):
    x = np.ascontiguousarray(np.asarray(L0_attn_x, dtype=np.float32))
    w_qkv = np.asarray(L0_attn_w_qkv, dtype=np.float32)
    b_qkv = np.asarray(L0_attn_b_qkv, dtype=np.float32)
    w_out = np.asarray(L0_attn_w_out, dtype=np.float32)
    b_out = np.asarray(L0_attn_b_out, dtype=np.float32)
    W_fc = np.asarray(L0_ffn_W_fc, dtype=np.float32)
    W_proj = np.asarray(L0_ffn_W_proj, dtype=np.float32)
    bf = ml_dtypes.bfloat16

    perm64 = np.concatenate([np.arange(0, 64, 2), np.arange(1, 64, 2)])
    qk_rows = np.concatenate([64 * h + perm64 for h in range(16)])
    wqT = np.ascontiguousarray(w_qkv[qk_rows, :].T)
    wkT = np.ascontiguousarray(w_qkv[1024 + qk_rows, :].T)
    bq_row = np.ascontiguousarray(b_qkv[qk_rows][None, :])
    bk_row = np.ascontiguousarray(b_qkv[1024 + qk_rows][None, :])
    wvT_full = w_qkv[2048:3072, :].T
    wvT = np.zeros((D, NPAIR * 130), dtype=np.float32)
    bv_row = np.zeros((1, NPAIR * 130), dtype=np.float32)
    bv_full = b_qkv[2048:3072]
    for p in range(NPAIR):
        wvT[:, p * 130: p * 130 + 64] = wvT_full[:, p * 128: p * 128 + 64]
        wvT[:, p * 130 + 65: p * 130 + 129] = wvT_full[:, p * 128 + 64:(p + 1) * 128]
        bv_row[0, p * 130: p * 130 + 64] = bv_full[p * 128: p * 128 + 64]
        bv_row[0, p * 130 + 64] = 1.0
        bv_row[0, p * 130 + 65: p * 130 + 129] = bv_full[p * 128 + 64:(p + 1) * 128]
        bv_row[0, p * 130 + 129] = 1.0
    woutT = np.ascontiguousarray(w_out.T)
    wfcT = np.ascontiguousarray(W_fc.T)
    wprojT = np.ascontiguousarray(W_proj.T)

    ident = np.eye(128, dtype=np.float32).astype(bf)
    sigma = np.concatenate([np.arange(32, 64), np.arange(0, 32),
                            np.arange(96, 128), np.arange(64, 96)])
    permM = np.zeros((128, 128), dtype=np.float32)
    permM[sigma, np.arange(128)] = 1.0
    permM = permM.astype(bf)

    rk_cos, rk_sin = _rope_tables(np.arange(T), 1.0)

    if not _NC_CACHE:
        _NC_CACHE.append(build_program())
    nc = _NC_CACHE[0]

    in_maps = []
    own_idx_all = []
    for c in range(8):
        b, g = c // 2, c % 2
        own_idx = np.concatenate(
            [np.arange(128 * (2 * j + g), 128 * (2 * j + g) + 128)
             for j in range(NTO)])
        own_idx_all.append(own_idx)
        xb = x[b]
        q_cos, q_sin = _rope_tables(own_idx, 0.125)
        in_maps.append({
            "x_seq": xb,
            "x_own": np.ascontiguousarray(xb[own_idx]),
            "wqT": wqT, "wkT": wkT, "wvT": wvT,
            "bq": bq_row, "bk": bk_row, "bv": bv_row,
            "woutT": woutT, "b_out": np.ascontiguousarray(b_out[None, :]),
            "wfcT": wfcT, "wprojT": wprojT,
            "ropeq_cos": q_cos, "ropeq_sin": q_sin,
            "ropek_cos": rk_cos, "ropek_sin": rk_sin,
            "masks": _build_masks(g),
            "ident": ident, "permT": permM,
        })

    res = run_bass_kernel_spmd(nc, in_maps, core_ids=list(range(8)))

    out = np.empty((4, T, D), dtype=np.float32)
    for c in range(8):
        b = c // 2
        out[b, own_idx_all[c], :] = np.asarray(res.results[c]["out"],
                                               dtype=np.float32)
    return out


# version 39 (final)
# speedup vs baseline: 1.2201x; 1.0031x over previous
import contextlib
import numpy as np
import ml_dtypes

import concourse.bass as bass
import concourse.mybir as mybir
from concourse.bass_utils import run_bass_kernel_spmd

F32 = mybir.dt.float32
BF16 = mybir.dt.bfloat16
AF = mybir.ActivationFunctionType
ALU = mybir.AluOpType
AX = mybir.AxisListType

D = 1024
T = 4096
TOWN = 2048
NPAIR = 8
FF = 4096
NT = T // 128
NTO = TOWN // 128
EPS = 1e-5

ENGINES = ("sync", "vector", "scalar", "tensor", "gpsimd")
COL0 = [0, 0, 128, 128, 256, 256, 384, 384]
MASK_OFF = np.cumsum([0] + [512 - c for c in COL0]).tolist()


class Builder:
    def __init__(self):
        self.ops = {e: [] for e in ENGINES}
        self.cnt = {s: 0 for s in ("ld", "g", "st", "pe", "act", "dve", "pool")}

    def inc(self, sem, n):
        self.cnt[sem] += n
        return self.cnt[sem]

    def wait(self, eng, sem, val):
        if val and val > 0:
            self.ops[eng].append(lambda e, S, s=sem, v=val: e.wait_ge(S[s], v))

    def op(self, eng, fn, inc_sem=None, inc_n=1, drain=None):
        if inc_sem is None:
            self.ops[eng].append(lambda e, S, f=fn: f(e))
            return None
        if drain is None:
            drain = inc_sem == "pool"
        tgt = self.inc(inc_sem, inc_n)
        if drain:
            self.ops[eng].append(lambda e, S, f=fn: f(e))
            self.ops[eng].append(
                lambda e, S, s=inc_sem, n=inc_n: e.drain().then_inc(S[s], n))
        else:
            self.ops[eng].append(
                lambda e, S, f=fn, s=inc_sem, n=inc_n: f(e).then_inc(S[s], n))
        return tgt

    def flush(self, block, S):
        for eng in ENGINES:
            ops = self.ops[eng]
            if not ops:
                continue
            self.ops[eng] = []

            def body(e, ops=ops):
                for fn in ops:
                    fn(e, S)
            getattr(block, eng)(body)


def build_program():
    nc = bass.Bass("TRN2", target_bir_lowering=False)

    def din(name, shape, dt=F32):
        return nc.dram_tensor(name, shape, dt, kind="ExternalInput").ap()

    x_seq = din("x_seq", [T, D])
    x_own = din("x_own", [TOWN, D])
    wqT = din("wqT", [D, D])
    wkT = din("wkT", [D, D])
    wvT = din("wvT", [D, NPAIR * 130])
    bq = din("bq", [1, D])
    bk = din("bk", [1, D])
    bv = din("bv", [1, NPAIR * 130])
    woutT_in = din("woutT", [D, D])
    b_out = din("b_out", [1, D])
    wfcT_in = din("wfcT", [D, FF])
    wprojT_in = din("wprojT", [FF, D])
    rq_cos = din("ropeq_cos", [128, TOWN], BF16)
    rq_sin = din("ropeq_sin", [128, TOWN], BF16)
    rk_cos = din("ropek_cos", [128, T], BF16)
    rk_sin = din("ropek_sin", [128, T], BF16)
    masks_in = din("masks", [128, MASK_OFF[8]], BF16)
    ident_in = din("ident", [128, 128], BF16)
    perm_in = din("permT", [128, 128], BF16)

    out_hbm = nc.dram_tensor("out", [TOWN, D], F32, kind="ExternalOutput").ap()
    aproj_hbm = nc.dram_tensor("aproj_scratch", [TOWN, D], F32).ap()

    B = Builder()

    with contextlib.ExitStack() as top:
        S = {}
        for s in ("ld", "g", "st", "pe", "act", "dve", "pool"):
            S[s] = top.enter_context(nc.semaphore(s))
        block = top.enter_context(nc.Block())

        onesb = top.enter_context(nc.sbuf_tensor("onesb", [65, 128], F32, side="right"))
        ones_bf = top.enter_context(nc.sbuf_tensor("ones_bf", [65, 512], BF16, side="right"))
        ident = top.enter_context(nc.sbuf_tensor("ident_sb", [128, 128], BF16, side="right"))
        permT = top.enter_context(nc.sbuf_tensor("permT_sb", [128, 128], BF16, side="right"))
        epscol = top.enter_context(nc.sbuf_tensor("epscol", [128, 1], F32, side="right"))

        B.op("gpsimd", lambda e: e.memset(onesb[:, :], 1.0), "pool")
        B.op("gpsimd", lambda e: e.memset(ones_bf[:, :], 1.0), "pool")
        B.op("gpsimd", lambda e: e.memset(epscol[:, :], EPS), "pool")
        for dst, src in ((ident, ident_in), (permT, perm_in)):
            B.op("sync", lambda e, d=dst, s=src: e.dma_start(out=d[:, :], in_=s[:, :]),
                 "ld", 16)
        consts_ld = B.cnt["ld"]
        consts_pool = B.cnt["pool"]
        B.wait("scalar", "pool", consts_pool)

        xnf = top.enter_context(nc.sbuf_tensor("xnf", [128, 8 * T], BF16))
        p12 = contextlib.ExitStack()
        q_all = p12.enter_context(nc.sbuf_tensor("q_all", [128, NPAIR * TOWN], BF16))

        def emit_ln(src_hbm, ntiles, dst_fm, dst_cols, xstage, stat, xdump,
                    xn_tm, tr_ps, hook=None):
            nslots = 4
            dve0 = B.cnt["dve"]
            slot_free = {}
            tr_free = {}
            red2_prev = 0
            for t0 in range(0, ntiles, 4):
                nb = min(4, ntiles - t0)
                sls = [(t0 + i) % nslots for i in range(nb)]
                for i in range(nb):
                    t = t0 + i
                    B.wait("sync", "dve", slot_free.get(sls[i], dve0))
                    B.op("sync", lambda e, slot=sls[i], t=t: e.dma_start(
                        out=xstage[:, slot * 1024:(slot + 1) * 1024],
                        in_=src_hbm[t * 128:(t + 1) * 128, :]), "ld", 16)
                ld_done = B.cnt["ld"]
                B.wait("vector", "ld", ld_done)
                for s in sls:
                    B.op("vector", lambda e, s=s: e.reduce_sum(
                        stat[:, s * 8:s * 8 + 1],
                        xstage[:, s * 1024:(s + 1) * 1024], AX.X))
                B.wait("scalar", "ld", ld_done)
                B.wait("scalar", "dve", red2_prev)
                sq = None
                for s in sls:
                    sq = B.op("scalar", lambda e, s=s: e.activation(
                        xdump[:, s * 1024:(s + 1) * 1024],
                        xstage[:, s * 1024:(s + 1) * 1024], AF.Square), "act")
                B.wait("vector", "act", sq)
                red2_prev = 0
                for s in sls:
                    red2_prev = B.op("vector", lambda e, s=s: e.reduce_sum(
                        stat[:, s * 8 + 1:s * 8 + 2],
                        xdump[:, s * 1024:(s + 1) * 1024], AX.X), "dve")
                for s in sls:
                    B.op("vector", lambda e, s0=s * 8: e.tensor_scalar(
                        stat[:, s0:s0 + 1], stat[:, s0:s0 + 1], -1.0 / D, None,
                        ALU.mult))
                for s in sls:
                    B.op("vector", lambda e, s0=s * 8: e.tensor_scalar(
                        stat[:, s0 + 2:s0 + 3], stat[:, s0 + 1:s0 + 2], 1.0 / D, None,
                        ALU.mult))
                for s in sls:
                    B.op("vector", lambda e, s0=s * 8: e.tensor_tensor(
                        stat[:, s0 + 3:s0 + 4], stat[:, s0:s0 + 1], stat[:, s0:s0 + 1],
                        ALU.mult))
                last = None
                for s in sls:
                    last = B.op("vector", lambda e, s0=s * 8: e.tensor_tensor(
                        stat[:, s0 + 4:s0 + 5], stat[:, s0 + 2:s0 + 3],
                        stat[:, s0 + 3:s0 + 4], ALU.subtract), "dve")
                B.wait("scalar", "dve", last)
                la = None
                for s in sls:
                    la = B.op("scalar", lambda e, s0=s * 8: e.activation(
                        stat[:, s0 + 5:s0 + 6], stat[:, s0 + 4:s0 + 5], AF.Sqrt,
                        bias=epscol[:, 0:1]), "act")
                B.wait("vector", "act", la)
                for s in sls:
                    B.op("vector", lambda e, s0=s * 8: e.reciprocal(
                        stat[:, s0 + 6:s0 + 7], stat[:, s0 + 5:s0 + 6]))
                for i in range(nb):
                    s = sls[i]
                    slot_free[s] = B.op("vector", lambda e, slot=s, s0=s * 8:
                        e.tensor_scalar(
                            xn_tm[:, slot * 1024:(slot + 1) * 1024],
                            xstage[:, slot * 1024:(slot + 1) * 1024],
                            stat[:, s0:s0 + 1], stat[:, s0 + 6:s0 + 7],
                            ALU.add, ALU.mult), "dve")
                if hook is not None:
                    hook(t0 // 4)
                for i in range(nb):
                    t = t0 + i
                    slot = sls[i]
                    B.wait("tensor", "dve", slot_free[slot])
                    for c in range(8):
                        k = t * 8 + c
                        pb = tr_ps[k % 2]
                        B.wait("tensor", "dve", tr_free.get(k % 2, dve0))
                        B.op("tensor", lambda e, pb=pb, slot=slot, c=c: e.transpose(
                            pb[:, 0:128],
                            xn_tm[:, slot * 1024 + c * 128: slot * 1024 + (c + 1) * 128],
                            ident[:, :]), "pe")
                        B.wait("vector", "pe", B.cnt["pe"])
                        tr_free[k % 2] = B.op("vector",
                            lambda e, pb=pb, c=c, t=t: e.tensor_copy(
                                out=dst_fm[:, c * dst_cols + t * 128:
                                           c * dst_cols + t * 128 + 128],
                                in_=pb[:, 0:128]), "dve")

        def emit_rope(pre_ap, perm_bank, cosap, sinap, dstap, t1, t2, pre_act):
            B.wait("tensor", "act", pre_act)
            pe_perm = B.op("tensor", lambda e: e.matmul(
                perm_bank, permT[:, :], pre_ap, start=True, stop=True), "pe")
            B.wait("vector", "act", pre_act)
            B.op("vector", lambda e: e.tensor_tensor(t1[:, :], pre_ap, cosap, ALU.mult))
            B.wait("vector", "pe", pe_perm)
            t2done = B.op("vector", lambda e: e.tensor_tensor(t2[:, :], perm_bank, sinap,
                                                              ALU.mult), "dve")
            B.wait("vector", "dve", t2done)
            return B.op("vector", lambda e: e.tensor_tensor(dstap, t1[:, :], t2[:, :],
                                                            ALU.add), "dve")

        with contextlib.ExitStack() as p1:
            xn_own = p1.enter_context(nc.sbuf_tensor("xn_own", [128, 8 * TOWN], BF16))
            xstage = p1.enter_context(nc.sbuf_tensor("xstage", [128, 4 * 1024], F32))
            xdump = p1.enter_context(nc.sbuf_tensor("xdump", [128, 4 * 1024], F32))
            stat = p1.enter_context(nc.sbuf_tensor("stat", [128, 64], F32))
            xn_tm = p1.enter_context(nc.sbuf_tensor("xn_tm", [128, 4 * 1024], BF16))
            tr_ps = [p1.enter_context(nc.psum_tensor(f"trp{i}", [128, 1024], BF16))
                     for i in range(2)]
            brow_q = p1.enter_context(nc.sbuf_tensor("brow_q", [1, D], F32))
            brow_qb = p1.enter_context(nc.sbuf_tensor("brow_qb", [1, D], BF16))
            rqc = p1.enter_context(nc.sbuf_tensor("rqc", [128, TOWN], BF16))
            rqs = p1.enter_context(nc.sbuf_tensor("rqs", [128, TOWN], BF16))
            wstage = p1.enter_context(nc.sbuf_tensor("wstage", [128, 1024], F32))
            wq_bf = p1.enter_context(nc.sbuf_tensor("wq_bf", [128, 1024], BF16))
            qpre = p1.enter_context(nc.sbuf_tensor("qpre", [128, 2 * 512], BF16))
            t1b = p1.enter_context(nc.sbuf_tensor("t1b", [128, 512], BF16))
            t2b = p1.enter_context(nc.sbuf_tensor("t2b", [128, 512], BF16))
            pr_ps = [p1.enter_context(nc.psum_tensor(f"prp{i}", [128, 512], F32))
                     for i in range(2)]
            pm_ps = [p1.enter_context(nc.psum_tensor(f"pmp{i}", [128, 512], F32))
                     for i in range(2)]

            B.op("sync", lambda e: e.dma_start(out=rqc[:, :], in_=rq_cos[:, :]), "ld", 16)
            B.op("sync", lambda e: e.dma_start(out=rqs[:, :], in_=rq_sin[:, :]), "ld", 16)
            B.op("sync", lambda e: e.dma_start(out=brow_q[:, :], in_=bq[:, :]), "ld", 16)
            rope_tab_ld = B.cnt["ld"]
            B.wait("vector", "ld", rope_tab_ld)
            bqb_dve = B.op("vector", lambda e: e.tensor_copy(out=brow_qb[:, :],
                                                             in_=brow_q[:, :]), "dve")
            B.wait("tensor", "dve", bqb_dve)
            B.wait("tensor", "ld", max(consts_ld, rope_tab_ld))
            B.wait("tensor", "pool", consts_pool)
            emit_ln(x_own, NTO, xn_own, TOWN, xstage, stat, xdump, xn_tm, tr_ps)
            ln_done_dve = B.cnt["dve"]

            B.wait("tensor", "dve", ln_done_dve)
            B.wait("vector", "ld", rope_tab_ld)
            proj_free = {}
            rope_done = {}

            def emit_qpair(p):
                B.wait("sync", "pool", B.cnt["pool"])
                B.op("sync", lambda e, p=p: e.dma_start(
                    out=wstage[:, :].rearrange("q (dc f) -> q dc f", dc=8),
                    in_=wqT[:, p * 128:(p + 1) * 128]
                    .rearrange("(dc q) f -> q dc f", q=128)), "ld", 16)
                B.wait("gpsimd", "ld", B.cnt["ld"])
                B.wait("gpsimd", "pe", B.cnt["pe"])
                wq_ready = B.op("gpsimd", lambda e: e.tensor_copy(
                    out=wq_bf[:, :], in_=wstage[:, :]), "pool")
                B.wait("tensor", "pool", wq_ready)
                for ch in range(4):
                    k = p * 4 + ch
                    par = k % 2
                    bank = pr_ps[par]
                    B.wait("tensor", "act", proj_free.get(par, 0))
                    for dc in range(8):
                        B.op("tensor", lambda e, bank=bank, dc=dc, ch=ch: e.matmul(
                            bank[:, :],
                            wq_bf[:, dc * 128:(dc + 1) * 128],
                            xn_own[:, dc * TOWN + ch * 512: dc * TOWN + (ch + 1) * 512],
                            start=(dc == 0), stop=False))
                    pe_proj = B.op("tensor", lambda e, bank=bank, p=p: e.matmul(
                        bank[:, :], brow_qb[0:1, p * 128:(p + 1) * 128],
                        ones_bf[0:1, 0:512], start=False, stop=True), "pe")
                    B.wait("scalar", "pe", pe_proj)
                    B.wait("scalar", "dve", rope_done.get(par, 0))
                    proj_free[par] = B.op("scalar", lambda e, bank=bank, par=par:
                        e.activation(qpre[:, par * 512:(par + 1) * 512], bank[:, :],
                                     AF.Copy), "act")
                    B.wait("tensor", "dve", rope_done.get(par, 0))
                    rope_done[par] = emit_rope(
                        qpre[:, par * 512:(par + 1) * 512],
                        pm_ps[par][:, :],
                        rqc[:, ch * 512:(ch + 1) * 512],
                        rqs[:, ch * 512:(ch + 1) * 512],
                        q_all[:, p * TOWN + ch * 512: p * TOWN + (ch + 1) * 512],
                        t1b, t2b, proj_free[par])

            emit_ln(x_seq, NT, xnf, T, xstage, stat, xdump, xn_tm, tr_ps,
                    hook=emit_qpair)
            q_done_dve = B.cnt["dve"]
            B.flush(block, S)


        p23 = contextlib.ExitStack()
        out_norm = p23.enter_context(
            nc.sbuf_tensor("out_norm", [128, NPAIR * TOWN], BF16, side="right"))
        with contextlib.ExitStack() as p2:
            brow_k = p2.enter_context(nc.sbuf_tensor("brow_k", [1, D], F32))
            brow_kb = p2.enter_context(nc.sbuf_tensor("brow_kb", [1, D], BF16))
            brow_vb = p2.enter_context(nc.sbuf_tensor("brow_vb", [1, NPAIR * 130], BF16))
            rkc = p2.enter_context(nc.sbuf_tensor("rkc", [128, T], BF16))
            rks = p2.enter_context(nc.sbuf_tensor("rks", [128, T], BF16))
            masks = p2.enter_context(
                nc.sbuf_tensor("masks_sb", [128, MASK_OFF[8]], BF16))
            k_bf = p2.enter_context(nc.sbuf_tensor("k_bf", [128, T], BF16))
            v_store = p2.enter_context(
                nc.sbuf_tensor("v_store", [128, NT * 130], BF16))
            wstage2 = p2.enter_context(nc.sbuf_tensor("wstage2", [128, 1040], F32))
            wk_bf = p2.enter_context(nc.sbuf_tensor("wk_bf", [128, 1024], BF16))
            wv_bf = p2.enter_context(nc.sbuf_tensor("wv_bf", [128, 1040], BF16))
            kpre = p2.enter_context(nc.sbuf_tensor("kpre", [128, 2 * 512], BF16))
            t1k = p2.enter_context(nc.sbuf_tensor("t1k", [128, 512], BF16))
            t2k = p2.enter_context(nc.sbuf_tensor("t2k", [128, 512], BF16))
            p_t = p2.enter_context(nc.sbuf_tensor("p_t", [128, 4 * 1024], BF16))
            rowbuf = p2.enter_context(nc.sbuf_tensor("rowbuf", [65, 1024], F32))
            recipbf = p2.enter_context(nc.sbuf_tensor("recipbf", [65, 1024], BF16))
            evacA = p2.enter_context(nc.sbuf_tensor("evacA", [64, 512], BF16))
            evacB = p2.enter_context(nc.sbuf_tensor("evacB", [64, 512], BF16))
            normA = p2.enter_context(nc.sbuf_tensor("normA", [64, 512], BF16))
            normB = p2.enter_context(nc.sbuf_tensor("normB", [64, 512], BF16))
            cstat = p2.enter_context(nc.sbuf_tensor("cstat", [128, 4], F32))
            negc = p2.enter_context(nc.sbuf_tensor("negc", [128, 1], F32))
            s_ps = [p2.enter_context(nc.psum_tensor(f"sps{i}", [128, 1024], F32))
                    for i in range(2)]
            o_ps = p2.enter_context(nc.psum_tensor("ops", [128, 1024], F32))
            bc_ps = [p2.enter_context(nc.psum_tensor(f"bcp{i}", [64, 512], F32))
                     for i in range(2)]

            B.wait("sync", "dve", q_done_dve)
            B.op("sync", lambda e: e.dma_start(out=brow_k[:, :], in_=bk[:, :]), "ld", 16)
            B.op("sync", lambda e: e.dma_start(out=wstage2[0:1, 0:NPAIR * 130],
                                               in_=bv[:, :]), "ld", 16)
            B.op("sync", lambda e: e.dma_start(out=rkc[:, :], in_=rk_cos[:, :]),
                 "ld", 16)
            B.op("sync", lambda e: e.dma_start(out=rks[:, :], in_=rk_sin[:, :]),
                 "ld", 16)
            B.op("sync", lambda e: e.dma_start(out=masks[:, :], in_=masks_in[:, :]),
                 "ld", 16)
            ph2_ld = B.cnt["ld"]
            B.wait("vector", "ld", ph2_ld)
            B.wait("tensor", "ld", ph2_ld)
            B.op("vector", lambda e: e.tensor_copy(out=brow_kb[:, :], in_=brow_k[:, :]))
            bkb_dve = B.op("vector", lambda e: e.tensor_copy(
                out=brow_vb[:, :], in_=wstage2[0:1, 0:NPAIR * 130]), "dve")
            B.wait("tensor", "dve", bkb_dve)
            B.wait("sync", "dve", bkb_dve)

            jglob = 0
            sfree_mark = {}
            ptfree_mark = {}
            evac_mark = 0
            place_prev = 0
            v_done_act = 0

            wfcst = {"ready": 0, "next": 0}

            def emit_wfc_group(n8):
                for k in range(8):
                    n = n8 * 8 + k
                    piece, dc = n // 8, n % 8
                    B.wait("sync", "act", wfcst["ready"])
                    B.op("sync", lambda e, piece=piece, dc=dc: e.dma_start(
                        out=wstage2[:, 0:1024],
                        in_=wfcT_in[dc * 128:(dc + 1) * 128,
                                    piece * 1024:(piece + 1) * 1024]), "ld", 16)
                    B.wait("scalar", "ld", B.cnt["ld"])
                    wfcst["ready"] = B.op("scalar",
                        lambda e, piece=piece, dc=dc: e.activation(
                            xnf[:, dc * FF + piece * 1024:
                                dc * FF + (piece + 1) * 1024],
                            wstage2[:, 0:1024], AF.Copy), "act")

            def emit_norm(p, qc):
                nonlocal evac_mark, place_prev
                last_pv = B.cnt["pe"]
                B.wait("scalar", "pe", last_pv)
                B.op("scalar", lambda e: e.activation(
                    rowbuf[64:65, 0:512], o_ps[64:65, 0:512], AF.Ln))
                B.op("scalar", lambda e: e.activation(
                    rowbuf[64:65, 512:1024], o_ps[64:65, 512:1024], AF.Ln))
                B.op("scalar", lambda e: e.activation(
                    recipbf[64:65, 0:512], rowbuf[64:65, 0:512], AF.Exp,
                    scale=-1.0))
                rows_act = B.op("scalar", lambda e: e.activation(
                    recipbf[64:65, 512:1024], rowbuf[64:65, 512:1024], AF.Exp,
                    scale=-1.0), "act")
                B.wait("tensor", "act", rows_act)
                B.wait("tensor", "dve", evac_mark)
                B.op("tensor", lambda e: e.matmul(
                    bc_ps[0][0:64, :], ones_bf[64:65, 0:64],
                    recipbf[64:65, 0:512], start=True, stop=True))
                bc_pe = B.op("tensor", lambda e: e.matmul(
                    bc_ps[1][0:64, :], ones_bf[64:65, 0:64],
                    recipbf[64:65, 512:1024], start=True, stop=True), "pe")
                B.wait("vector", "pe", bc_pe)
                B.wait("vector", "g", place_prev)
                B.op("vector", lambda e: e.tensor_copy(
                    out=evacA[:, :], in_=o_ps[0:64, 0:512]))
                B.op("vector", lambda e: e.tensor_copy(
                    out=evacB[:, :], in_=o_ps[0:64, 512:1024]))
                B.op("vector", lambda e: e.tensor_tensor(
                    normA[:, :], evacA[:, :], bc_ps[0][0:64, :], ALU.mult))
                evac_mark = B.op("vector", lambda e: e.tensor_tensor(
                    normB[:, :], evacB[:, :], bc_ps[1][0:64, :], ALU.mult),
                    "dve")
                B.wait("gpsimd", "dve", evac_mark)
                B.op("gpsimd", lambda e, p=p, qc=qc: e.dma_start(
                    out=out_norm[0:64, p * TOWN + qc * 512:
                                 p * TOWN + (qc + 1) * 512],
                    in_=normA[:, :]), "g", 16)
                place_prev = B.op("gpsimd", lambda e, p=p, qc=qc: e.dma_start(
                    out=out_norm[64:128, p * TOWN + qc * 512:
                                 p * TOWN + (qc + 1) * 512],
                    in_=normB[:, :]), "g", 16)
                if p == NPAIR - 1:
                    if qc == 0:
                        B.wait("sync", "pool", B.cnt["pool"])
                    emit_wfc_group(qc)

            for p in range(NPAIR):
                B.wait("sync", "pool", B.cnt["pool"])
                B.op("sync", lambda e, p=p: e.dma_start(
                    out=wstage2[:, 0:1024].rearrange("q (dc f) -> q dc f", dc=8),
                    in_=wkT[:, p * 128:(p + 1) * 128]
                    .rearrange("(dc q) f -> q dc f", q=128)), "ld", 16)
                B.wait("gpsimd", "ld", B.cnt["ld"])
                B.wait("gpsimd", "pe", B.cnt["pe"])
                wk_ready = B.op("gpsimd", lambda e: e.tensor_copy(
                    out=wk_bf[:, :], in_=wstage2[:, 0:1024]), "pool")
                B.wait("sync", "pool", wk_ready)
                B.op("sync", lambda e, p=p: e.dma_start(
                    out=wstage2[:, 0:1040].rearrange("q (dc f) -> q dc f", dc=8),
                    in_=wvT[:, p * 130:(p + 1) * 130]
                    .rearrange("(dc q) f -> q dc f", q=128)), "ld", 16)
                B.wait("gpsimd", "ld", B.cnt["ld"])
                B.wait("gpsimd", "pe", B.cnt["pe"])
                wv_ready = B.op("gpsimd", lambda e: e.tensor_copy(
                    out=wv_bf[:, :], in_=wstage2[:, 0:1040]), "pool")

                prev_attn_pe = B.cnt["pe"]
                kproj_free = dict(sfree_mark)
                kpre_free = {}
                kperm_free = {0: evac_mark, 1: evac_mark}
                B.wait("tensor", "pool", wk_ready)
                B.wait("vector", "pe", prev_attn_pe)
                for ch in range(8):
                    par = ch % 2
                    bank = s_ps[par]
                    B.wait("tensor", "act", kproj_free.get(par, 0))
                    for dc in range(8):
                        B.op("tensor", lambda e, bank=bank, dc=dc, ch=ch: e.matmul(
                            bank[:, 0:512],
                            wk_bf[:, dc * 128:(dc + 1) * 128],
                            xnf[:, dc * T + ch * 512: dc * T + (ch + 1) * 512],
                            start=(dc == 0), stop=False))
                    pe_kproj = B.op("tensor", lambda e, bank=bank, p=p: e.matmul(
                        bank[:, 0:512], brow_kb[0:1, p * 128:(p + 1) * 128],
                        ones_bf[0:1, 0:512], start=False, stop=True), "pe")
                    B.wait("scalar", "pe", pe_kproj)
                    B.wait("scalar", "dve", kpre_free.get(par, 0))
                    kproj_free[par] = B.op("scalar",
                        lambda e, bank=bank, par=par: e.activation(
                            kpre[:, par * 512:(par + 1) * 512], bank[:, 0:512],
                            AF.Copy), "act")
                    B.wait("tensor", "dve", kperm_free.get(par, 0))
                    kpre_free[par] = kperm_free[par] = emit_rope(
                        kpre[:, par * 512:(par + 1) * 512],
                        o_ps[:, par * 512:(par + 1) * 512],
                        rkc[:, ch * 512:(ch + 1) * 512],
                        rks[:, ch * 512:(ch + 1) * 512],
                        k_bf[:, ch * 512:(ch + 1) * 512],
                        t1k, t2k, kproj_free[par])
                kv_rope_dve = B.cnt["dve"]

                vproj_free = dict(kproj_free)
                B.wait("tensor", "pool", wv_ready)
                B.wait("scalar", "pe", prev_attn_pe)
                for tt in range(NT):
                    par = tt % 2
                    bank = s_ps[par]
                    B.wait("tensor", "act", vproj_free.get(par, 0))
                    for dc in range(8):
                        B.op("tensor", lambda e, bank=bank, dc=dc, tt=tt: e.matmul(
                            bank[:, 0:130],
                            xnf[:, dc * T + tt * 128: dc * T + (tt + 1) * 128],
                            wv_bf[:, dc * 130:(dc + 1) * 130],
                            start=(dc == 0), stop=False))
                    pe_v = B.op("tensor", lambda e, bank=bank, p=p: e.matmul(
                        bank[:, 0:130], ones_bf[0:1, 0:128],
                        brow_vb[0:1, p * 130:(p + 1) * 130], start=False,
                        stop=True), "pe")
                    B.wait("scalar", "pe", pe_v)
                    vproj_free[par] = B.op("scalar",
                        lambda e, bank=bank, tt=tt: e.activation(
                            v_store[:, tt * 130:(tt + 1) * 130], bank[:, 0:130],
                            AF.Copy), "act")
                v_done_act = B.cnt["act"]

                B.wait("vector", "dve", kv_rope_dve)
                B.op("vector", lambda e, p=p: e.reduce_sum(
                    cstat[:, 0:1], q_all[:, p * TOWN:(p + 1) * TOWN:16], AX.X))
                cs_dve = B.op("vector", lambda e: e.reduce_sum(
                    cstat[:, 1:2], k_bf[:, ::32], AX.X), "dve")
                B.wait("tensor", "dve", max(cs_dve, evac_mark))
                cpe = B.op("tensor", lambda e: e.matmul(
                    bc_ps[0][0:1, 0:1], cstat[:, 1:2], cstat[:, 0:1],
                    start=True, stop=True), "pe")
                B.wait("vector", "pe", cpe)
                cdve = B.op("vector", lambda e: e.tensor_copy(
                    out=cstat[0:1, 3:4], in_=bc_ps[0][0:1, 0:1]), "dve")
                B.wait("tensor", "dve", cdve)
                B.wait("tensor", "act", v_done_act)
                cpe2 = B.op("tensor", lambda e: e.matmul(
                    s_ps[0][:, 0:1], onesb[0:1, 0:128], cstat[0:1, 3:4],
                    start=True, stop=True), "pe")
                B.wait("vector", "pe", cpe2)
                negc_dve = B.op("vector", lambda e: e.tensor_scalar(
                    negc[:, 0:1], s_ps[0][:, 0:1], -0.5 / (128.0 * 128.0), None,
                    ALU.mult), "dve")
                B.wait("scalar", "dve", negc_dve)

                B.wait("tensor", "dve", negc_dve)
                steps = [(qc, j) for qc in range(4) for j in range(8 * qc + 8)]
                pending = []

                def emit_pv(ctx):
                    qc, j, pt3, c0, gsem, gval = ctx
                    nsteps = 8 * qc + 8
                    B.wait("tensor", gsem, gval)
                    if j == 0:
                        B.wait("tensor", "dve", evac_mark)
                    B.op("tensor", lambda e, j=j, pt3=pt3, c0=c0, n=nsteps:
                        e.matmul(
                            o_ps[0:65, c0:512],
                            v_store[:, j * 130: j * 130 + 65],
                            p_t[:, pt3 * 1024 + c0: pt3 * 1024 + 512],
                            start=(j == 0), stop=(j == n - 1)))
                    pv_pe = B.op("tensor",
                        lambda e, j=j, pt3=pt3, c0=c0, n=nsteps: e.matmul(
                            o_ps[0:65, 512 + c0:1024],
                            v_store[:, j * 130 + 65: j * 130 + 130],
                            p_t[:, pt3 * 1024 + 512 + c0: pt3 * 1024 + 1024],
                            start=(j == 0), stop=(j == n - 1)), "pe")
                    ptfree_mark[pt3] = pv_pe

                def drain_one():
                    ctx = pending.pop(0)
                    emit_pv(ctx)
                    if ctx[1] == 8 * ctx[0] + 7:
                        emit_norm(p, ctx[0])

                for (qc, j) in steps:
                    r = j - 8 * qc
                    c0 = COL0[r] if r >= 0 else 0
                    qbase = p * TOWN + qc * 512
                    par = jglob % 2
                    pt3 = jglob % 4
                    bank = s_ps[par]
                    B.wait("tensor", "act", sfree_mark.get(par, 0))
                    B.op("tensor",
                         lambda e, bank=bank, j=j, c0=c0, qbase=qbase, r=r: e.matmul(
                             bank[:, c0:512],
                             k_bf[0:64, j * 128:(j + 1) * 128],
                             q_all[0:64, qbase + c0: qbase + 512],
                             start=True, stop=(r < 0)))
                    qk_pe = B.op("tensor",
                         lambda e, bank=bank, j=j, c0=c0, qbase=qbase, r=r: e.matmul(
                             bank[:, 512 + c0:1024],
                             k_bf[64:128, j * 128:(j + 1) * 128],
                             q_all[64:128, qbase + c0: qbase + 512],
                             start=True, stop=(r < 0)), "pe" if r < 0 else None)
                    if r >= 0:
                        B.op("tensor", lambda e, bank=bank, c0=c0, r=r: e.matmul(
                            bank[:, c0:512], ident[:, :],
                            masks[:, MASK_OFF[r]:MASK_OFF[r + 1]],
                            start=False, stop=True))
                        qk_pe = B.op("tensor",
                            lambda e, bank=bank, c0=c0, r=r: e.matmul(
                                bank[:, 512 + c0:1024], ident[:, :],
                                masks[:, MASK_OFF[r]:MASK_OFF[r + 1]],
                                start=False, stop=True), "pe")
                    if p == NPAIR - 1 and qc == 3 and j == 31:
                        wfcst["qk_last"] = qk_pe
                    B.wait("scalar", "pe", qk_pe)
                    B.wait("scalar", "pe", ptfree_mark.get(pt3, 0))
                    if r < 0:
                        em = B.op("scalar", lambda e, bank=bank, pt3=pt3:
                            e.activation(
                                p_t[:, pt3 * 1024:(pt3 + 1) * 1024],
                                bank[:, :], AF.Exp, bias=negc[:, 0:1]), "act")
                    else:
                        B.op("scalar", lambda e, bank=bank, pt3=pt3, c0=c0:
                            e.activation(
                                p_t[:, pt3 * 1024 + c0: pt3 * 1024 + 512],
                                bank[:, c0:512], AF.Exp, bias=negc[:, 0:1]))
                        em = B.op("scalar", lambda e, bank=bank, pt3=pt3, c0=c0:
                            e.activation(
                                p_t[:, pt3 * 1024 + 512 + c0:
                                    pt3 * 1024 + 1024],
                                bank[:, 512 + c0:1024], AF.Exp,
                                bias=negc[:, 0:1]), "act")
                    sfree_mark[par] = em
                    jglob += 1
                    if len(pending) >= 3:
                        drain_one()
                    pending.append((qc, j, pt3, c0, "act", em))
                while pending:
                    drain_one()

            attn_done_g = B.cnt["g"]
            ph2_act = B.cnt["act"]
            ph2_dve = B.cnt["dve"]
            B.flush(block, S)
        p12.close()

        wfc_bf = xnf
        wfc_ready = wfcst["ready"]
        with contextlib.ExitStack() as p3:
            wout_stage = p3.enter_context(
                nc.sbuf_tensor("wout_stage", [128, 8192], F32))
            wout_bf = p3.enter_context(nc.sbuf_tensor("wout_bf", [128, 8192], BF16))
            aev = [p3.enter_context(nc.sbuf_tensor(f"aev{i}", [128, 1024], F32))
                   for i in range(2)]
            pp = [p3.enter_context(nc.psum_tensor(f"pp{i}", [128, 1024], F32))
                  for i in range(2)]

            B.wait("sync", "pe", wfcst["qk_last"])
            B.op("sync", lambda e: e.dma_start(
                out=wout_stage[:, :].rearrange("q (dc f) -> q dc f", dc=8),
                in_=woutT_in.rearrange("(dc q) f -> q dc f", q=128)), "ld", 16)
            B.wait("gpsimd", "ld", B.cnt["ld"])
            wout_ready = B.op("gpsimd", lambda e: e.tensor_copy(
                out=wout_bf[:, :], in_=wout_stage[:, :]), "pool")

            B.wait("tensor", "pool", wout_ready)
            B.wait("tensor", "g", attn_done_g)
            B.wait("tensor", "act", ph2_act)
            B.wait("tensor", "dve", ph2_dve)
            ap_free = {}
            ap_store = {}
            for tt in range(NTO):
                par = tt % 2
                bank = pp[par]
                B.wait("tensor", "dve", ap_free.get(par, 0))
                op_pe = None
                for dc in range(8):
                    B.op("tensor", lambda e, bank=bank, dc=dc, tt=tt: e.matmul(
                        bank[:, 0:512],
                        out_norm[:, dc * TOWN + tt * 128: dc * TOWN + (tt + 1) * 128],
                        wout_bf[:, dc * 1024: dc * 1024 + 512],
                        start=(dc == 0), stop=(dc == 7)))
                    if dc < 7:
                        B.op("tensor", lambda e, bank=bank, dc=dc, tt=tt: e.matmul(
                            bank[:, 512:1024],
                            out_norm[:, dc * TOWN + tt * 128:
                                     dc * TOWN + (tt + 1) * 128],
                            wout_bf[:, dc * 1024 + 512:(dc + 1) * 1024],
                            start=(dc == 0), stop=False))
                    else:
                        op_pe = B.op("tensor",
                            lambda e, bank=bank, dc=dc, tt=tt: e.matmul(
                                bank[:, 512:1024],
                                out_norm[:, dc * TOWN + tt * 128:
                                         dc * TOWN + (tt + 1) * 128],
                                wout_bf[:, dc * 1024 + 512:(dc + 1) * 1024],
                                start=(dc == 0), stop=True), "pe")
                B.wait("vector", "pe", op_pe)
                B.wait("vector", "g", ap_store.get(par, 0))
                ap_free[par] = B.op("vector", lambda e, bank=bank, par=par:
                    e.tensor_copy(out=aev[par][:, :], in_=bank[:, :]), "dve")
                B.wait("sync", "dve", ap_free[par])
                ap_store[par] = B.op("sync", lambda e, tt=tt: e.dma_start(
                    out=aproj_hbm[tt * 128:(tt + 1) * 128, :],
                    in_=aev[tt % 2][:, :]), "g", 16)
            aproj_g = B.cnt["g"]
            ph3_dve = B.cnt["dve"]
            B.flush(block, S)
        p23.close()

        with contextlib.ExitStack() as p4:
            xn_own2 = p4.enter_context(nc.sbuf_tensor("xn_own2", [128, 8 * TOWN], BF16))
            b_bc = p4.enter_context(nc.sbuf_tensor("b_bc", [128, 1024], F32))
            brow_o = p4.enter_context(nc.sbuf_tensor("brow_o", [1, D], F32))
            with contextlib.ExitStack() as p4a:
                xstage4 = p4a.enter_context(nc.sbuf_tensor("xstage4", [128, 4096], F32))
                xdump4 = p4a.enter_context(nc.sbuf_tensor("xdump4", [128, 4 * 1024], F32))
                stat4 = p4a.enter_context(nc.sbuf_tensor("stat4", [128, 64], F32))
                xn_tm4 = p4a.enter_context(nc.sbuf_tensor("xn_tm4", [128, 4096], BF16))
                tr_ps4 = [p4a.enter_context(nc.psum_tensor(f"trq{i}", [128, 1024], BF16))
                          for i in range(2)]
                bb_ps = p4a.enter_context(nc.psum_tensor("bb_ps", [128, 1024], F32))

                B.wait("tensor", "dve", ph3_dve)
                B.wait("sync", "pool", wout_ready)
                B.op("sync", lambda e: e.dma_start(out=brow_o[:, :], in_=b_out[:, :]),
                     "ld", 16)
                bo_ld = B.cnt["ld"]
                emit_ln(x_own, NTO, xn_own2, TOWN, xstage4, stat4, xdump4, xn_tm4,
                        tr_ps4)
                xn2_dve = B.cnt["dve"]

                B.wait("tensor", "ld", bo_ld)
                B.op("tensor", lambda e: e.matmul(
                    bb_ps[:, 0:512], onesb[0:1, 0:128], brow_o[0:1, 0:512],
                    start=True, stop=True))
                bb_pe = B.op("tensor", lambda e: e.matmul(
                    bb_ps[:, 512:1024], onesb[0:1, 0:128], brow_o[0:1, 512:1024],
                    start=True, stop=True), "pe")
                B.wait("vector", "pe", bb_pe)
                bb_dve = B.op("vector", lambda e: e.tensor_copy(
                    out=b_bc[:, :], in_=bb_ps[:, :]), "dve")
                B.flush(block, S)

            h_store = p4.enter_context(nc.sbuf_tensor("h_store", [128, 32 * 512], BF16))
            ztmp = p4.enter_context(nc.sbuf_tensor("ztmp", [128, 2 * 512], BF16))
            wpstage = [p4.enter_context(nc.sbuf_tensor(f"wpst{i}", [128, 1024], F32))
                       for i in range(2)]
            wproj_bf = [p4.enter_context(nc.sbuf_tensor(f"wpbf{i}", [128, 1024], BF16))
                        for i in range(2)]
            ybuf = p4.enter_context(nc.sbuf_tensor("ybuf", [128, 3 * 1024], F32))
            smem = [p4.enter_context(nc.sbuf_tensor(f"smem{i}", [128, 1024], F32))
                    for i in range(2)]
            stmp = [p4.enter_context(nc.sbuf_tensor(f"stmp{i}", [128, 1024], F32))
                    for i in range(2)]
            sacc = [p4.enter_context(nc.sbuf_tensor(f"sacc{i}", [128, 1024], F32))
                    for i in range(2)]
            xo_t = [p4.enter_context(nc.sbuf_tensor(f"xo_t{i}", [128, 1024], F32))
                    for i in range(2)]
            ap_t = [p4.enter_context(nc.sbuf_tensor(f"ap_t{i}", [128, 1024], F32))
                    for i in range(2)]
            o_stage = stmp

            wp_consumed = {}
            wp_ready = {}
            out_store = {}
            fin_mark = {0: 0, 1: 0}
            rel4 = {}
            y_evac_act = 0
            y_pe_prev = 0
            for tc in range(4):
                with contextlib.ExitStack() as pth:
                    hps = [pth.enter_context(
                        nc.psum_tensor(f"hps{tc}_{i}", [128, 512], F32))
                        for i in range(2)]
                    if tc == 0:
                        B.wait("tensor", "act", wfc_ready)
                        B.wait("tensor", "dve", max(xn2_dve, bb_dve))
                    else:
                        B.wait("tensor", "act", y_evac_act)
                        B.wait("scalar", "pe", y_pe_prev)
                    hz_free = {}
                    hsq_done = {}
                    for fc in range(32):
                        par = fc % 2
                        bank = hps[par]
                        B.wait("tensor", "act", hz_free.get(par, 0))
                        hmm = None
                        for dc in range(8):
                            if dc < 7:
                                B.op("tensor",
                                    lambda e, bank=bank, dc=dc, fc=fc, tc=tc: e.matmul(
                                        bank[:, :],
                                        wfc_bf[:, dc * FF + fc * 128:
                                               dc * FF + (fc + 1) * 128],
                                        xn_own2[:, dc * TOWN + tc * 512:
                                                dc * TOWN + (tc + 1) * 512],
                                        start=(dc == 0), stop=False))
                            else:
                                hmm = B.op("tensor",
                                    lambda e, bank=bank, dc=dc, fc=fc, tc=tc: e.matmul(
                                        bank[:, :],
                                        wfc_bf[:, dc * FF + fc * 128:
                                               dc * FF + (fc + 1) * 128],
                                        xn_own2[:, dc * TOWN + tc * 512:
                                                dc * TOWN + (tc + 1) * 512],
                                        start=False, stop=True), "pe")
                        B.wait("scalar", "pe", hmm)
                        hz = B.op("scalar",
                            lambda e, bank=bank, par=par: e.activation(
                                ztmp[:, par * 512:(par + 1) * 512], bank[:, :],
                                AF.Relu), "act", drain=False)
                        B.wait("scalar", "act", hz)
                        hz_free[par] = hsq_done[par] = B.op("scalar",
                            lambda e, fc=fc, par=par: e.activation(
                                h_store[:, fc * 512:(fc + 1) * 512],
                                ztmp[:, par * 512:(par + 1) * 512],
                                AF.Square), "act")
                    h_dve = B.cnt["act"]
                    B.flush(block, S)
                with contextlib.ExitStack() as pty:
                    yps = [pty.enter_context(
                        nc.psum_tensor(f"yps{tc}_{i}", [128, 1024], F32))
                        for i in range(4)]

                    for weng in ("tensor", "scalar", "vector", "gpsimd", "sync"):
                        for wsem in ("pe", "act", "dve", "pool"):
                            if wsem == "dve" and (
                                    weng in ("tensor", "scalar")
                                    or (weng == "sync" and tc > 0)):
                                continue
                            B.wait(weng, wsem, B.cnt[wsem])
                    B.wait("tensor", "act", h_dve)
                    for fc in range(32):
                        par = fc % 2
                        B.wait("sync", "pe", wp_consumed.get(par, 0))
                        B.op("sync", lambda e, fc=fc: e.dma_start(
                            out=wpstage[fc % 2][:, :],
                            in_=wprojT_in[fc * 128:(fc + 1) * 128, :]), "ld", 16)
                        B.wait("scalar", "ld", B.cnt["ld"])
                        B.wait("scalar", "pe", wp_consumed.get(par, 0))
                        wp_ready[par] = B.op("scalar", lambda e, par=par: e.activation(
                            wproj_bf[par][:, :], wpstage[par][:, :], AF.Copy), "act")
                        B.wait("tensor", "act", wp_ready[par])
                        last = None
                        for ti in range(4):
                            B.op("tensor", lambda e, fc=fc, ti=ti, par=par: e.matmul(
                                yps[ti][:, 0:512],
                                h_store[:, fc * 512 + ti * 128:
                                        fc * 512 + (ti + 1) * 128],
                                wproj_bf[par][:, 0:512],
                                start=(fc == 0), stop=(fc == 31)))
                            if ti < 3:
                                B.op("tensor", lambda e, fc=fc, ti=ti, par=par:
                                    e.matmul(
                                        yps[ti][:, 512:1024],
                                        h_store[:, fc * 512 + ti * 128:
                                                fc * 512 + (ti + 1) * 128],
                                        wproj_bf[par][:, 512:1024],
                                        start=(fc == 0), stop=(fc == 31)))
                            else:
                                last = B.op("tensor", lambda e, fc=fc, ti=ti, par=par:
                                    e.matmul(
                                        yps[ti][:, 512:1024],
                                        h_store[:, fc * 512 + ti * 128:
                                                fc * 512 + (ti + 1) * 128],
                                        wproj_bf[par][:, 512:1024],
                                        start=(fc == 0), stop=(fc == 31)), "pe")
                        wp_consumed[par] = last
                    y_pe = B.cnt["pe"]
                    y_pe_prev = y_pe

                    for weng in ("tensor", "scalar", "vector", "gpsimd", "sync"):
                        for wsem in ("pe", "act", "dve", "pool"):
                            if wsem == "dve" and (
                                    weng in ("tensor", "scalar")
                                    or (weng == "sync" and tc > 0)):
                                continue
                            B.wait(weng, wsem, B.cnt[wsem])
                    for ti in range(4):
                        tt = tc * 4 + ti
                        par = ti % 2
                        eng = "vector"
                        esem = "dve"
                        slot = ti % 3
                        B.wait("scalar", "pe", y_pe)
                        B.wait("scalar", esem, rel4.get(slot, 0))
                        yv = B.op("scalar", lambda e, slot=slot, ti=ti:
                            e.activation(
                                ybuf[:, slot * 1024:(slot + 1) * 1024],
                                yps[ti][:, :], AF.Copy), "act")
                        y_evac_act = yv
                        B.wait("sync", "g", aproj_g)
                        B.wait("sync", esem, fin_mark[par])
                        B.op("sync", lambda e, tt=tt, par=par: e.dma_start(
                            out=xo_t[par][:, :],
                            in_=x_own[tt * 128:(tt + 1) * 128, :]), "ld", 16)
                        B.op("sync", lambda e, tt=tt, par=par: e.dma_start(
                            out=ap_t[par][:, :],
                            in_=aproj_hbm[tt * 128:(tt + 1) * 128, :]), "ld", 16)
                        fin_ld = B.cnt["ld"]
                        B.wait(eng, "act", yv)
                        B.wait(eng, "dve", bb_dve)
                        sm, st_, sa = smem[par], stmp[par], sacc[par]

                        yoff = (ti % 3) * 1024

                        def sop(fn, eng=eng, esem=esem, fence=False):
                            B.op(eng, lambda e, f=fn: f(e, 0, 512),
                                 esem, drain=False)
                            t = B.op(eng, lambda e, f=fn: f(e, 512, 1024),
                                     esem, drain=False)
                            if fence:
                                B.wait(eng, esem, t)
                            return t
                        sop(lambda e, lo, hi, yoff=yoff, sa=sa: e.tensor_scalar(
                            sa[:, lo:hi], ybuf[:, yoff + lo:yoff + hi],
                            1.0, None, ALU.is_gt))
                        sop(lambda e, lo, hi, yoff=yoff, st_=st_:
                            e.scalar_tensor_tensor(
                                st_[:, lo:hi], ybuf[:, yoff + lo:yoff + hi], 1.0,
                                ybuf[:, yoff + lo:yoff + hi],
                                ALU.is_gt, ALU.subtract))
                        sop(lambda e, lo, hi, yoff=yoff, sm=sm, st_=st_:
                            e.scalar_tensor_tensor(
                                sm[:, lo:hi], ybuf[:, yoff + lo:yoff + hi], 0.9,
                                st_[:, lo:hi], ALU.mult, ALU.subtract))
                        sop(lambda e, lo, hi, sm=sm, sa=sa:
                            e.scalar_tensor_tensor(
                                sa[:, lo:hi], sm[:, lo:hi], 1.0, sa[:, lo:hi],
                                ALU.is_gt, ALU.add))
                        for _ in range(3):
                            sop(lambda e, lo, hi, yoff=yoff, sm=sm, st_=st_:
                                e.scalar_tensor_tensor(
                                    st_[:, lo:hi], sm[:, lo:hi], 1.0,
                                    ybuf[:, yoff + lo:yoff + hi],
                                    ALU.is_gt, ALU.subtract))
                            sop(lambda e, lo, hi, sm=sm, st_=st_:
                                e.scalar_tensor_tensor(
                                    sm[:, lo:hi], sm[:, lo:hi], 0.9, st_[:, lo:hi],
                                    ALU.mult, ALU.subtract))
                            sop(lambda e, lo, hi, sm=sm, sa=sa:
                                e.scalar_tensor_tensor(
                                    sa[:, lo:hi], sm[:, lo:hi], 1.0, sa[:, lo:hi],
                                    ALU.is_gt, ALU.add))
                        B.wait(eng, "ld", fin_ld)
                        B.wait(eng, "st", out_store.get(par, 0))
                        sop(lambda e, lo, hi, sa=sa, par=par:
                            e.scalar_tensor_tensor(
                                o_stage[par][:, lo:hi], sa[:, lo:hi], 0.2,
                                ap_t[par][:, lo:hi], ALU.mult, ALU.add))
                        sop(lambda e, lo, hi, par=par: e.tensor_tensor(
                            o_stage[par][:, lo:hi], o_stage[par][:, lo:hi],
                            xo_t[par][:, lo:hi], ALU.add))
                        fin_mark[par] = sop(lambda e, lo, hi, par=par:
                            e.tensor_tensor(
                                o_stage[par][:, lo:hi], o_stage[par][:, lo:hi],
                                b_bc[:, lo:hi], ALU.add), fence=True)
                        rel4[ti % 3] = fin_mark[par]
                        B.wait("sync", esem, fin_mark[par])
                        out_store[par] = B.op("sync", lambda e, tt=tt, par=par:
                            e.dma_start(
                                out=out_hbm[tt * 128:(tt + 1) * 128, :],
                                in_=o_stage[par][:, :]), "st", 16)

                    B.flush(block, S)

            B.wait("sync", "st", B.cnt["st"])
            B.flush(block, S)

    return nc


def _rope_tables(positions, scale):
    inv = 1.0 / (10000.0 ** (np.arange(0, 64, 2, dtype=np.float64) / 64.0))
    fr = positions.astype(np.float64)[None, :] * inv[:, None]
    c = np.cos(fr) * scale
    s = np.sin(fr) * scale
    bf = ml_dtypes.bfloat16
    cos_t = np.concatenate([c, c, c, c], axis=0).astype(np.float32)
    sin_t = np.concatenate([-s, s, -s, s], axis=0).astype(np.float32)
    return cos_t.astype(bf), sin_t.astype(bf)


def _build_masks(g):
    bf = ml_dtypes.bfloat16
    m = np.zeros((128, MASK_OFF[8]), dtype=np.float32)
    tri = (np.arange(128)[None, :] >= np.arange(128)[:, None]).astype(np.float32)
    for r in range(8):
        w = 512 - COL0[r]
        tile = np.zeros((128, w), dtype=np.float32)
        for col in range(w):
            gc = COL0[r] + col
            i = gc // 128
            if r < 2 * i + g:
                tile[:, col] = 1.0
            elif r == 2 * i + g:
                tile[:, col] = tri[:, gc % 128]
        m[:, MASK_OFF[r]:MASK_OFF[r + 1]] = tile
    return np.ascontiguousarray(((m - 1.0) * 100.0).astype(bf))


_NC_CACHE = []


def kernel(L0_attn_x, L0_attn_w_qkv, L0_attn_b_qkv, L0_attn_w_out, L0_attn_b_out,
           L0_ffn_W_fc, L0_ffn_W_proj):
    x = np.ascontiguousarray(np.asarray(L0_attn_x, dtype=np.float32))
    w_qkv = np.asarray(L0_attn_w_qkv, dtype=np.float32)
    b_qkv = np.asarray(L0_attn_b_qkv, dtype=np.float32)
    w_out = np.asarray(L0_attn_w_out, dtype=np.float32)
    b_out = np.asarray(L0_attn_b_out, dtype=np.float32)
    W_fc = np.asarray(L0_ffn_W_fc, dtype=np.float32)
    W_proj = np.asarray(L0_ffn_W_proj, dtype=np.float32)
    bf = ml_dtypes.bfloat16

    perm64 = np.concatenate([np.arange(0, 64, 2), np.arange(1, 64, 2)])
    qk_rows = np.concatenate([64 * h + perm64 for h in range(16)])
    wqT = np.ascontiguousarray(w_qkv[qk_rows, :].T)
    wkT = np.ascontiguousarray(w_qkv[1024 + qk_rows, :].T)
    bq_row = np.ascontiguousarray(b_qkv[qk_rows][None, :])
    bk_row = np.ascontiguousarray(b_qkv[1024 + qk_rows][None, :])
    wvT_full = w_qkv[2048:3072, :].T
    wvT = np.zeros((D, NPAIR * 130), dtype=np.float32)
    bv_row = np.zeros((1, NPAIR * 130), dtype=np.float32)
    bv_full = b_qkv[2048:3072]
    for p in range(NPAIR):
        wvT[:, p * 130: p * 130 + 64] = wvT_full[:, p * 128: p * 128 + 64]
        wvT[:, p * 130 + 65: p * 130 + 129] = wvT_full[:, p * 128 + 64:(p + 1) * 128]
        bv_row[0, p * 130: p * 130 + 64] = bv_full[p * 128: p * 128 + 64]
        bv_row[0, p * 130 + 64] = 1.0
        bv_row[0, p * 130 + 65: p * 130 + 129] = bv_full[p * 128 + 64:(p + 1) * 128]
        bv_row[0, p * 130 + 129] = 1.0
    woutT = np.ascontiguousarray(w_out.T)
    wfcT = np.ascontiguousarray(W_fc.T)
    wprojT = np.ascontiguousarray(W_proj.T)

    ident = np.eye(128, dtype=np.float32).astype(bf)
    sigma = np.concatenate([np.arange(32, 64), np.arange(0, 32),
                            np.arange(96, 128), np.arange(64, 96)])
    permM = np.zeros((128, 128), dtype=np.float32)
    permM[sigma, np.arange(128)] = 1.0
    permM = permM.astype(bf)

    rk_cos, rk_sin = _rope_tables(np.arange(T), 1.0)

    if not _NC_CACHE:
        _NC_CACHE.append(build_program())
    nc = _NC_CACHE[0]

    in_maps = []
    own_idx_all = []
    for c in range(8):
        b, g = c // 2, c % 2
        own_idx = np.concatenate(
            [np.arange(128 * (2 * j + g), 128 * (2 * j + g) + 128)
             for j in range(NTO)])
        own_idx_all.append(own_idx)
        xb = x[b]
        q_cos, q_sin = _rope_tables(own_idx, 0.125)
        in_maps.append({
            "x_seq": xb,
            "x_own": np.ascontiguousarray(xb[own_idx]),
            "wqT": wqT, "wkT": wkT, "wvT": wvT,
            "bq": bq_row, "bk": bk_row, "bv": bv_row,
            "woutT": woutT, "b_out": np.ascontiguousarray(b_out[None, :]),
            "wfcT": wfcT, "wprojT": wprojT,
            "ropeq_cos": q_cos, "ropeq_sin": q_sin,
            "ropek_cos": rk_cos, "ropek_sin": rk_sin,
            "masks": _build_masks(g),
            "ident": ident, "permT": permM,
        })

    res = run_bass_kernel_spmd(nc, in_maps, core_ids=list(range(8)))

    out = np.empty((4, T, D), dtype=np.float32)
    for c in range(8):
        b = c // 2
        out[b, own_idx_all[c], :] = np.asarray(res.results[c]["out"],
                                               dtype=np.float32)
    return out
